# version 10
# baseline (speedup 1.0000x reference)
import numpy as np
from contextlib import ExitStack

import concourse.bass as bass
import concourse.mybir as mybir
from concourse.bass_utils import run_bass_kernel_spmd

F32 = mybir.dt.float32
BF16 = mybir.dt.bfloat16
I32 = mybir.dt.int32
AX = mybir.AxisListType
OP = mybir.AluOpType
ACT = mybir.ActivationFunctionType

P = 128
TLOC = 1024
NTT = TLOC // P
D = 2048
NFT = D // P
I_FFN = 8192
LN2 = float(np.log(2.0))
INV_LN2 = float(1.0 / np.log(2.0))
ISQRT_DH = float(np.float32(1.0) / np.sqrt(np.float32(128.0)))
FLOOR_OFF = -0.49993896484375

N_SUB = 6


class B:

    def __init__(self, n_sub=N_SUB, dbg=()):
        self.n_sub = n_sub
        self.dbg = set(dbg)
        self.nc = bass.Bass("TRN2", target_bir_lowering=False, debug=False,
                            detect_race_conditions=False)
        self.cnt = {}

    def _declare_io(self):
        nc = self.nc
        d = {}

        def din(name, shape):
            d[name] = nc.dram_tensor(name, list(shape), F32, kind="ExternalInput").ap()

        din("x_in", (TLOC, D))
        for n, s in [("W_dkv", (D, 512)), ("W_dq", (D, 512)), ("W_uk", (512, 1024)),
                     ("W_uv", (512, 2048)), ("W_uq", (512, 1024)), ("W_kr", (D, 1024)),
                     ("W_qr", (512, 1024)), ("W_out", (D, D)),
                     ("wg_nat", (I_FFN // 2, D)), ("wu_nat", (I_FFN // 2, D)),
                     ("wd_nat", (D, I_FFN // 2)),
                     ("wg_T", (D, I_FFN // 2)), ("wu_T", (D, I_FFN // 2)),
                     ("wd_T", (I_FFN // 2, D)),
                     ("w1", (D, 160)), ("w2", (5, 32, D)), ("td_w1", (D, 64)),
                     ("td_w2", (64, D)), ("wr", (D, D)), ("wk", (D, D)), ("wv", (D, D)),
                     ("wg2", (D, D)), ("wo", (D, D)), ("u_bd", (D, 32)),
                     ("tm_x", (P, NFT)), ("tm_w", (P, NFT)), ("tm_k", (P, NFT)),
                     ("tm_v", (P, NFT)), ("tm_r", (P, NFT)), ("tm_g", (P, NFT)),
                     ("tdecay", (P, NFT)), ("lnw_rep", (P, D)), ("lnb_rep", (P, D)),
                     ("l1f_nat", (D, D)), ("l1f_T", (D, D)), ("l1f_bias", (1, D)),
                     ("ident", (P, P)), ("mask_su", (64, 64)), ("blk4", (4, P)),
                     ("ones1", (1, P)), ("rope_cos", (P, NTT * 32)), ("rope_sin", (P, NTT * 32)),
                     ("w2pad", (5, 160, D)), ("masksub", (P, 1024)),
                     ("mask_first", (P, 1)), ("mask_second", (P, 1))]:
            din(n, s)
        d["out"] = nc.dram_tensor("out", [TLOC, D], F32, kind="ExternalOutput").ap()
        def kind(n):
            return "ExternalOutput" if n in self.dbg else "Internal"
        d["qwg"] = nc.dram_tensor("qwg", [32, P, 4096], BF16, kind=kind("qwg")).ap()
        d["qwu"] = nc.dram_tensor("qwu", [32, P, 4096], BF16, kind=kind("qwu")).ap()
        d["qwd"] = nc.dram_tensor("qwd", [16, P, 8192], BF16, kind=kind("qwd")).ap()
        d["ql1"] = nc.dram_tensor("ql1", [NFT, P, D], BF16, kind=kind("ql1")).ap()
        d["qwgh"] = nc.dram_tensor("qwgh", [16, P, 4096], BF16, kind=kind("qwgh")).ap()
        d["qwuh"] = nc.dram_tensor("qwuh", [16, P, 4096], BF16, kind=kind("qwuh")).ap()
        d["qwdh"] = nc.dram_tensor("qwdh", [8, P, 8192], BF16, kind=kind("qwdh")).ap()
        d["thr_b"] = nc.dram_tensor("thr_b", [NFT, P], F32).ap()
        d["xspill"] = nc.dram_tensor("xspill", [TLOC, D], F32).ap()
        d["yspill"] = nc.dram_tensor("yspill", [TLOC, D], F32).ap()
        d["ccm_i"] = nc.dram_tensor("ccm_i", [P, 16 * P], F32).ap()
        d["ccm_o"] = nc.dram_tensor("ccm_o", [P, 16 * P], F32).ap()
        d["ccs_i"] = nc.dram_tensor("ccs_i", [P, 1024], F32).ap()
        d["ccs_o"] = nc.dram_tensor("ccs_o", [P, 1024], F32).ap()
        d["vsp"] = nc.dram_tensor("vsp", [NTT, P, D], BF16).ap()
        d["xgsp"] = nc.dram_tensor("xgsp", [NFT, P, TLOC], BF16).ap()
        d["ysp"] = nc.dram_tensor("ysp", [16, 64, D], BF16).ap()
        d["ynsp"] = nc.dram_tensor("ynsp", [16, 64, D], BF16).ap()
        d["cch_i"] = nc.dram_tensor("cch_i", [P, NFT], F32).ap()
        d["cch_o"] = nc.dram_tensor("cch_o", [P, NFT], F32).ap()
        self.d = d

    def _alloc(self, stack):
        nc = self.nc
        e = stack.enter_context
        self.xres = e(nc.sbuf_tensor("xres", [P, NTT * D], F32))
        self.PA = e(nc.sbuf_tensor("PA", [P, 16384], F32))
        self.PB = e(nc.sbuf_tensor("PB", [P, 8192], F32))
        self.PC = e(nc.sbuf_tensor("PC", [P, 8192], F32))
        self.WS = e(nc.sbuf_tensor("WS", [P, 2048], F32))
        self.SM = e(nc.sbuf_tensor("SM", [P, 1024], F32))
        self.CN = e(nc.sbuf_tensor("CN", [P, 512], F32))
        self.ps = [e(nc.psum_tensor("ps%d" % i, [P, 512], F32)) for i in range(4)]
        self.psA = e(nc.psum_tensor("psA", [P, 1024], F32))
        self.psS = e(nc.psum_tensor("psS", [P, 1024], F32))
        self.sems = {}
        for name in ["ldh", "ldg", "st", "sv", "ss", "stt", "sg", "cc", "ha", "sa", "ga"]:
            self.sems[name] = e(nc.semaphore(name))
            self.cnt[name] = 0
        self.block = e(nc.Block())

    def inc(self, instr, name, n=None):
        if n is None:
            n = 16 if name in ("ldh", "ldg", "st", "ha", "sa", "ga") else 1
        instr.then_inc(self.sems[name], n)
        self.cnt[name] += n
        return self.cnt[name]

    def w(self, eng, name, val=None):
        eng.wait_ge(self.sems[name], self.cnt[name] if val is None else val)

    def fence(self, eng, ek, instr):
        nm = self._SEM_OF[ek]
        ev = self.inc(instr, nm)
        eng.wait_ge(self.sems[nm], ev)
        return instr

    def barrier(self):
        block = self.block
        tgt = dict(self.cnt)
        deco = {"y": block.sync, "g": block.gpsimd, "v": block.vector,
                "s": block.scalar, "t": block.tensor}
        for ek in ["y", "g", "v", "s", "t"]:
            def body(eng, ek=ek):
                for nm, v in tgt.items():
                    if v:
                        eng.wait_ge(self.sems[nm], v)
            deco[ek](body)

    def build(self):
        self._declare_io()
        with ExitStack() as stack:
            self._alloc(stack)
            self.emit_prelude()
            if self.n_sub >= 1:
                self.emit_wquant_mx()
            if self.n_sub >= 4:
                self.emit_ternary()
            if self.n_sub >= 1:
                self.emit_l0ffn(1)
            if self.n_sub >= 2:
                self.emit_l0attn()
            if self.n_sub >= 3:
                self.emit_l0ffn(3)
            if self.n_sub >= 4:
                self.emit_l1ffn(4)
            if self.n_sub >= 5:
                self.emit_l1attn()
            if self.n_sub >= 6:
                self.emit_l1ffn(6)
            self.emit_store()
        return self.nc

    def emit_prelude(self):
        d, block = self.d, self.block
        CN = self.CN
        self.identf = CN[:, 0:128]
        self.identb = CN[:, 128:192].bitcast(BF16)
        self.masksu = CN[0:64, 192:256]
        self.blk4f = CN[0:4, 256:384]
        self.ones1t = CN[0:1, 384:448].bitcast(BF16)
        self.mfirst = CN[:, 448:449]
        self.msecond = CN[:, 449:450]
        self._ccol_next = [452]
        self._ccols = {}
        sems, cnt = self.sems, self.cnt

        @block.sync
        def _(sync):
            self.inc(sync.dma_start(
                out=self.xres[:, :].rearrange("p (tt dd) -> p tt dd", dd=D),
                in_=d["x_in"].rearrange("(tt p) dd -> p tt dd", p=P)), "ldh")
            self.inc(sync.dma_start(out=self.masksu, in_=d["mask_su"]), "ldh")
            self.inc(sync.dma_start(out=self.mfirst, in_=d["mask_first"]), "ldh")
            self.inc(sync.dma_start(out=self.msecond, in_=d["mask_second"]), "ldh")
            self.inc(sync.dma_start(out=self.identf, in_=d["ident"]), "ldh")
            self.inc(sync.dma_start(out=self.blk4f, in_=d["blk4"]), "ldh")

        @block.gpsimd
        def _(g):
            self.inc(g.dma_start(out=self.identb, in_=d["ident"]), "ldg")
            self.inc(g.dma_start(out=self.ones1t, in_=d["ones1"]), "ldg")

        consts = [1e-5, 6.4e-4, 0.5, 0.25, -1.0, 1.0,
                  LN2, -LN2, 3.0 * LN2, 5.0 * LN2, -2.0 * LN2,
                  INV_LN2, FLOOR_OFF, 1.0 / 2048.0, 0.7 / 2048.0, ISQRT_DH,
                  1.0 / 64.0, 2.0 ** -30]

        @block.vector
        def _(v):
            last = None
            for c in consts:
                col = CN[:, self._ccol_next[0]:self._ccol_next[0] + 1]
                self._ccols[float(np.float32(c))] = col
                self._ccol_next[0] += 1
                last = v.memset(col, float(np.float32(c)))
            self.inc(last, "sv")

        self.epsln = self.cc(1e-5)
        self.epsgn = self.cc(6.4e-4)
        self.base_ready = dict(self.cnt)
        self.barrier()

    def cc(self, val):
        return self._ccols[float(np.float32(val))]

    def emit_store(self):
        d, block = self.d, self.block

        @block.sync
        def _(sync):
            self.w(sync, "sv"); self.w(sync, "ss"); self.w(sync, "stt")
            self.w(sync, "ldh"); self.w(sync, "ldg"); self.w(sync, "sg")
            self.inc(sync.dma_start(
                out=d["out"].rearrange("(tt p) dd -> p tt dd", p=P),
                in_=self.xres[:, :].rearrange("p (tt dd) -> p tt dd", dd=D)), "st")
            sync.wait_ge(self.sems["st"], self.cnt["st"])

    _SEM_OF = {"v": "sv", "s": "ss", "t": "stt", "y": "ldh", "g": "ldg"}

    def seq(self, *steps):
        block = self.block
        deco = {"v": block.vector, "s": block.scalar, "t": block.tensor,
                "y": block.sync, "g": block.gpsimd}
        prev = getattr(self, "_last_seq", None)
        for ek, fn in steps:
            semname = self._SEM_OF[ek]
            pv = prev

            def body(eng, fn=fn, pv=pv, ek=ek, semname=semname):
                if pv is not None and pv[0] != ek:
                    eng.wait_ge(self.sems[self._SEM_OF[pv[0]]], pv[1])
                last = fn(eng)
                if last is not None:
                    self.inc(last, semname)
            deco[ek](body)
            prev = (ek, self.cnt[semname])
        self._last_seq = prev
        return prev

    def _scalepath_steps(self, sc, Ei, Arow, Srow, cA_ln):
        def sv(v):
            return v.tensor_single_scalar(out=sc, in_=sc, scalar=2.0 ** -30, op=OP.max)

        def s1(s):
            self.fence(s, "s", s.activation(out=sc, in_=sc, func=ACT.Ln))
            return s.activation(out=Ei, in_=sc, func=ACT.Identity,
                                bias=self.cc(FLOOR_OFF), scale=self.cc(INV_LN2))

        def v2(v):
            return v.tensor_copy(out=sc, in_=Ei)

        def s2(s):
            s.activation(out=Arow, in_=sc, func=ACT.Exp,
                         bias=self.cc(cA_ln), scale=self.cc(-LN2))
            return s.activation(out=Srow, in_=sc, func=ACT.Exp,
                                bias=self.cc(-2.0 * LN2), scale=self.cc(LN2))
        return [("v", sv), ("s", s1), ("v", v2), ("s", s2)]

    def _mxchain_steps(self, src, dst_bf, Abc, Sbc, sgb, au, r0, r1, r2, u1, u2, u3, d0):
        cap = 2.0 if d0 == 0.5 else 3.5

        def s_abs(s):
            s.activation(out=au, in_=src, func=ACT.Abs)
            return s.activation(out=sgb, in_=src, func=ACT.Sign)

        def v_zz(v):
            return v.tensor_tensor(out=au, in0=au, in1=Abc, op=OP.mult)

        def s_cast(s):
            s.copy(out=r0, in_=au)
            s.mul(out=r1, in_=au, mul=self.cc(0.5))
            return s.mul(out=r2, in_=au, mul=self.cc(0.25))

        def v_q(v):
            v.tensor_scalar(out=u1, in0=r0, scalar1=d0, scalar2=2.0, op0=OP.mult, op1=OP.min)
            v.tensor_scalar(out=u2, in0=r1, scalar1=2 * d0, scalar2=-2.0, op0=OP.mult, op1=OP.add)
            v.tensor_scalar(out=u2, in0=u2, scalar1=0.0, scalar2=2.0, op0=OP.max, op1=OP.min)
            v.tensor_scalar(out=u3, in0=r2, scalar1=4 * d0, scalar2=-4.0, op0=OP.mult, op1=OP.add)
            v.tensor_scalar(out=u3, in0=u3, scalar1=0.0, scalar2=cap, op0=OP.max, op1=OP.min)
            v.tensor_tensor(out=u1, in0=u1, in1=u2, op=OP.add)
            v.tensor_tensor(out=u1, in0=u1, in1=u3, op=OP.add)
            v.tensor_tensor(out=u1, in0=u1, in1=sgb, op=OP.mult)
            return v.tensor_tensor(out=dst_bf, in0=u1, in1=Sbc, op=OP.mult)
        return [("s", s_abs), ("v", v_zz), ("s", s_cast), ("v", v_q)]

    def emit_wquant_mx(self):
        d = self.d
        nc = self.nc
        k = "ExternalOutput" if "A_sc" in self.dbg else "Internal"
        d["A_sc"] = nc.dram_tensor("A_sc", [256, 8192], F32, kind=k).ap()
        d["S_sc"] = nc.dram_tensor("S_sc", [256, 8192], F32, kind=k).ap()
        for nat, wT, qout, gout, Om, Hm, om_loc, hm_loc in [
                ("wg_nat", "wg_T", "qwgh", "qwg", I_FFN, D, I_FFN // 2, D),
                ("wu_nat", "wu_T", "qwuh", "qwu", I_FFN, D, I_FFN // 2, D),
                ("wd_nat", "wd_T", "qwdh", "qwd", D, I_FFN, D, I_FFN // 2)]:
            self._wquant_one(d[nat], d[wT], d[qout], Om, Hm, om_loc, hm_loc)
            self.barrier()
            if getattr(self, "skip_gather", False):
                continue

            @self.block.gpsimd
            def _(g, src=qout, dst=gout):
                self.inc(g.collective_compute(
                    "AllGather", OP.bypass,
                    replica_groups=[[0, 1], [2, 3], [4, 5], [6, 7]],
                    ins=[d[src].opt()], outs=[d[dst].opt()]), "cc")
        self.barrier()

    def _wquant_one(self, nat, wT, qout, Om, Hm, om_loc, hm_loc):
        d, block = self.d, self.block
        PA, PB, PC, WS, SM = self.PA, self.PB, self.PC, self.WS, self.SM
        sems, cnt = self.sems, self.cnt
        nhb = hm_loc // 32
        assert nhb <= 128

        blk4b = WS[0:4, 512:576].bitcast(BF16)

        @block.gpsimd
        def _(g):
            self.inc(g.dma_start(out=blk4b, in_=d["blk4"]), "ldg")

        ld_ev, amax_ev, s2_ev, vev_ev, st_ev, ttr_ev = {}, {}, {}, {}, {}, {}
        for ot in range(om_loc // P):
            sl = ot % 2
            lsem = ["ldh", "ha"][sl]
            ssem = ["st", "sa"][sl]
            nt = PB[:, sl * 4096:sl * 4096 + hm_loc]
            sc = SM[:, sl * 128:sl * 128 + nhb]
            Ei = SM[:, 256 + sl * 128:256 + sl * 128 + nhb].bitcast(I32)[:, 0:nhb]
            Arow = SM[:, 512 + sl * 128:512 + sl * 128 + nhb]
            Srow = SM[:, 768 + sl * 128:768 + sl * 128 + nhb]

            @block.sync
            def _(sync, ot=ot, nt=nt):
                if ot >= 2:
                    sync.wait_ge(sems["sv"], amax_ev[ot - 2])
                self.inc(sync.dma_start(out=nt, in_=nat[ot * P:(ot + 1) * P, :]), lsem)
            ld_ev[ot] = cnt[lsem]

            self._last_seq = None

            def v_amax(v, ot=ot, nt=nt, sc=sc, lsem=lsem):
                v.wait_ge(sems[lsem], ld_ev[ot])
                if ot >= 2:
                    v.wait_ge(sems["ss"], s2_ev[ot - 2])
                    v.wait_ge(sems["stt"], ttr_ev[ot - 2])
                return v.tensor_reduce(out=sc, in_=nt.rearrange("p (b k) -> p b k", k=32),
                                       axis=AX.X, op=OP.max, apply_absolute_value=True)
            stp = [("v", v_amax)] + self._scalepath_steps(sc, Ei, Arow, Srow, 3.0 * LN2)
            self.seq(*stp)
            amax_ev[ot] = cnt["sv"]
            s2_ev[ot] = cnt["ss"]
            for qi, (row, hbm) in enumerate([(Arow, d["A_sc"]), (Srow, d["S_sc"])]):
                pst = [self.psA, self.psS][qi][:, sl * 512:sl * 512 + 128]
                tb = WS[0:nhb, (sl * 2 + qi) * 128:(sl * 2 + qi) * 128 + 128]

                def t_tr(t, row=row, pst=pst, ot=ot):
                    if ot >= 2:
                        t.wait_ge(sems["sv"], vev_ev[ot - 2])
                    return t.transpose(pst[0:nhb, 0:P], row, self.identf)

                def v_ev(v, pst=pst, tb=tb, ot=ot, ssem=ssem):
                    if ot >= 2:
                        v.wait_ge(sems[ssem], st_ev[ot - 2])
                    return v.tensor_copy(out=tb, in_=pst[0:nhb, 0:P])

                def y_st(sync, hbm=hbm, ot=ot, tb=tb, ssem=ssem):
                    self.inc(sync.dma_start(out=hbm[0:nhb, ot * P:(ot + 1) * P],
                                            in_=tb), ssem)
                    return None
                self.seq(("t", t_tr), ("v", v_ev), ("y", y_st))
            ttr_ev[ot] = cnt["stt"]
            vev_ev[ot] = cnt["sv"]
            st_ev[ot] = cnt[ssem]

        self.barrier()
        psA, psS, ps = self.psA, self.psS, self.ps
        n_osl = om_loc // 2048
        d0 = 0.5
        cap = 2.0
        wld_ev, sld_ev, bc_ev, abs_ev, zz_ev, cast_ev, vq_ev, stq_ev = {}, {}, {}, {}, {}, {}, {}, {}
        for it in range((hm_loc // P) * n_osl):
            sl = it % 2
            lsem = ["ldh", "ha"][sl]
            ssem = ["st", "sa"][sl]
            gsem = ["ldg", "ga"][sl]
            ht, osl = it // n_osl, it % n_osl
            wt = PB[:, sl * 4096:sl * 4096 + 2048]
            qb = PB[:, sl * 4096 + 2048:sl * 4096 + 3072].bitcast(BF16)
            au = PA[:, sl * 8192:sl * 8192 + 2048]
            r0 = PA[:, sl * 8192 + 2048:sl * 8192 + 4096].bitcast(I32)
            r1 = PA[:, sl * 8192 + 4096:sl * 8192 + 6144].bitcast(I32)
            r2 = PA[:, sl * 8192 + 6144:sl * 8192 + 8192].bitcast(I32)
            u2 = PA[:, sl * 8192 + 2048:sl * 8192 + 3072].bitcast(BF16)
            u3 = PA[:, sl * 8192 + 4096:sl * 8192 + 5120].bitcast(BF16)
            sgb = PC[:, sl * 2048:sl * 2048 + 1024].bitcast(BF16)
            u1 = PC[:, sl * 2048 + 1024:sl * 2048 + 2048].bitcast(BF16)
            A4 = PC[0:4, 4096 + sl * 1024:4096 + sl * 1024 + 1024].bitcast(BF16)
            S4 = PC[0:4, 6144 + sl * 1024:6144 + sl * 1024 + 1024].bitcast(BF16)

            @block.sync
            def _(sync, it=it, ht=ht, osl=osl, wt=wt, lsem=lsem):
                if it >= 2:
                    sync.wait_ge(sems["ss"], abs_ev[it - 2])
                self.inc(sync.dma_start(out=wt, in_=wT[ht * P:(ht + 1) * P,
                                                       osl * 2048:(osl + 1) * 2048]), lsem)
            wld_ev[it] = cnt[lsem]

            @block.gpsimd
            def _(g, it=it, ht=ht, osl=osl, A4=A4, S4=S4, gsem=gsem):
                if it >= 2:
                    g.wait_ge(sems["stt"], bc_ev[it - 2])
                self.inc(g.dma_start(
                    out=A4, in_=d["A_sc"][4 * ht:4 * ht + 4,
                                          osl * 2048:(osl + 1) * 2048]), gsem)
                self.inc(g.dma_start(
                    out=S4, in_=d["S_sc"][4 * ht:4 * ht + 4,
                                          osl * 2048:(osl + 1) * 2048]), gsem)
            sld_ev[it] = cnt[gsem]

            @block.tensor
            def _(t, it=it, A4=A4, S4=S4, gsem=gsem):
                t.wait_ge(sems[gsem], sld_ev[it])
                if it >= 1:
                    t.wait_ge(sems["sv"], vq_ev[it - 1])
                t.matmul(psA[:, 0:512], blk4b, A4[:, 0:512], start=True, stop=True)
                t.matmul(psA[:, 512:1024], blk4b, A4[:, 512:1024], start=True, stop=True)
                t.matmul(ps[0][:, 0:512], blk4b, A4[:, 1024:1536], start=True, stop=True)
                t.matmul(ps[1][:, 0:512], blk4b, A4[:, 1536:2048], start=True, stop=True)
                t.matmul(psS[:, 0:512], blk4b, S4[:, 0:512], start=True, stop=True)
                t.matmul(psS[:, 512:1024], blk4b, S4[:, 512:1024], start=True, stop=True)
                t.matmul(ps[2][:, 0:512], blk4b, S4[:, 1024:1536], start=True, stop=True)
                mm = t.matmul(ps[3][:, 0:512], blk4b, S4[:, 1536:2048], start=True, stop=True)
                self.inc(mm, "stt")
            bc_ev[it] = cnt["stt"]

            @block.scalar
            def _(s, it=it, wt=wt, au=au, sgb=sgb, lsem=lsem):
                s.wait_ge(sems[lsem], wld_ev[it])
                if it >= 2:
                    s.wait_ge(sems["sv"], vq_ev[it - 2])
                s.activation(out=au, in_=wt, func=ACT.Abs)
                last = s.activation(out=sgb, in_=wt, func=ACT.Sign)
                self.inc(last, "ss")
            abs_ev[it] = cnt["ss"]

            @block.vector
            def _(v, it=it, au=au):
                v.wait_ge(sems["stt"], bc_ev[it])
                v.wait_ge(sems["ss"], abs_ev[it])
                v.tensor_tensor(out=au[:, 0:1024], in0=au[:, 0:1024],
                                in1=psA[:, 0:1024], op=OP.mult)
                v.tensor_tensor(out=au[:, 1024:1536], in0=au[:, 1024:1536],
                                in1=ps[0][:, 0:512], op=OP.mult)
                last = v.tensor_tensor(out=au[:, 1536:2048], in0=au[:, 1536:2048],
                                       in1=ps[1][:, 0:512], op=OP.mult)
                self.inc(last, "sv")
            zz_ev[it] = cnt["sv"]

            @block.vector
            def _(v, it=it, qb=qb, au=au, r0=r0, r1=r1, r2=r2, u2=u2, u3=u3,
                  sgb=sgb, u1=u1, ssem=ssem):
                if it >= 2:
                    v.wait_ge(sems[ssem], stq_ev[it - 2])
                v.tensor_copy(out=r0, in_=au)
                v.tensor_scalar(out=r1, in0=au, scalar1=0.5, scalar2=None,
                                op0=OP.mult, op1=OP.bypass)
                v.tensor_scalar(out=r2, in0=au, scalar1=0.25, scalar2=None,
                                op0=OP.mult, op1=OP.bypass)
                v.tensor_scalar(out=u1, in0=r0, scalar1=d0, scalar2=2.0,
                                op0=OP.mult, op1=OP.min)
                v.tensor_scalar(out=u2, in0=r1, scalar1=2 * d0, scalar2=-2.0,
                                op0=OP.mult, op1=OP.add)
                v.tensor_scalar(out=u2, in0=u2, scalar1=0.0, scalar2=2.0,
                                op0=OP.max, op1=OP.min)
                v.tensor_scalar(out=u3, in0=r2, scalar1=4 * d0, scalar2=-4.0,
                                op0=OP.mult, op1=OP.add)
                v.tensor_scalar(out=u3, in0=u3, scalar1=0.0, scalar2=cap,
                                op0=OP.max, op1=OP.min)
                v.tensor_tensor(out=u1, in0=u1, in1=u2, op=OP.add)
                v.tensor_tensor(out=u1, in0=u1, in1=u3, op=OP.add)
                v.tensor_tensor(out=u1, in0=u1, in1=sgb, op=OP.mult)
                v.tensor_tensor(out=qb[:, 0:1024], in0=u1[:, 0:1024],
                                in1=psS[:, 0:1024], op=OP.mult)
                v.tensor_tensor(out=qb[:, 1024:1536], in0=u1[:, 1024:1536],
                                in1=ps[2][:, 0:512], op=OP.mult)
                last = v.tensor_tensor(out=qb[:, 1536:2048], in0=u1[:, 1536:2048],
                                       in1=ps[3][:, 0:512], op=OP.mult)
                self.inc(last, "sv")
            vq_ev[it] = cnt["sv"]

            @block.sync
            def _(sync, it=it, ht=ht, osl=osl, qb=qb, ssem=ssem):
                sync.wait_ge(sems["sv"], vq_ev[it])
                last = None
                if Om == I_FFN:
                    for j in range(8):
                        sid = osl * 8 + j
                        last = sync.dma_start(
                            out=qout[sid, :, ht * 256:(ht + 1) * 256],
                            in_=qb[:, j * 256:(j + 1) * 256])
                        self.inc(last, ssem)
                else:
                    for o2 in range(4):
                        sid = (ht // 16) * 4 + o2
                        last = sync.dma_start(
                            out=qout[sid, :, (ht % 16) * 512:(ht % 16) * 512 + 512],
                            in_=qb[:, o2 * 512:(o2 + 1) * 512])
                        self.inc(last, ssem)
            stq_ev[it] = cnt[ssem]
        self._last_seq = None

    def _ln_steps(self, src, dst, scrap, smb):
        s1, s2 = smb[:, 0:1], smb[:, 1:2]
        mu, ex2 = smb[:, 2:3], smb[:, 3:4]
        var, rstd, bneg = smb[:, 4:5], smb[:, 5:6], smb[:, 6:7]

        def v_sum(v):
            return v.tensor_reduce(out=s1, in_=src, axis=AX.X, op=OP.add)

        def s_sq(s):
            return s.activation(out=scrap, in_=src, func=ACT.Square, accum_out=s2)

        def v_stat(v):
            self.fence(v, "v", v.tensor_single_scalar(out=mu, in_=s1, scalar=1.0 / 2048.0, op=OP.mult))
            self.fence(v, "v", v.tensor_single_scalar(out=ex2, in_=s2, scalar=1.0 / 2048.0, op=OP.mult))
            self.fence(v, "v", v.tensor_tensor(out=var, in0=mu, in1=mu, op=OP.mult))
            return v.tensor_tensor(out=var, in0=ex2, in1=var, op=OP.subtract)

        def s_rstd(s):
            return s.activation(out=rstd, in_=var, func=ACT.Sqrt, bias=self.epsln)

        def v_bneg(v):
            self.fence(v, "v", v.reciprocal(out=rstd, in_=rstd))
            return v.scalar_tensor_tensor(out=bneg, in0=mu, scalar=-1.0, in1=rstd,
                                          op0=OP.mult, op1=OP.mult)

        def s_norm(s):
            return s.activation(out=dst, in_=src, func=ACT.Identity, bias=bneg, scale=rstd)
        return [("v", v_sum), ("s", s_sq), ("v", v_stat), ("s", s_rstd),
                ("v", v_bneg), ("s", s_norm)]

    def _actquant(self, src, dst_bf, T, smb, W=512, stt_ev=None):
        WTOT = src.shape[1]
        nb = WTOT // 32
        am = smb[:, 0:nb]
        Ei = smb[:, 64:64 + nb].bitcast(I32)[:, 0:nb]
        Arow = smb[:, 128:128 + nb]
        Srow = smb[:, 192:192 + nb]

        def v_amax(v):
            if stt_ev is None:
                self.w(v, "stt")
            else:
                v.wait_ge(self.sems["stt"], stt_ev)
            self.fence(v, "v", v.tensor_reduce(
                out=am, in_=src.rearrange("p (b k) -> p b k", k=32),
                axis=AX.X, op=OP.max, apply_absolute_value=True))
            return None
        stp = [("v", v_amax)] + self._scalepath_steps(am, Ei, Arow, Srow, 5.0 * LN2)
        self.seq(*stp)
        au = T[:, 0:512]
        r0 = T[:, 512:1024].bitcast(I32)
        r1 = T[:, 1024:1536].bitcast(I32)
        r2 = T[:, 1536:2048].bitcast(I32)
        sgb = T[:, 2048:2304].bitcast(BF16)
        u1 = T[:, 2304:2560].bitcast(BF16)
        u2 = T[:, 2560:2816].bitcast(BF16)
        u3 = T[:, 2816:3072].bitcast(BF16)
        for c0 in range(0, WTOT, W):
            blk = slice(c0 // 32, c0 // 32 + W // 32)
            Abc = Arow[:, blk].unsqueeze(2).broadcast_to([P, W // 32, 32])
            Sbc = Srow[:, blk].unsqueeze(2).broadcast_to([P, W // 32, 32])
            sh = src[:, c0:c0 + W].rearrange("p (b k) -> p b k", k=32)
            dh = dst_bf[:, c0:c0 + W].rearrange("p (b k) -> p b k", k=32)
            auv = au.rearrange("p (b k) -> p b k", k=32)
            sgv = sgb.rearrange("p (b k) -> p b k", k=32)
            u1v = u1.rearrange("p (b k) -> p b k", k=32)
            steps = self._mxchain3(sh, dh, Abc, Sbc, sgv, auv, au, r0, r1, r2,
                                   u1v, u2, u3, 0.125)
            self.seq(*steps)

    def _mxchain3(self, src, dst_bf, Abc, Sbc, sgb, au, au2, r0, r1, r2, u1, u2, u3, d0):
        cap = 2.0 if d0 == 0.5 else 3.5

        def s_abs(s):
            s.activation(out=au, in_=src, func=ACT.Abs)
            return s.activation(out=sgb, in_=src, func=ACT.Sign)

        def v_all(v):
            v.tensor_tensor(out=au, in0=au, in1=Abc, op=OP.mult)
            v.tensor_copy(out=r0, in_=au2)
            v.tensor_scalar(out=r1, in0=au2, scalar1=0.5, scalar2=None,
                            op0=OP.mult, op1=OP.bypass)
            v.tensor_scalar(out=r2, in0=au2, scalar1=0.25, scalar2=None,
                            op0=OP.mult, op1=OP.bypass)
            v.tensor_scalar(out=u1, in0=r0, scalar1=d0, scalar2=2.0, op0=OP.mult, op1=OP.min)
            v.tensor_scalar(out=u2, in0=r1, scalar1=2 * d0, scalar2=-2.0, op0=OP.mult, op1=OP.add)
            v.tensor_scalar(out=u2, in0=u2, scalar1=0.0, scalar2=2.0, op0=OP.max, op1=OP.min)
            v.tensor_scalar(out=u3, in0=r2, scalar1=4 * d0, scalar2=-4.0, op0=OP.mult, op1=OP.add)
            v.tensor_scalar(out=u3, in0=u3, scalar1=0.0, scalar2=cap, op0=OP.max, op1=OP.min)
            v.tensor_tensor(out=u1, in0=u1, in1=u2.rearrange("p (b k) -> p b k", k=32), op=OP.add)
            v.tensor_tensor(out=u1, in0=u1, in1=u3.rearrange("p (b k) -> p b k", k=32), op=OP.add)
            v.tensor_tensor(out=u1, in0=u1, in1=sgb, op=OP.mult)
            return v.tensor_tensor(out=dst_bf, in0=u1, in1=Sbc, op=OP.mult)
        return [("s", s_abs), ("v", v_all)]

    def emit_l0ffn(self, which):
        d, block = self.d, self.block
        PA, PB, PC, SM = self.PA, self.PB, self.PC, self.SM
        xres = self.xres
        qxT = PC.bitcast(BF16)
        qiT = PB.bitcast(BF16)
        xln = PA[:, 8192:10240]
        interb = PA[:, 10240:12288]
        T = PA[:, 12288:16384]
        qbuf = T[:, 3072:4096].bitcast(BF16)
        smb = SM[:, 0:16]

        for tt in range(NTT):
            xt = xres[:, tt * D:(tt + 1) * D]
            self.seq(*self._ln_steps(xt, xln, T[:, 0:2048], smb))
            self._actquant(xln, qbuf, T, SM[:, 64:320])
            for ft in range(NFT):
                pst = self.psA[:, (ft % 2) * 512:(ft % 2) * 512 + 64].bitcast(BF16)

                def t_tr(t, ft=ft, pst=pst):
                    self.w(t, "sv")
                    return t.transpose(pst, qbuf[:, ft * P:(ft + 1) * P], self.identb)

                def v_ev(v, ft=ft, tt=tt, pst=pst):
                    c = (ft * NTT + tt) * P
                    return v.tensor_copy(out=qxT[:, c:c + P], in_=pst)
                self.seq(("t", t_tr), ("v", v_ev))

        gsl = [PA[:, 0:2048].bitcast(BF16), PA[:, 2048:4096].bitcast(BF16)]
        usl = [PA[:, 4096:6144].bitcast(BF16), PA[:, 6144:8192].bitcast(BF16)]
        interbs = [PA[:, 10240:12288], PA[:, 8192:10240], self.WS[:, 0:2048]]
        sgb = SM[:, 512:768]
        slot_free = [0, 0]
        bank_free = [0, 0]
        tr_ev = {-2: 0, -1: 0}
        for Imac in range(4):
            imac_base = self.cnt["stt"]

            def emit_qt(osl2, Imac=Imac):
                ib = interbs[osl2 % 3]
                self._actquant(ib, qbuf, T, SM[:, 64:320], stt_ev=tr_ev[osl2 - 2])
                itl0 = osl2 * 2
                for j in range(2):
                    for tt in range(NTT):
                        pst = self.psS[:, ((j * NTT + tt) % 2) * 512:
                                       ((j * NTT + tt) % 2) * 512 + 64].bitcast(BF16)

                        def t_tr(t, j=j, tt=tt, pst=pst):
                            return t.transpose(pst,
                                               qbuf[:, (tt * 2 + j) * P:(tt * 2 + j) * P + P],
                                               self.identb)

                        def v_ev(v, j=j, tt=tt, pst=pst, itl0=itl0):
                            c = ((itl0 + j) * NTT + tt) * P
                            return v.tensor_copy(out=qiT[:, c:c + P], in_=pst)
                        self.seq(("t", t_tr), ("v", v_ev))
                tr_ev[osl2] = self.cnt["stt"]

            for osl in range(8):
                oc = Imac * 2048 + osl * 256
                slot = osl % 2
                ib = interbs[osl % 3]

                def y_ld(sync, Imac=Imac, osl=osl, slot=slot, sf=slot_free[slot], base=imac_base):
                    sync.wait_ge(self.sems["stt"], max(sf, base))
                    sid = Imac * 8 + osl
                    self.inc(sync.dma_start(out=gsl[slot], in_=d["qwg"][sid]), "ldh")
                    return sync.dma_start(out=usl[slot], in_=d["qwu"][sid])

                @block.sync
                def _(sync, f=y_ld):
                    self.inc(f(sync), "ldh")
                ld_ev = self.cnt["ldh"]

                for tt in range(NTT):
                    bnk = tt % 2
                    bg, bu = self.ps[bnk], self.ps[2 + bnk]

                    @block.tensor
                    def _(t, tt=tt, slot=slot, bg=bg, bu=bu, bf_ev=bank_free[bnk], ld_ev=ld_ev):
                        t.wait_ge(self.sems["ldh"], ld_ev)
                        if bf_ev:
                            t.wait_ge(self.sems["sv"], bf_ev)
                        mm = None
                        for ft in range(NFT):
                            lt = qxT[:, (ft * NTT + tt) * P:(ft * NTT + tt) * P + P]
                            t.matmul(bg[:, 0:256], lt, gsl[slot][:, ft * 256:(ft + 1) * 256],
                                     start=(ft == 0), stop=(ft == NFT - 1))
                            mm = t.matmul(bu[:, 0:256], lt, usl[slot][:, ft * 256:(ft + 1) * 256],
                                          start=(ft == 0), stop=(ft == NFT - 1))
                        self.inc(mm, "stt")
                    mm_ev = self.cnt["stt"]
                    if tt >= NTT - 2:
                        slot_free[slot] = mm_ev

                    @block.scalar
                    def _(s, bg=bg, mm_ev=mm_ev):
                        s.wait_ge(self.sems["stt"], mm_ev)
                        self.inc(s.activation(out=sgb, in_=bg[:, 0:256], func=ACT.Silu), "ss")
                    si_ev = self.cnt["ss"]

                    @block.vector
                    def _(v, tt=tt, bu=bu, si_ev=si_ev, ib=ib):
                        v.wait_ge(self.sems["ss"], si_ev)
                        self.inc(v.tensor_tensor(out=ib[:, tt * 256:(tt + 1) * 256],
                                                 in0=sgb, in1=bu[:, 0:256], op=OP.mult), "sv")
                    bank_free[bnk] = self.cnt["sv"]

                if osl >= 2:
                    emit_qt(osl - 2)
            emit_qt(6)
            emit_qt(7)

            dsl = [PA[:, 0:4096].bitcast(BF16), PA[:, 4096:8192].bitcast(BF16)]
            dfree = [0, 0]
            down_base = self.cnt["stt"]
            for o2 in range(4):
                slot = o2 % 2

                def y_ld(sync, Imac=Imac, o2=o2, slot=slot, sf=dfree[slot], base=down_base):
                    sync.wait_ge(self.sems["stt"], max(sf, base))
                    return sync.dma_start(out=dsl[slot], in_=d["qwd"][Imac * 4 + o2])

                @block.sync
                def _(sync, f=y_ld):
                    self.inc(f(sync), "ldh")
                ld_ev = self.cnt["ldh"]

                for tt in range(NTT):
                    pd = self.psA[:, (tt % 2) * 512:(tt % 2) * 512 + 512]


                    @block.tensor
                    def _(t, tt=tt, slot=slot, pd=pd):
                        t.wait_ge(self.sems["ldh"], ld_ev)
                        self.w(t, "sv")
                        mm = None
                        for it in range(16):
                            lt = qiT[:, (it * NTT + tt) * P:(it * NTT + tt) * P + P]
                            mm = t.matmul(pd, lt, dsl[slot][:, it * 512:(it + 1) * 512],
                                          start=(it == 0), stop=(it == 15))
                        self.inc(mm, "stt")
                    mm_ev = self.cnt["stt"]
                    if tt >= NTT - 2:
                        dfree[slot] = mm_ev

                    @block.vector
                    def _(v, tt=tt, o2=o2, pd=pd):
                        v.wait_ge(self.sems["stt"], mm_ev)
                        xsl = xres[:, tt * D + o2 * 512:tt * D + o2 * 512 + 512]
                        self.inc(v.scalar_tensor_tensor(
                            out=xsl, in0=pd, scalar=0.5, in1=xsl,
                            op0=OP.mult, op1=OP.add), "sv")
        self.barrier()

    def emit_l0attn(self):
        d, block = self.d, self.block
        PA, PB, PC, WS, SM = self.PA, self.PB, self.PC, self.WS, self.SM
        xres = self.xres
        xlnT = PC.bitcast(BF16)
        kT = PC.bitcast(BF16)
        xln = PA[:, 8192:10240]
        T = PA[:, 12288:16384]
        xlnb = T[:, 3072:4096].bitcast(BF16)
        asm = T[:, 0:1024].bitcast(BF16)
        rtmp = T[:, 1024:1536]
        smb = SM[:, 0:16]
        cosb = SM[:, 320:576]
        sinb = SM[:, 576:832]
        ckvT = PA[:, 0:2048].bitcast(BF16)
        cqT = PA[:, 2048:4096].bitcast(BF16)
        WREG = PA[:, 4096:8192]
        krnat = PA[:, 8192:12288].bitcast(BF16)
        vnat = PA[:, 8192:16384].bitcast(BF16)
        qnat = PB.bitcast(BF16)
        aoT = PB.bitcast(BF16)

        def y_rope(sync):
            self.inc(sync.dma_start(out=cosb, in_=d["rope_cos"]), "ldh")
            return sync.dma_start(out=sinb, in_=d["rope_sin"])
        self.seq(("y", y_rope))

        for tt in range(NTT):
            xt = xres[:, tt * D:(tt + 1) * D]
            steps = self._ln_steps(xt, xln, T[:, 0:2048], smb)

            def s_cast(s):
                return s.copy(out=xlnb, in_=xln)
            steps.append(("s", s_cast))
            self.seq(*steps)
            for ft in range(NFT):
                pst = self.psA[:, (ft % 2) * 512:(ft % 2) * 512 + 64].bitcast(BF16)

                def t_tr(t, ft=ft, pst=pst):
                    self.w(t, "sv")
                    return t.transpose(pst, xlnb[:, ft * P:(ft + 1) * P], self.identb)

                def v_ev(v, ft=ft, tt=tt, pst=pst):
                    c = (ft * NTT + tt) * P
                    return v.tensor_copy(out=xlnT[:, c:c + P], in_=pst)
                self.seq(("t", t_tr), ("v", v_ev))

        def load_w(name, K, N, wreg, cols=None):
            @block.gpsimd
            def _(g):
                self.w(g, "stt")
                for kt in range(K // P):
                    src = d[name][kt * P:(kt + 1) * P, :] if cols is None else                         d[name][kt * P:(kt + 1) * P, cols[0]:cols[1]]
                    self.inc(g.dma_start(out=wreg[:, kt * N:(kt + 1) * N], in_=src), "ldg")
            return self.cnt["ldg"]

        def proj_mm(t, pdst, W, nkt, lhs_fn, rhs_fn):
            mm = None
            for kt in range(nkt):
                for o2 in range(W // 512):
                    mm = t.matmul(pdst[:, o2 * 512:(o2 + 1) * 512], lhs_fn(kt),
                                  rhs_fn(kt, o2), start=(kt == 0), stop=(kt == nkt - 1))
            return mm

        for wname, dst in [("W_dkv", ckvT), ("W_dq", cqT)]:
            wreg = WREG.bitcast(BF16)
            ldg_ev = load_w(wname, D, 512, wreg)
            for mt in range(4):
                for t2 in range(2):
                    pm = self.ps[(mt * 2 + t2) % 2]

                    @block.tensor
                    def _(t, mt=mt, t2=t2, pm=pm, wreg=wreg, ldg_ev=ldg_ev):
                        t.wait_ge(self.sems["ldg"], ldg_ev)
                        self.w(t, "sv")
                        mm = None
                        for ft in range(NFT):
                            mm = t.matmul(pm[:, 0:512],
                                          wreg[:, ft * 512 + mt * P:ft * 512 + mt * P + P],
                                          xlnT[:, ft * 1024 + t2 * 512:ft * 1024 + t2 * 512 + 512],
                                          start=(ft == 0), stop=(ft == NFT - 1))
                        self.inc(mm, "stt")
                    mm_ev = self.cnt["stt"]

                    @block.vector
                    def _(v, dst=dst, mt=mt, t2=t2, pm=pm):
                        v.wait_ge(self.sems["stt"], mm_ev)
                        self.inc(v.tensor_copy(
                            out=dst[:, mt * 1024 + t2 * 512:mt * 1024 + t2 * 512 + 512],
                            in_=pm[:, 0:512]), "sv")

        def emit_rope(v, psrc, dstbf, tt):
            hv = psrc.rearrange("p (h x) -> p h x", x=64)
            c1 = cosb[:, tt * 32:tt * 32 + 16].unsqueeze(1).broadcast_to([P, 16, 16])
            s1 = sinb[:, tt * 32:tt * 32 + 16].unsqueeze(1).broadcast_to([P, 16, 16])
            t1 = rtmp[:, 0:256].rearrange("p (h x) -> p h x", x=16)
            t2 = rtmp[:, 256:512].rearrange("p (h x) -> p h x", x=16)
            dv = dstbf.rearrange("p (h x) -> p h x", x=128)
            x1, x2 = hv[:, :, 0:16], hv[:, :, 16:32]
            v.tensor_tensor(out=t1, in0=x1, in1=c1, op=OP.mult)
            v.tensor_tensor(out=t2, in0=x2, in1=s1, op=OP.mult)
            self.fence(v, "v", v.tensor_tensor(out=dv[:, :, 64:80], in0=t1, in1=t2, op=OP.subtract))
            v.tensor_tensor(out=t1, in0=x2, in1=c1, op=OP.mult)
            v.tensor_tensor(out=t2, in0=x1, in1=s1, op=OP.mult)
            self.fence(v, "v", v.tensor_tensor(out=dv[:, :, 80:96], in0=t1, in1=t2, op=OP.add))
            return v.tensor_copy(out=dv[:, :, 96:128], in_=hv[:, :, 32:64])

        wkr = WREG.bitcast(BF16)
        for hh in range(2):
            ldg_ev = load_w("W_kr", D, 512, wkr, cols=(hh * 512, hh * 512 + 512))
            for tt in range(NTT):
                pk = self.ps[tt % 2]

                @block.tensor
                def _(t, tt=tt, pk=pk, ldg_ev=ldg_ev):
                    t.wait_ge(self.sems["ldg"], ldg_ev)
                    self.w(t, "sv")
                    mm = None
                    for kt in range(NFT):
                        mm = t.matmul(pk[:, 0:512],
                                      xlnT[:, kt * 1024 + tt * P:kt * 1024 + tt * P + P],
                                      wkr[:, kt * 512:(kt + 1) * 512],
                                      start=(kt == 0), stop=(kt == NFT - 1))
                    self.inc(mm, "stt")
                mm_ev = self.cnt["stt"]

                @block.vector
                def _(v, tt=tt, pk=pk, hh=hh):
                    v.wait_ge(self.sems["stt"], mm_ev)
                    kr = krnat[:, tt * 1024 + hh * 512:tt * 1024 + hh * 512 + 512]
                    hv = pk[:, 0:512].rearrange("p (h x) -> p h x", x=64)
                    c1 = cosb[:, tt * 32:tt * 32 + 16].unsqueeze(1).broadcast_to([P, 8, 16])
                    s1 = sinb[:, tt * 32:tt * 32 + 16].unsqueeze(1).broadcast_to([P, 8, 16])
                    t1 = rtmp[:, 0:128].rearrange("p (h x) -> p h x", x=16)
                    t2 = rtmp[:, 128:256].rearrange("p (h x) -> p h x", x=16)
                    dv = kr.rearrange("p (h x) -> p h x", x=64)
                    x1, x2 = hv[:, :, 0:16], hv[:, :, 16:32]
                    v.tensor_tensor(out=t1, in0=x1, in1=c1, op=OP.mult)
                    self.fence(v, "v", v.tensor_tensor(out=t2, in0=x2, in1=s1, op=OP.mult))
                    self.fence(v, "v", v.tensor_tensor(out=dv[:, :, 0:16], in0=t1, in1=t2, op=OP.subtract))
                    v.tensor_tensor(out=t1, in0=x2, in1=c1, op=OP.mult)
                    self.fence(v, "v", v.tensor_tensor(out=t2, in0=x1, in1=s1, op=OP.mult))
                    v.tensor_tensor(out=dv[:, :, 16:32], in0=t1, in1=t2, op=OP.add)
                    self.inc(v.tensor_copy(out=dv[:, :, 32:64], in_=hv[:, :, 32:64]), "sv")

        wuk = WREG.bitcast(BF16)
        ldg_ev = load_w("W_uk", 512, 1024, wuk)
        for tt in range(NTT):
            pk = [self.psA, self.psS][tt % 2]

            @block.tensor
            def _(t, tt=tt, pk=pk, ldg_ev=ldg_ev):
                t.wait_ge(self.sems["ldg"], ldg_ev)
                self.w(t, "sv")
                self.inc(proj_mm(t, pk, 1024, 4,
                                 lambda kt: ckvT[:, kt * 1024 + tt * P:kt * 1024 + tt * P + P],
                                 lambda kt, o2: wuk[:, kt * 1024 + o2 * 512:kt * 1024 + o2 * 512 + 512]), "stt")
            mm_ev = self.cnt["stt"]

            @block.vector
            def _(v, tt=tt, pk=pk):
                v.wait_ge(self.sems["stt"], mm_ev)
                self.w(v, "stt")
                av = asm.rearrange("p (h x) -> p h x", x=128)
                v.tensor_copy(out=av[:, :, 0:64],
                              in_=pk[:, 0:1024].rearrange("p (h x) -> p h x", x=64))
                self.inc(v.tensor_copy(
                    out=av[:, :, 64:128],
                    in_=krnat[:, tt * 1024:(tt + 1) * 1024].rearrange("p (h x) -> p h x", x=64)), "sv")
            for ft in range(NFT):
                pst = self.ps[2 + ft % 2][:, 0:64].bitcast(BF16)

                def t_tr(t, ft=ft, pst=pst):
                    self.w(t, "sv")
                    return t.transpose(pst, asm[:, ft * P:(ft + 1) * P], self.identb)

                def v_ev(v, ft=ft, tt=tt, pst=pst):
                    c = (ft * NTT + tt) * P
                    return v.tensor_copy(out=kT[:, c:c + P], in_=pst)
                self.seq(("t", t_tr), ("v", v_ev))

        wuq = WREG.bitcast(BF16)
        ldg_ev = load_w("W_uq", 512, 1024, wuq)
        wqr = PA[:, 6144:8192].bitcast(BF16)
        ldg_ev2 = load_w("W_qr", 512, 1024, wqr)
        for tt in range(NTT):
            pq = [self.psA, self.psS][tt % 2]
            pr = self.ps[tt % 2]

            @block.tensor
            def _(t, tt=tt, pq=pq, ldg_ev=ldg_ev2):
                t.wait_ge(self.sems["ldg"], ldg_ev)
                self.w(t, "sv")
                proj_mm(t, pq, 1024, 4,
                        lambda kt: cqT[:, kt * 1024 + tt * P:kt * 1024 + tt * P + P],
                        lambda kt, o2: wuq[:, kt * 1024 + o2 * 512:kt * 1024 + o2 * 512 + 512])
                mm = None
                for kt in range(4):
                    for o2 in range(2):
                        mm = t.matmul(self.ps[o2][:, 0:512],
                                      cqT[:, kt * 1024 + tt * P:kt * 1024 + tt * P + P],
                                      wqr[:, kt * 1024 + o2 * 512:kt * 1024 + o2 * 512 + 512],
                                      start=(kt == 0), stop=(kt == 3))
                self.inc(mm, "stt")
            mm_ev = self.cnt["stt"]

            @block.vector
            def _(v, tt=tt, pq=pq):
                v.wait_ge(self.sems["stt"], mm_ev)
                qn = qnat[:, tt * 2048:(tt + 1) * 2048]
                qv = qn.rearrange("p (h x) -> p h x", x=128)
                v.tensor_copy(out=qv[:, :, 0:64],
                              in_=pq[:, 0:1024].rearrange("p (h x) -> p h x", x=64))
                for half in range(2):
                    hv = self.ps[half][:, 0:512].rearrange("p (h x) -> p h x", x=64)
                    c1 = cosb[:, tt * 32:tt * 32 + 16].unsqueeze(1).broadcast_to([P, 8, 16])
                    s1 = sinb[:, tt * 32:tt * 32 + 16].unsqueeze(1).broadcast_to([P, 8, 16])
                    t1 = rtmp[:, 0:128].rearrange("p (h x) -> p h x", x=16)
                    t2 = rtmp[:, 128:256].rearrange("p (h x) -> p h x", x=16)
                    dv = qv[:, half * 8:half * 8 + 8, :]
                    x1, x2 = hv[:, :, 0:16], hv[:, :, 16:32]
                    v.tensor_tensor(out=t1, in0=x1, in1=c1, op=OP.mult)
                    self.fence(v, "v", v.tensor_tensor(out=t2, in0=x2, in1=s1, op=OP.mult))
                    self.fence(v, "v", v.tensor_tensor(out=dv[:, :, 64:80], in0=t1, in1=t2, op=OP.subtract))
                    v.tensor_tensor(out=t1, in0=x2, in1=c1, op=OP.mult)
                    self.fence(v, "v", v.tensor_tensor(out=t2, in0=x1, in1=s1, op=OP.mult))
                    v.tensor_tensor(out=dv[:, :, 80:96], in0=t1, in1=t2, op=OP.add)
                    v.tensor_copy(out=dv[:, :, 96:128], in_=hv[:, :, 32:64])
                self.inc(v.engine_nop() if False else v.tensor_copy(
                    out=rtmp[:, 0:1], in_=rtmp[:, 0:1]), "sv")

        wuv = WREG.bitcast(BF16)
        ldg_ev = load_w("W_uv", 512, 2048, wuv)
        for tt in range(NTT):
            @block.tensor
            def _(t, tt=tt, ldg_ev=ldg_ev):
                t.wait_ge(self.sems["ldg"], ldg_ev)
                self.w(t, "sv")
                for half, pv in [(0, self.psA), (1, self.psS)]:
                    self.inc(proj_mm(t, pv, 1024, 4,
                             lambda kt: ckvT[:, kt * 1024 + tt * P:kt * 1024 + tt * P + P],
                             lambda kt, o2, half=half: wuv[:, kt * 2048 + half * 1024 + o2 * 512:
                                                           kt * 2048 + half * 1024 + o2 * 512 + 512]), "stt")
            mm_ev = self.cnt["stt"]

            @block.vector
            def _(v, tt=tt):
                v.wait_ge(self.sems["stt"], mm_ev)
                v.tensor_copy(out=vnat[:, tt * 2048:tt * 2048 + 1024], in_=self.psA[:, 0:1024])
                self.inc(v.tensor_copy(out=vnat[:, tt * 2048 + 1024:tt * 2048 + 2048],
                                       in_=self.psS[:, 0:1024]), "sv")

        Msb = WS[:, 0:2048]
        @block.tensor
        def _(t):
            self.w(t, "sv")
            for h in range(16):
                pm = [self.psA, self.psS][h // 8][:, (h % 8) * P:(h % 8) * P + P]
                mm = None
                for tt in range(NTT):
                    mm = t.matmul(pm, qnat[:, tt * 2048 + h * P:tt * 2048 + h * P + P],
                                  vnat[:, tt * 2048 + h * P:tt * 2048 + h * P + P],
                                  start=(tt == 0), stop=(tt == NTT - 1))
                self.inc(mm, "stt")
        mm_ev = self.cnt["stt"]

        @block.vector
        def _(v):
            v.wait_ge(self.sems["stt"], mm_ev)
            v.tensor_scalar(out=Msb[:, 0:1024], in0=self.psA[:, 0:1024],
                            scalar1=ISQRT_DH, scalar2=None, op0=OP.mult)
            self.inc(v.tensor_scalar(out=Msb[:, 1024:2048], in0=self.psS[:, 0:1024],
                                     scalar1=ISQRT_DH, scalar2=None, op0=OP.mult), "sv")
        sv_ev = self.cnt["sv"]

        Mb = WS[:, 0:1024].bitcast(BF16)
        @block.gpsimd
        def _(g):
            g.wait_ge(self.sems["sv"], sv_ev)
            self.inc(g.dma_start(out=d["ccm_i"], in_=Msb), "ldg")
            self.w(g, "ldg")
            self.inc(g.collective_compute(
                "AllReduce", OP.add,
                replica_groups=[[0, 1], [2, 3], [4, 5], [6, 7]],
                ins=[d["ccm_i"].opt()], outs=[d["ccm_o"].opt()]), "cc")
            self.w(g, "cc")
            self.inc(g.dma_start(out=Mb, in_=d["ccm_o"]), "ldg")
        cc_ev = self.cnt["ldg"]

        for h in range(16):
            for t2 in range(2):
                po = self.ps[(h * 2 + t2) % 2]

                @block.tensor
                def _(t, h=h, t2=t2, po=po):
                    t.wait_ge(self.sems["ldg"], cc_ev)
                    self.w(t, "sv")
                    self.inc(t.matmul(po[:, 0:512], Mb[:, h * P:(h + 1) * P],
                                      kT[:, h * 1024 + t2 * 512:h * 1024 + t2 * 512 + 512],
                                      start=True, stop=True), "stt")
                mm_ev = self.cnt["stt"]

                @block.vector
                def _(v, h=h, t2=t2, po=po):
                    v.wait_ge(self.sems["stt"], mm_ev)
                    self.inc(v.tensor_copy(
                        out=aoT[:, h * 1024 + t2 * 512:h * 1024 + t2 * 512 + 512],
                        in_=po[:, 0:512]), "sv")

        wo_slot = [WS[:, 0:256].bitcast(BF16), WS[:, 256:512].bitcast(BF16)]
        banks = [self.psA[:, 0:512], self.psA[:, 512:1024],
                 self.psS[:, 0:512], self.psS[:, 512:1024],
                 self.ps[0][:, 0:512], self.ps[1][:, 0:512],
                 self.ps[2][:, 0:512], self.ps[3][:, 0:512]]
        for o2 in range(4):
            base_ev = self.cnt["sv"]
            for ft in range(NFT):
                slot = ft % 2

                @block.gpsimd
                def _(g, o2=o2, ft=ft, slot=slot):
                    self.w(g, "stt")
                    self.inc(g.dma_start(
                        out=wo_slot[slot],
                        in_=d["W_out"][ft * P:(ft + 1) * P, o2 * 512:(o2 + 1) * 512]), "ldg")
                ld_ev = self.cnt["ldg"]

                @block.tensor
                def _(t, o2=o2, ft=ft, slot=slot, base_ev=base_ev):
                    t.wait_ge(self.sems["ldg"], ld_ev)
                    if ft == 0:
                        t.wait_ge(self.sems["sv"], base_ev)
                    mm = None
                    for tt in range(NTT):
                        mm = t.matmul(banks[tt], aoT[:, ft * 1024 + tt * P:ft * 1024 + tt * P + P],
                                      wo_slot[slot], start=(ft == 0), stop=(ft == NFT - 1))
                    self.inc(mm, "stt")
            mm_ev = self.cnt["stt"]

            @block.vector
            def _(v, o2=o2):
                v.wait_ge(self.sems["stt"], mm_ev)
                last = None
                for tt in range(NTT):
                    xsl = xres[:, tt * D + o2 * 512:tt * D + o2 * 512 + 512]
                    last = v.tensor_tensor(out=xsl, in0=banks[tt], in1=xsl, op=OP.add)
                self.inc(last, "sv")
        self.barrier()

    def emit_ternary(self):
        d, block = self.d, self.block
        PA, PB, SM = self.PA, self.PB, self.SM
        thrb = SM[:, 832:848]
        negb = SM[:, 848:864]
        qfull = PA.bitcast(BF16)

        ld_ev, thr_ev, tr_ev = {}, {}, {-2: 0, -1: 0}
        PC = self.PC
        for ot in range(NFT):
            pool = [PB, PC][ot % 2]
            lsem = ["ldh", "ha"][ot % 2]
            nt = pool[:, 0:2048]
            ab = pool[:, 2048:4096]
            qn = pool[:, 2048:3072].bitcast(BF16)
            c1 = pool[:, 4096:6144]
            c2 = pool[:, 6144:8192]

            @block.sync
            def _(sync, ot=ot, nt=nt, lsem=lsem):
                if ot >= 2:
                    sync.wait_ge(self.sems["sv"], thr_ev[ot - 2])
                self.inc(sync.dma_start(out=nt, in_=d["l1f_nat"][ot * P:(ot + 1) * P, :]),
                         lsem)
            ld_ev[ot] = self.cnt[lsem]

            self._last_seq = None

            def s_abs(s, ot=ot, nt=nt, ab=ab, lsem=lsem):
                s.wait_ge(self.sems[lsem], ld_ev[ot])
                if ot >= 2:
                    s.wait_ge(self.sems["stt"], tr_ev[ot - 2])
                    s.wait_ge(self.sems["sv"], thr_ev[ot - 2])
                return s.activation(out=ab, in_=nt, func=ACT.Abs)

            def v_thr(v, ot=ot, ab=ab, nt=nt, c1=c1, c2=c2, qn=qn):
                self.fence(v, "v", v.tensor_reduce(out=thrb[:, ot:ot + 1], in_=ab,
                                                   axis=AX.X, op=OP.add))
                self.fence(v, "v", v.tensor_single_scalar(
                    out=thrb[:, ot:ot + 1], in_=thrb[:, ot:ot + 1],
                    scalar=0.7 / 2048.0, op=OP.mult))
                self.fence(v, "v", v.tensor_single_scalar(
                    out=negb[:, ot:ot + 1], in_=thrb[:, ot:ot + 1],
                    scalar=-1.0, op=OP.mult))
                v.tensor_single_scalar(out=c1, in_=nt, scalar=thrb[:, ot:ot + 1], op=OP.is_gt)
                self.fence(v, "v", v.tensor_single_scalar(
                    out=c2, in_=nt, scalar=negb[:, ot:ot + 1], op=OP.is_lt))
                return v.tensor_tensor(out=qn, in0=c1, in1=c2, op=OP.subtract)
            self.seq(("s", s_abs), ("v", v_thr))
            thr_ev[ot] = self.cnt["sv"]
            for it in range(NFT):
                pst = self.psA[:, (it % 2) * 512:(it % 2) * 512 + 64].bitcast(BF16)

                def t_tr(t, it=it, pst=pst, qn=qn):
                    self.w(t, "sv")
                    return t.transpose(pst, qn[:, it * P:(it + 1) * P], self.identb)

                def v_ev(v, it=it, ot=ot, pst=pst):
                    return v.tensor_copy(out=qfull[:, it * 2048 + ot * P:it * 2048 + ot * P + P],
                                         in_=pst)
                self.seq(("t", t_tr), ("v", v_ev))
            tr_ev[ot] = self.cnt["stt"]

        def y_store(sync):
            self.w(sync, "sv")
            last = None
            for it in range(NFT):
                last = sync.dma_start(out=d["ql1"][it], in_=qfull[:, it * 2048:(it + 1) * 2048])
                if it < NFT - 1:
                    self.inc(last, "st")
            return last
        self.seq(("y", y_store))
        self.barrier()

    def emit_l1ffn(self, which):
        d, block = self.d, self.block
        PA, PB, PC, SM = self.PA, self.PB, self.PC, self.SM
        xres = self.xres
        xlnT = PC.bitcast(BF16)
        xln = PA[:, 8192:10240]
        T = PA[:, 12288:16384]
        xlnb = T[:, 3072:4096].bitcast(BF16)
        smb = SM[:, 0:16]
        biasrow = PA[:, 10240:11264].bitcast(BF16)

        def g_bias(g):
            return g.dma_start(out=biasrow[0:1, :], in_=d["l1f_bias"])
        self.seq(("g", g_bias))

        for tt in range(NTT):
            xt = xres[:, tt * D:(tt + 1) * D]
            steps = self._ln_steps(xt, xln, T[:, 0:2048], smb)

            def s_cast(s):
                return s.copy(out=xlnb, in_=xln)
            steps.append(("s", s_cast))
            self.seq(*steps)
            for ft in range(NFT):
                pst = self.psA[:, (ft % 2) * 512:(ft % 2) * 512 + 64].bitcast(BF16)

                def t_tr(t, ft=ft, pst=pst):
                    self.w(t, "sv")
                    return t.transpose(pst, xlnb[:, ft * P:(ft + 1) * P], self.identb)

                def v_ev(v, ft=ft, tt=tt, pst=pst):
                    c = (ft * NTT + tt) * P
                    return v.tensor_copy(out=xlnT[:, c:c + P], in_=pst)
                self.seq(("t", t_tr), ("v", v_ev))

        qsl = [PB.bitcast(BF16), PA[:, 0:8192].bitcast(BF16)]

        def y_lq(sync):
            last = None
            for it in range(NFT):
                dstb = qsl[it // 8]
                last = sync.dma_start(out=dstb[:, (it % 8) * 2048:(it % 8) * 2048 + 2048],
                                      in_=d["ql1"][it])
                if it < NFT - 1:
                    self.inc(last, "ldh")
            return last
        self.seq(("y", y_lq))
        lq_ev = self.cnt["ldh"]

        for tt in range(NTT):
            for o2 in range(4):
                po = self.ps[(tt * 4 + o2) % 2]

                @block.tensor
                def _(t, tt=tt, o2=o2, po=po):
                    t.wait_ge(self.sems["ldh"], lq_ev)
                    self.w(t, "ldg")
                    self.w(t, "sv")
                    for it in range(NFT):
                        dstb = qsl[it // 8]
                        t.matmul(po[:, 0:512],
                                 xlnT[:, it * 1024 + tt * P:it * 1024 + tt * P + P],
                                 dstb[:, (it % 8) * 2048 + o2 * 512:(it % 8) * 2048 + o2 * 512 + 512],
                                 start=(it == 0), stop=False)
                    self.inc(t.matmul(po[:, 0:512], self.ones1t,
                                      biasrow[0:1, o2 * 512:o2 * 512 + 512],
                                      start=False, stop=True), "stt")
                mm_ev = self.cnt["stt"]

                @block.vector
                def _(v, tt=tt, o2=o2, po=po):
                    v.wait_ge(self.sems["stt"], mm_ev)
                    xsl = xres[:, tt * D + o2 * 512:tt * D + o2 * 512 + 512]
                    self.inc(v.scalar_tensor_tensor(out=xsl, in0=po[:, 0:512], scalar=0.5,
                                                    in1=xsl, op0=OP.mult, op1=OP.add), "sv")
        self.barrier()

    def emit_l1attn(self, stop=99):
        d, block = self.d, self.block
        PA, PB, PC, WS, SM = self.PA, self.PB, self.PC, self.WS, self.SM
        xres = self.xres
        self.m0f = PB.bitcast(BF16)
        self.m1f = PA[:, 0:8192].bitcast(BF16)
        self.kTf = PC.bitcast(BF16)
        self.rTf = xres[:, 8192:16384].bitcast(BF16)
        self.vnf = xres[:, 0:8192].bitcast(BF16)
        self.mixedin = PA[:, 12288:16384].bitcast(BF16)
        self.smLN = SM[:, 0:16]
        self.smGN = SM[:, 16:32]
        self.tmv = SM[:, 32:144]
        self.hcol = SM[:, 144:160]
        self.hprev = SM[:, 160:176]
        self.cLb = SM[:, 176:432]
        self.ALb = SM[:, 432:688]
        self.Lam = SM[:, 688:944]
        self.zbuf = SM[:, 960:1024]

        def v_z(v):
            return v.memset(self.zbuf, 0.0)
        self.seq(("v", v_z))

        def y_sp(sync):
            self.inc(sync.dma_start(
                out=d["xspill"].rearrange("(tt p) dd -> p tt dd", p=P),
                in_=xres[:, :].rearrange("p (tt dd) -> p tt dd", dd=D)), "st")
            last = None
            for i, nm in enumerate(["tm_x", "tm_w", "tm_k", "tm_v", "tm_r", "tm_g", "tdecay"]):
                last = sync.dma_start(out=self.tmv[:, i * 16:(i + 1) * 16], in_=d[nm])
                if i < 6:
                    self.inc(last, "ldh")
            return last
        self.seq(("y", y_sp))

        self.ubdb = WS[:, 0:256].bitcast(BF16)

        if stop >= 2:
            self._l1a_halo()
        for th in range(2):
            if stop < 3 + th * 3:
                break
            self._l1a_lnhalf(th)
            if stop < 4 + th * 3:
                break
            self._l1a_mpath(th)
            if stop < 5 + th * 3:
                break
            for pname in ["w", "g", "k", "v", "r"]:
                self._l1a_mix(pname, th)
                self._l1a_sweep(pname, th)
        if stop >= 9:
            self._l1a_pass1()
        if stop >= 10:
            self._l1a_pass2()
        if stop >= 11:
            self._l1a_end()
        self.barrier()

    def _l1a_halo(self):
        d, block = self.d, self.block
        PA, SM = self.PA, self.SM
        hrow = PA[0:1, 11264:13312]
        smb = self.smGN
        xrow = PA[0:1, 8192:10240]

        def y_row(sync):
            return sync.dma_start(out=xrow, in_=self.xres[127:128, 7 * D:8 * D])
        self.seq(("y", y_row))

        def v_s(v):
            self.fence(v, "v", v.tensor_reduce(out=smb[0:1, 0:1], in_=xrow, axis=AX.X, op=OP.add))
            return None

        def s_sq(s):
            return s.activation(out=hrow, in_=xrow, func=ACT.Square, accum_out=smb[0:1, 1:2])

        def v_st(v):
            self.fence(v, "v", v.tensor_single_scalar(out=smb[0:1, 2:3], in_=smb[0:1, 0:1],
                                                      scalar=1.0 / 2048.0, op=OP.mult))
            self.fence(v, "v", v.tensor_single_scalar(out=smb[0:1, 3:4], in_=smb[0:1, 1:2],
                                                      scalar=1.0 / 2048.0, op=OP.mult))
            self.fence(v, "v", v.tensor_tensor(out=smb[0:1, 4:5], in0=smb[0:1, 2:3],
                                               in1=smb[0:1, 2:3], op=OP.mult))
            return v.tensor_tensor(out=smb[0:1, 4:5], in0=smb[0:1, 3:4],
                                   in1=smb[0:1, 4:5], op=OP.subtract)

        def s_sd(s):
            return s.activation(out=smb[0:1, 5:6], in_=smb[0:1, 4:5], func=ACT.Sqrt,
                                bias=self.epsln[0:1, :])

        def v_r(v):
            self.fence(v, "v", v.reciprocal(out=smb[0:1, 5:6], in_=smb[0:1, 5:6]))
            return v.scalar_tensor_tensor(out=smb[0:1, 6:7], in0=smb[0:1, 2:3],
                                          scalar=-1.0, in1=smb[0:1, 5:6],
                                          op0=OP.mult, op1=OP.mult)

        def s_n(s):
            return s.activation(out=hrow, in_=xrow, func=ACT.Identity,
                                bias=smb[0:1, 6:7], scale=smb[0:1, 5:6])
        self.seq(("v", v_s), ("s", s_sq), ("v", v_st), ("s", s_sd), ("v", v_r), ("s", s_n))
        for ft in range(NFT):
            pst = self.ps[ft % 2]

            def t_tr(t, ft=ft, pst=pst):
                self.w(t, "sv")
                return t.transpose(pst[:, 0:1], hrow[0:1, ft * P:(ft + 1) * P],
                                   self.identf[0:1, 0:1])

            def v_ev(v, ft=ft, pst=pst):
                return v.tensor_copy(out=self.hcol[:, ft:ft + 1], in_=pst[:, 0:1])
            self.seq(("t", t_tr), ("v", v_ev))

        def v_m(v):
            return v.tensor_single_scalar(out=self.hcol, in_=self.hcol,
                                          scalar=self.mfirst, op=OP.mult)
        self.seq(("v", v_m))

        @block.gpsimd
        def _(g):
            self.w(g, "sv")
            self.inc(g.dma_start(out=d["cch_i"], in_=self.hcol), "ldg")
            self.w(g, "ldg")
            self.inc(g.collective_compute(
                "AllReduce", OP.add, replica_groups=[[0, 1], [2, 3], [4, 5], [6, 7]],
                ins=[d["cch_i"].opt()], outs=[d["cch_o"].opt()]), "cc")
            self.w(g, "cc")
            self.inc(g.dma_start(out=self.hcol, in_=d["cch_o"]), "ldg")
        hl_ev = self.cnt["ldg"]

        def v_m2(v):
            v.wait_ge(self.sems["ldg"], hl_ev)
            return v.tensor_single_scalar(out=self.hcol, in_=self.hcol,
                                          scalar=self.msecond, op=OP.mult)
        self.seq(("v", v_m2))

    def _l1a_lnhalf(self, th):
        d = self.d
        PA = self.PA
        xbuf = PA[:, 13312:15360]
        xln = PA[:, 11264:13312]
        smb = self.smLN
        if th == 1:
            def v_hp(v):
                self.w(v, "stt")
                return v.tensor_copy(
                    out=self.hprev,
                    in_=self.xres[:, 0:8192].rearrange("p (ft s) -> p ft s", s=512)[:, :, 511])
            self.seq(("v", v_hp))
        for ttl in range(4):
            tt = th * 4 + ttl

            def y_ld(sync, tt=tt):
                self.w(sync, "ss")
                return sync.dma_start(
                    out=xbuf, in_=d["xspill"][tt * P:(tt + 1) * P, :])
            self.seq(("y", y_ld))
            self.seq(*self._ln_steps(xbuf, xln, xln, smb))
            for ft in range(NFT):
                pst = self.psA[:, (ft % 2) * 512:(ft % 2) * 512 + 128]

                def t_tr(t, ft=ft, pst=pst, xln=xln):
                    self.w(t, "sv")
                    return t.transpose(pst, xln[:, ft * P:(ft + 1) * P], self.identf)

                def v_ev(v, ft=ft, ttl=ttl, pst=pst):
                    c = (ft * 4 + ttl) * P
                    return v.tensor_copy(out=self.xres[:, c:c + P], in_=pst)
                self.seq(("t", t_tr), ("v", v_ev))

    def _l1a_mpath(self, th):
        d, block = self.d, self.block
        PA, WS = self.PA, self.WS
        mixedin = self.mixedin
        mTa = PA[:, 10240:10496].bitcast(BF16)
        mTb = PA[0:32, 10752:11008].bitcast(BF16)
        prev = self.hcol if th == 0 else self.hprev

        def v_xxx(v, th=th, prev=prev):
            self.w(v, "stt")
            xxp = PA[:, 8704:9216]
            for ct in range(NFT):
                xp = self.xres[:, ct * 512:(ct + 1) * 512]
                v.tensor_tensor(out=xxp[:, 1:512], in0=xp[:, 0:511], in1=xp[:, 1:512],
                                op=OP.subtract)
                self.fence(v, "v", v.tensor_tensor(out=xxp[:, 0:1], in0=prev[:, ct:ct + 1],
                                                   in1=xp[:, 0:1], op=OP.subtract))
                tmcol = self.tmv[:, 0 * 16 + ct:0 * 16 + ct + 1]
                self.fence(v, "v", v.tensor_single_scalar(out=xxp, in_=xxp, scalar=tmcol,
                                                          op=OP.mult))
                self.fence(v, "v", v.tensor_tensor(
                    out=mixedin[:, ct * 512:(ct + 1) * 512], in0=xxp, in1=xp, op=OP.add))
            return None
        self.seq(("v", v_xxx))

        w1b = WS[:, 0:1280].bitcast(BF16)

        def g_w1(g):
            self.w(g, "stt")
            last = None
            for kt in range(NFT):
                last = g.dma_start(out=w1b[:, kt * 160:(kt + 1) * 160],
                                   in_=d["w1"][kt * P:(kt + 1) * P, :])
                if kt < NFT - 1:
                    self.inc(last, "ldg")
            return last
        self.seq(("g", g_w1))
        ld_ev = self.cnt["ldg"]

        @block.tensor
        def _(t):
            t.wait_ge(self.sems["ldg"], ld_ev)
            self.w(t, "sv")
            self.w(t, "ss")
            mm = None
            for kt in range(NFT):
                t.matmul(self.ps[0][:, 0:512], w1b[:, kt * 160:kt * 160 + 128],
                         mixedin[:, kt * 512:(kt + 1) * 512],
                         start=(kt == 0), stop=(kt == NFT - 1))
                mm = t.matmul(self.ps[1][0:32, 0:512], w1b[:, kt * 160 + 128:kt * 160 + 160],
                              mixedin[:, kt * 512:(kt + 1) * 512],
                              start=(kt == 0), stop=(kt == NFT - 1))
            self.inc(mm, "stt")
        mm_ev = self.cnt["stt"]

        @block.scalar
        def _(s):
            s.wait_ge(self.sems["stt"], mm_ev)
            s.activation(out=mTa, in_=self.ps[0][:, 0:512], func=ACT.Tanh)
            self.inc(s.activation(out=mTb, in_=self.ps[1][0:32, 0:512], func=ACT.Tanh), "ss")

    def _l1a_mix(self, pname, th):
        d, block = self.d, self.block
        PA = self.PA
        pidx = {"w": 1, "k": 2, "v": 3, "r": 4, "g": 5}[pname]
        widx = {"w": 0, "k": 1, "v": 2, "r": 3, "g": 4}[pname]
        mTa = PA[:, 10240:10496].bitcast(BF16)
        mTb = PA[0:32, 10752:11008].bitcast(BF16)
        mixedin = self.mixedin
        wpA = PA[:, 11264:12288].bitcast(BF16)
        wpB = self.WS[0:32, 0:1024].bitcast(BF16)

        def g_w2(g, widx=widx):
            self.w(g, "stt")
            self.w(g, "sv")
            self.w(g, "ss")
            self.inc(g.dma_start(out=wpA, in_=d["w2pad"][widx, 0:128, :]), "ldg")
            return g.dma_start(out=wpB, in_=d["w2pad"][widx, 128:160, :])
        self.seq(("g", g_w2))
        ld_ev = self.cnt["ldg"]

        for ct in range(NFT):
            pm = self.ps[2 + ct % 2]

            @block.tensor
            def _(t, ct=ct, pm=pm, ld_ev=ld_ev):
                t.wait_ge(self.sems["ldg"], ld_ev)
                self.w(t, "sv")
                t.matmul(pm[:, 0:512], wpA[:, ct * P:(ct + 1) * P], mTa,
                         start=True, stop=False)
                self.inc(t.matmul(pm[:, 0:512], wpB[:, ct * P:(ct + 1) * P], mTb,
                                  start=False, stop=True), "stt")
            mm_ev = self.cnt["stt"]

            @block.vector
            def _(v, ct=ct, pm=pm, pidx=pidx, th=th):
                v.wait_ge(self.sems["stt"], mm_ev)
                e1 = PA[:, 8192:8704]
                xxp = PA[:, 8704:9216]
                xp = self.xres[:, ct * 512:(ct + 1) * 512]
                prev = self.hcol if th == 0 else self.hprev
                v.tensor_tensor(out=xxp[:, 1:512], in0=xp[:, 0:511], in1=xp[:, 1:512],
                                op=OP.subtract)
                self.fence(v, "v", v.tensor_tensor(out=xxp[:, 0:1], in0=prev[:, ct:ct + 1],
                                                   in1=xp[:, 0:1], op=OP.subtract))
                tmcol = self.tmv[:, pidx * 16 + ct:pidx * 16 + ct + 1]
                self.fence(v, "v", v.tensor_single_scalar(
                    out=e1, in_=pm[:, 0:512], scalar=tmcol, op=OP.add))
                self.fence(v, "v", v.tensor_tensor(out=e1, in0=e1, in1=xxp, op=OP.mult))
                self.inc(v.tensor_tensor(
                    out=mixedin[:, ct * 512:(ct + 1) * 512],
                    in0=e1, in1=xp, op=OP.add), "sv")

    def _l1a_sweep(self, pname, th):
        if pname == "w":
            self._l1a_sweep_w(th)
        elif pname == "g":
            d = self.d

            def y_xgspill(sync, th=th):
                self.w(sync, "sv")
                last = None
                for ct in range(NFT):
                    last = sync.dma_start(
                        out=d["xgsp"][ct, :, th * 512:(th + 1) * 512],
                        in_=self.mixedin[:, ct * 512:(ct + 1) * 512])
                    if ct < NFT - 1:
                        self.inc(last, "st")
                return last
            self.seq(("y", y_xgspill))
        elif pname == "k":
            self._l1a_sweep_big("wk", th, out="kT")
        elif pname == "v":
            self._l1a_sweep_big("wv", th, out="vsp")
        elif pname == "r":
            self._l1a_sweep_big("wr", th, out="rT")

    def _l1a_sweep_big(self, wname, th, out, spdst=None):
        d, block = self.d, self.block
        WS = self.WS
        mixedin = self.mixedin
        wslot = [WS[:, 1024:1536].bitcast(BF16), WS[:, 1536:2048].bitcast(BF16)]
        banks = [self.psA[:, 0:512], self.psA[:, 512:1024],
                 self.psS[:, 0:512], self.psS[:, 512:1024],
                 self.ps[0][:, 0:512], self.ps[1][:, 0:512],
                 self.ps[2][:, 0:512], self.ps[3][:, 0:512]]
        natural = (out == "vsp")
        for oh in range(2):
            base_ev = self.cnt["sv"]
            for kt in range(NFT):
                slot = kt % 2

                @block.gpsimd
                def _(g, oh=oh, kt=kt, slot=slot):
                    self.w(g, "stt")
                    self.inc(g.dma_start(
                        out=wslot[slot],
                        in_=d[wname][kt * P:(kt + 1) * P, oh * 1024:oh * 1024 + 1024]), "ldg")
                ld_ev = self.cnt["ldg"]

                @block.tensor
                def _(t, oh=oh, kt=kt, slot=slot, base_ev=base_ev):
                    t.wait_ge(self.sems["ldg"], ld_ev)
                    if kt == 0:
                        t.wait_ge(self.sems["sv"], base_ev)
                    mm = None
                    if natural:
                        for ttl in range(4):
                            for o2 in range(2):
                                mm = t.matmul(banks[ttl * 2 + o2],
                                              mixedin[:, (kt * 4 + ttl) * P:(kt * 4 + ttl) * P + P],
                                              wslot[slot][:, o2 * 512:o2 * 512 + 512],
                                              start=(kt == 0), stop=(kt == NFT - 1))
                    else:
                        for fo in range(8):
                            mm = t.matmul(banks[fo],
                                          wslot[slot][:, fo * P:(fo + 1) * P],
                                          mixedin[:, kt * 512:(kt + 1) * 512],
                                          start=(kt == 0), stop=(kt == NFT - 1))
                    self.inc(mm, "stt")
            mm_ev = self.cnt["stt"]

            if natural or out == "spill":
                stage = self.PA[:, 11264:12288].bitcast(BF16)
                for grp in range(2):
                    @block.vector
                    def _(v, oh=oh, grp=grp):
                        v.wait_ge(self.sems["stt"], mm_ev)
                        self.w(v, "ldg")
                        last = None
                        for j in range(4):
                            last = v.tensor_copy(out=stage[:, j * 512:(j + 1) * 512],
                                                 in_=banks[grp * 4 + j])
                        self.inc(last, "sv")
                    sv_ev = self.cnt["sv"]

                    @block.gpsimd
                    def _(g, oh=oh, grp=grp, sv_ev=sv_ev):
                        g.wait_ge(self.sems["sv"], sv_ev)
                        if natural:
                            self.inc(g.dma_start(
                                out=d["vsp"][th * 4 + grp * 2:th * 4 + grp * 2 + 2, :,
                                             oh * 1024:oh * 1024 + 1024]
                                    .rearrange("ttl p s -> p ttl s"),
                                in_=stage.rearrange("p (ttl s) -> p ttl s", s=1024)), "ldg")
                        else:
                            fo0 = oh * 8 + grp * 4
                            self.inc(g.dma_start(
                                out=d["xgsp"][fo0:fo0 + 4, :, th * 512:(th + 1) * 512]
                                    .rearrange("fo p s -> p fo s"),
                                in_=stage.rearrange("p (fo s) -> p fo s", s=512)), "ldg")
            else:
                @block.vector
                def _(v, oh=oh):
                    v.wait_ge(self.sems["stt"], mm_ev)
                    last = None
                    for fo in range(8):
                        fog = oh * 8 + fo
                        dst = self.kTf if out == "kT" else self.rTf
                        dv = dst.rearrange("p (ft tt s) -> p ft tt s", tt=NTT, s=P)
                        last = v.tensor_copy(
                            out=dv[:, fog, th * 4:(th + 1) * 4, :],
                            in_=banks[fo].rearrange("p (ttl s) -> p ttl s", s=P))
                    self.inc(last, "sv")

    def _l1a_sweep_w(self, th):
        d, block = self.d, self.block
        PA, WS = self.PA, self.WS
        mixedin = self.mixedin
        tw1b = WS[:, 1280:1792].bitcast(BF16)
        twb = PA[64:128, 10752:11008].bitcast(BF16)
        tw2b = WS[64:128, 0:1024].bitcast(BF16)

        def g_td(g):
            self.w(g, "stt")
            for kt in range(NFT):
                self.inc(g.dma_start(out=tw1b[:, kt * 64:(kt + 1) * 64],
                                     in_=d["td_w1"][kt * P:(kt + 1) * P, :]), "ldg")
            return g.dma_start(out=tw2b, in_=d["td_w2"][:, :])
        self.seq(("g", g_td))
        ld_ev = self.cnt["ldg"]

        @block.tensor
        def _(t):
            t.wait_ge(self.sems["ldg"], ld_ev)
            self.w(t, "sv")
            mm = None
            for kt in range(NFT):
                mm = t.matmul(self.ps[0][64:128, 0:512], tw1b[:, kt * 64:(kt + 1) * 64],
                              mixedin[:, kt * 512:(kt + 1) * 512],
                              start=(kt == 0), stop=(kt == NFT - 1))
            self.inc(mm, "stt")
        mm_ev = self.cnt["stt"]

        @block.scalar
        def _(s):
            s.wait_ge(self.sems["stt"], mm_ev)
            self.inc(s.activation(out=twb, in_=self.ps[0][64:128, 0:512], func=ACT.Tanh), "ss")
        tw_ev = self.cnt["ss"]

        dbuf = PA[:, 11264:11776]
        cbuf = PA[:, 11776:12288]
        for fo in range(NFT):
            pw = self.ps[2 + fo % 2]

            @block.tensor
            def _(t, fo=fo, pw=pw):
                self.w(t, "ss")
                self.w(t, "sv")
                self.inc(t.matmul(pw[64:128, 0:512] if False else pw[:, 0:512],
                                  tw2b[:, fo * P:(fo + 1) * P], twb,
                                  start=True, stop=True), "stt")
            mm_ev2 = self.cnt["stt"]

            @block.scalar
            def _(s, fo=fo, pw=pw):
                s.wait_ge(self.sems["stt"], mm_ev2)
                self.w(s, "sv")
                tdcol = self.tmv[:, 6 * 16 + fo:6 * 16 + fo + 1]
                self.inc(s.activation(out=dbuf, in_=pw[:, 0:512], func=ACT.Exp,
                                      bias=tdcol), "ss")
            d_ev = self.cnt["ss"]

            @block.vector
            def _(v, fo=fo, th=th):
                v.wait_ge(self.sems["ss"], d_ev)
                last = None
                for ch in range(8):
                    last = v.tensor_tensor_scan(
                        out=cbuf[:, ch * 64:(ch + 1) * 64],
                        data0=dbuf[:, ch * 64:(ch + 1) * 64],
                        data1=self.zbuf,
                        initial=0.0, op0=OP.add, op1=OP.add)
                self.fence(v, "v", last)
                gch0 = th * 8
                self.fence(v, "v", v.tensor_copy(
                    out=self.cLb[:, fo * 16 + gch0:fo * 16 + gch0 + 8],
                    in_=cbuf.rearrange("p (ch s) -> p ch s", s=64)[:, :, 63]))
                return v.tensor_tensor(out=dbuf, in0=dbuf, in1=cbuf, op=OP.subtract)
            cd_ev = self.cnt["sv"]

            @block.scalar
            def _(s, fo=fo, th=th):
                s.wait_ge(self.sems["sv"], cd_ev)
                dst0 = self.m0f.rearrange("p (ft tt s) -> p ft tt s", tt=NTT, s=P)
                dst1 = self.m1f.rearrange("p (ft tt s) -> p ft tt s", tt=NTT, s=P)
                s.activation(out=dst0[:, fo, th * 4:(th + 1) * 4, :],
                             in_=dbuf.rearrange("p (a b) -> p a b", b=P), func=ACT.Exp)
                self.inc(s.activation(out=dst1[:, fo, th * 4:(th + 1) * 4, :],
                                      in_=cbuf.rearrange("p (a b) -> p a b", b=P),
                                      func=ACT.Exp), "ss")
        def s_al(s, th=th):
            gch0 = th * 8
            alv = self.ALb.rearrange("p (f c) -> p f c", c=16)[:, :, gch0:gch0 + 8]
            clv = self.cLb.rearrange("p (f c) -> p f c", c=16)[:, :, gch0:gch0 + 8]
            return s.activation(out=alv, in_=clv, func=ACT.Exp, scale=self.cc(-1.0))
        self.seq(("s", s_al))



    def _l1a_pass1(self, nch=16, exch=True, csteps=10):
        d, block = self.d, self.block
        PA = self.PA
        Sb = PA[:, 8192:9216]

        msb = PA[:, 9216:10240]
        Pmb = [PA[:, 10240:10752].bitcast(BF16), PA[:, 10752:11264].bitcast(BF16)]
        khn = PA[:, 10240:11264].bitcast(BF16)
        tb2 = PA[:, 14336:14848].bitcast(BF16)
        kpb = PA[:, 14848:15360].bitcast(BF16)
        khb = PA[:, 15360:15872].bitcast(BF16)
        dsb = PA[0:32, 15872:15936]
        dTs = PA[:, 15936:15968]
        Sbc = PA[:, 11264:11776].bitcast(BF16)
        ybuf = PA[:, 12288:14336]
        self.Sb, self.ybuf = Sb, ybuf

        def y_pre(sync):
            self.w(sync, "stt"); self.w(sync, "sv"); self.w(sync, "ss")
            for tt in range(NTT):
                self.inc(sync.dma_start(out=self.vnf[:, tt * 2048:(tt + 1) * 2048],
                                        in_=d["vsp"][tt]), "ldh")
            return sync.dma_start(out=msb, in_=d["masksub"])
        self.seq(("y", y_pre))

        def g_ubd(g):
            self.w(g, "stt"); self.w(g, "sv")
            last = None
            for ft in range(NFT):
                last = g.dma_start(out=self.ubdb[:, ft * 32:(ft + 1) * 32],
                                   in_=d["u_bd"][ft * P:(ft + 1) * P, :])
                if ft < NFT - 1:
                    self.inc(last, "ldg")
            return last
        self.seq(("g", g_ubd))

        def v_init(v):
            self.fence(v, "v", v.memset(Sb, 0.0))
            self.fence(v, "v", v.memset(Sbc, 0.0))
            lam = self.Lam.rearrange("p (f c) -> p f c", c=16)
            return v.memset(lam[:, :, 0], 1.0)
        self.seq(("v", v_init))

        vstage2 = self.WS[0:64, 256:1280].bitcast(BF16)
        vhi = self.WS[64:128, 256:768].bitcast(BF16)
        for ct in range(nch):
            tt, rb = ct // 2, 0

            def cview(buf):
                return buf.rearrange("p (ft tt u s) -> p ft tt u s",
                                     tt=NTT, u=2, s=64)[:, :, tt, ct % 2, :]
            kv, rv = cview(self.kTf), cview(self.rTf)
            m0v, m1v = cview(self.m0f), cview(self.m1f)
            alv = self.ALb.rearrange("p (f c) -> p f c", c=16)[:, :, ct:ct + 1].broadcast_to([P, 16, 64])
            tb2v = tb2.rearrange("p (f s) -> p f s", s=64)
            kpv = kpb.rearrange("p (f s) -> p f s", s=64)
            khv = khb.rearrange("p (f s) -> p f s", s=64)

            def v_prep(v, kv=kv, rv=rv, m0v=m0v, m1v=m1v, alv=alv):
                self.w(v, "stt")
                v.tensor_tensor(out=tb2v, in0=rv, in1=kv, op=OP.mult)
                v.tensor_tensor(out=kpv, in0=kv, in1=m1v, op=OP.mult)
                self.fence(v, "v", v.tensor_tensor(out=khv, in0=kpv, in1=alv, op=OP.mult))
                return v.tensor_tensor(out=rv, in0=rv, in1=m0v, op=OP.mult)
            self.seq(("v", v_prep))
            prep_ev = self.cnt["sv"]
            if csteps < 2:
                continue

            if ct % 2 == 1:
                def y_vst(sync, tt=tt):
                    self.w(sync, "stt")
                    return sync.dma_start(out=vstage2,
                                          in_=self.vnf[64:128, tt * 2048:(tt + 1) * 2048])
                self.seq(("y", y_vst))
            vst_ev = self.cnt["ldh"]

            def v_vhi(v, ct=ct, tt=tt, vst_ev=vst_ev):
                v.wait_ge(self.sems["ldh"], vst_ev)
                vsrc = (self.vnf[0:64, tt * 2048:(tt + 1) * 2048] if ct % 2 == 0
                        else vstage2)
                vv = vsrc.rearrange("p (f u s) -> p f u s", u=2, s=64)
                return v.tensor_copy(out=vhi.rearrange("p (f s) -> p f s", s=64),
                                     in_=vv[:, :, 1, :])
            self.seq(("v", v_vhi))

            def vsl(ft, sub, tt=tt, ct=ct):
                if ct % 2 == 0:
                    return self.vnf[0:64, (tt * 16 + ft) * P + sub * 64:
                                    (tt * 16 + ft) * P + sub * 64 + 64]
                return vstage2[0:64, ft * P + sub * 64:ft * P + sub * 64 + 64]

            @block.tensor
            def _(t, ct=ct, tt=tt, rb=rb):
                t.wait_ge(self.sems["sv"], prep_ev)
                mm = None
                for ft in range(NFT):
                    mm = t.matmul(self.ps[2][0:32, 0:64], self.ubdb[:, ft * 32:(ft + 1) * 32],
                                  tb2[:, ft * 64:(ft + 1) * 64],
                                  start=(ft == 0), stop=(ft == NFT - 1))
                self.inc(mm, "stt")
                for ft in range(NFT):
                    for sub in range(2):
                        pb = [self.psA, self.psS][sub]
                        qslice = self.rTf[sub * 64:sub * 64 + 64,
                                          ((ft * NTT + tt) * P + (ct % 2) * 64) * 1:
                                          (ft * NTT + tt) * P + (ct % 2) * 64 + 64]
                        mm = t.matmul(pb[sub * 64:sub * 64 + 64, ft * 64:(ft + 1) * 64],
                                      kpb[sub * 64:sub * 64 + 64, ft * 64:(ft + 1) * 64],
                                      qslice, start=True, stop=True)
                self.inc(mm, "stt")
            pmm_ev = self.cnt["stt"]
            if csteps < 3:
                continue

            def v_de(v, rb=rb):
                v.wait_ge(self.sems["stt"], pmm_ev)
                v.tensor_copy(out=dsb, in_=self.ps[2][0:32, 0:64])
                v.tensor_tensor(out=Pmb[0][0:64, :], in0=self.psA[0:64, :],
                                in1=msb[0:64, :], op=OP.mult)
                return v.tensor_tensor(out=Pmb[1][64:128, :], in0=self.psS[64:128, :],
                                       in1=msb[64:128, :], op=OP.mult)
            self.seq(("v", v_de))
            if csteps < 4:
                continue

            def t_dtr(t, rb=rb):
                self.w(t, "sv")
                return t.transpose(self.ps[3][rb:rb + 64, 0:32], dsb, self.identf[0:32, 0:32])

            def v_dev(v, rb=rb):
                return v.tensor_copy(out=dTs[rb:rb + 64, :], in_=self.ps[3][rb:rb + 64, 0:32])
            self.seq(("t", t_dtr), ("v", v_dev))
            pm_ev = self.cnt["sv"]
            if csteps < 5:
                continue

            @block.tensor
            def _(t, ct=ct, tt=tt, rb=rb, vsl=vsl, vst_ev=vst_ev):
                t.wait_ge(self.sems["sv"], pm_ev)
                t.wait_ge(self.sems["ldh"], vst_ev)
                mm = None
                for ft in range(NFT):
                    for sub in range(2):
                        pb = [self.psA, self.psS][sub]
                        ysl = pb[rb:rb + 64, ft * 64:(ft + 1) * 64]
                        sl = slice(sub * 64, sub * 64 + 64)
                        t.matmul(ysl, Pmb[sub][sl, ft * 64:(ft + 1) * 64],
                                 vsl(ft, 0) if sub == 0 else vhi[:, ft * 64:(ft + 1) * 64],
                                 start=True, stop=False)
                        qslice = self.rTf[sl,
                                          (ft * NTT + tt) * P + (ct % 2) * 64:
                                          (ft * NTT + tt) * P + (ct % 2) * 64 + 64]
                        mm = t.matmul(ysl, qslice,
                                      Sbc[sl, ft * 64:(ft + 1) * 64],
                                      start=False, stop=True)
                self.inc(mm, "stt")
            ymm_ev = self.cnt["stt"]
            if csteps < 6:
                continue

            def v_ye(v, ct=ct, tt=tt, rb=rb):
                v.wait_ge(self.sems["stt"], ymm_ev)
                yscr = PA[64:128, 12288:14336]
                vsrc = (self.vnf[0:64, tt * 2048:(tt + 1) * 2048] if ct % 2 == 0
                        else vstage2)
                vvw = vsrc.rearrange("p (f u s) -> p f u s", u=2, s=64)
                ybw = ybuf[rb:rb + 64, :].rearrange("p (f u s) -> p f u s", u=2, s=64)
                ysw = yscr.rearrange("p (f u s) -> p f u s", u=2, s=64)
                dtw = dTs[rb:rb + 64, :].rearrange("p (f u) -> p f u", u=2)
                last = None
                for sub in range(2):
                    dbc = dtw[:, :, sub:sub + 1].broadcast_to([64, 16, 64])
                    self.fence(v, "v", v.tensor_tensor(out=ysw[:, :, sub, :],
                                                       in0=vvw[:, :, sub, :], in1=dbc, op=OP.mult))
                    pb = [self.psA, self.psS][sub]
                    last = v.tensor_tensor(
                        out=ybw[:, :, sub, :], in0=ysw[:, :, sub, :],
                        in1=pb[rb:rb + 64, :].rearrange("p (f s) -> p f s", s=64), op=OP.add)
                self.inc(last, "sv")
            self.seq(("v", v_ye))
            ye_ev = self.cnt["sv"]
            if csteps < 7:
                continue

            @block.gpsimd
            def _(g, ct=ct, rb=rb):
                g.wait_ge(self.sems["sv"], ye_ev)
                self.inc(g.dma_start(out=d["ysp"][ct], in_=ybuf[rb:rb + 64, :]), "ldg")
            if csteps < 8:
                continue

            for ft in range(NFT):
                pst = self.ps[ft % 2][:, 0:64].bitcast(BF16)

                def t_ktr(t, ft=ft, pst=pst, rb=rb):
                    self.w(t, "sv")
                    return t.transpose(pst[rb:rb + 64, 0:128] if False else
                                       self.ps[ft % 2].bitcast(BF16)[rb:rb + 64, 0:128],
                                       khb[:, ft * 64:(ft + 1) * 64], self.identb)

                def v_kev(v, ft=ft, rb=rb):
                    return v.tensor_copy(out=khn[rb:rb + 64, ft * P:(ft + 1) * P],
                                         in_=self.ps[ft % 2].bitcast(BF16)[rb:rb + 64, 0:128])
                self.seq(("t", t_ktr), ("v", v_kev))
            ktr_ev = self.cnt["sv"]
            if csteps < 9:
                continue

            @block.tensor
            def _(t, ct=ct, tt=tt, rb=rb, vsl=vsl):
                t.wait_ge(self.sems["sv"], ktr_ev)
                mm = None
                for ft in range(NFT):
                    for sub in range(2):
                        pb = [self.psA, self.psS][sub]
                        mm = t.matmul(pb[sub * 64:sub * 64 + 64, ft * 64:(ft + 1) * 64],
                                      khn[rb:rb + 64, ft * P + sub * 64:ft * P + sub * 64 + 64],
                                      vsl(ft, sub),
                                      start=True, stop=True)
                self.inc(mm, "stt")
            kv_ev = self.cnt["stt"]
            if csteps < 10:
                continue

            def v_su(v, ct=ct):
                v.wait_ge(self.sems["stt"], kv_ev)
                albc = self.ALb.rearrange("p (f c) -> p f c", c=16)[:, :, ct:ct + 1]
                for sub in range(2):
                    pb = [self.psA, self.psS][sub]
                    sl = slice(sub * 64, sub * 64 + 64)
                    sv3 = Sb[sl, :].rearrange("p (f s) -> p f s", s=64)
                    self.fence(v, "v", v.tensor_tensor(
                        out=sv3, in0=sv3, in1=albc[sl].broadcast_to([64, 16, 64]), op=OP.mult))
                    self.fence(v, "v", v.tensor_tensor(
                        out=Sb[sl, :], in0=Sb[sl, :], in1=pb[sl, :], op=OP.add))
                self.fence(v, "v", v.tensor_copy(out=Sbc, in_=Sb))
                if ct < 15:
                    lam = self.Lam.rearrange("p (f c) -> p f c", c=16)
                    alv2 = self.ALb.rearrange("p (f c) -> p f c", c=16)
                    return v.tensor_tensor(out=lam[:, :, ct + 1], in0=lam[:, :, ct],
                                           in1=alv2[:, :, ct], op=OP.mult)
                return None
            self.seq(("v", v_su))

        if not exch:
            return
        sbb = PA[:, 14336:15360]

        def v_msk(v):
            return v.tensor_single_scalar(out=sbb, in_=Sb, scalar=self.mfirst, op=OP.mult)
        self.seq(("v", v_msk))
        msk_ev = self.cnt["sv"]

        @block.gpsimd
        def _(g):
            g.wait_ge(self.sems["sv"], msk_ev)
            self.inc(g.dma_start(out=d["ccs_i"], in_=sbb), "ldg")
            self.w(g, "ldg")
            self.inc(g.collective_compute(
                "AllReduce", OP.add, replica_groups=[[0, 1], [2, 3], [4, 5], [6, 7]],
                ins=[d["ccs_i"].opt()], outs=[d["ccs_o"].opt()]), "cc")
            self.w(g, "cc")
            self.inc(g.dma_start(out=Sb, in_=d["ccs_o"]), "ldg")
        sl_ev = self.cnt["ldg"]

        def v_msk2(v):
            v.wait_ge(self.sems["ldg"], sl_ev)
            return v.tensor_single_scalar(out=Sb, in_=Sb, scalar=self.msecond, op=OP.mult)
        self.seq(("v", v_msk2))

    def Sbb(self, sub, ft):
        return self.Sb[sub * 64:sub * 64 + 64, ft * 64:(ft + 1) * 64]

    def _alcol(self, ct, sl):
        return self.ALb.rearrange("p (f c) -> p f c", c=16)[:, :, ct][sl, 0:0 + 16][:, 0:1] if False \
            else self.ALb[sl, ct:ct + 1]

    def _l1a_pass2(self):
        d, block = self.d, self.block
        PA = self.PA
        Sb, ybuf = self.Sb, self.ybuf
        stmp = PA[:, 14336:14848].bitcast(BF16)
        gnm = PA[:, 15360:15968]
        ynb = PA[:, 15968:16384].bitcast(BF16)

        self.lnwb = PA[:, 9216:11264]
        self.lnbb = PA[:, 11264:13312] if False else None

        def y_ln(sync):
            self.w(sync, "sv"); self.w(sync, "stt")
            return sync.dma_start(out=self.lnwb, in_=d["lnw_rep"])
        self.seq(("y", y_ln))
        self.lnbb = PB_lnb = self.PB[:, 0:2048]

        def y_ln2(sync):
            self.w(sync, "sv"); self.w(sync, "stt")
            return sync.dma_start(out=self.lnbb, in_=d["lnb_rep"])
        self.seq(("y", y_ln2))

        for ct in range(16):
            tt, rb = ct // 2, 0

            @block.gpsimd
            def _(g, ct=ct, rb=rb):
                self.w(g, "sv")
                self.inc(g.dma_start(out=ybuf[rb:rb + 64, :], in_=d["ysp"][ct]), "ldg")
            yld_ev = self.cnt["ldg"]

            def v_st(v, ct=ct):
                self.w(v, "stt")
                self.w(v, "st")
                lamv = self.Lam.rearrange("p (f c) -> p f c", c=16)[:, :, ct:ct + 1]
                return v.tensor_tensor(out=stmp.rearrange("p (f s) -> p f s", s=64),
                                       in0=Sb.rearrange("p (f s) -> p f s", s=64),
                                       in1=lamv.broadcast_to([P, 16, 64]), op=OP.mult)
            self.seq(("v", v_st))
            st_ev = self.cnt["sv"]

            @block.tensor
            def _(t, ct=ct, tt=tt, rb=rb):
                t.wait_ge(self.sems["sv"], st_ev)
                mm = None
                for ft in range(NFT):
                    for sub in range(2):
                        pb = [self.psA, self.psS][sub]
                        qslice = self.rTf[sub * 64:sub * 64 + 64,
                                          (ft * NTT + tt) * P + (ct % 2) * 64:
                                          (ft * NTT + tt) * P + (ct % 2) * 64 + 64]
                        mm = t.matmul(pb[rb:rb + 64, ft * 64:(ft + 1) * 64], qslice,
                                      stmp[sub * 64:sub * 64 + 64, ft * 64:(ft + 1) * 64],
                                      start=True, stop=True)
                self.inc(mm, "stt")
            t3_ev = self.cnt["stt"]

            def v_add(v, ct=ct, rb=rb):
                v.wait_ge(self.sems["stt"], t3_ev)
                v.wait_ge(self.sems["ldg"], yld_ev)
                ybw = ybuf[rb:rb + 64, :].rearrange("p (f u s) -> p f u s", u=2, s=64)
                last = None
                for sub in range(2):
                    pb = [self.psA, self.psS][sub]
                    last = v.tensor_tensor(
                        out=ybw[:, :, sub, :], in0=ybw[:, :, sub, :],
                        in1=pb[rb:rb + 64, :].rearrange("p (f s) -> p f s", s=64), op=OP.add)
                self.inc(last, "sv")
            self.seq(("v", v_add))

            mu = gnm[rb:rb + 64, 0:32]
            s2c = gnm[rb:rb + 64, 32:64]
            rs = gnm[rb:rb + 64, 64:96]
            yv = ybuf[rb:rb + 64, :].rearrange("p (h n) -> p h n", n=64)

            def v_gn1(v):
                self.fence(v, "v", v.tensor_reduce(out=mu, in_=yv, axis=AX.X, op=OP.add))
                return v.tensor_single_scalar(out=mu, in_=mu, scalar=1.0 / 64.0, op=OP.mult)
            self.seq(("v", v_gn1))

            def s_gn2(s, rb=rb):
                return s.activation(out=self._gnscrap(rb), in_=yv, func=ACT.Square)
            self.seq(("s", s_gn2))

            def v_gn2b(v, rb=rb):
                return v.tensor_reduce(out=s2c, in_=self._gnscrap(rb), axis=AX.X, op=OP.add)
            self.seq(("v", v_gn2b))

            def v_gn3(v):
                self.fence(v, "v", v.tensor_single_scalar(out=s2c, in_=s2c,
                                                          scalar=1.0 / 64.0, op=OP.mult))
                self.fence(v, "v", v.scalar_tensor_tensor(out=rs, in0=mu, scalar=-1.0,
                                                          in1=mu, op0=OP.mult, op1=OP.mult))
                return v.tensor_tensor(out=rs, in0=s2c, in1=rs, op=OP.add)
            self.seq(("v", v_gn3))

            def s_gn4(s, rb=rb):
                return s.activation(out=rs, in_=rs, func=ACT.Sqrt,
                                    bias=self.epsgn[rb:rb + 64, :])
            self.seq(("s", s_gn4))

            def v_gn5(v, ct=ct, rb=rb):
                self.fence(v, "v", v.reciprocal(out=rs, in_=rs))
                mub = mu.unsqueeze(2).broadcast_to([64, 32, 64])
                rsb = rs.unsqueeze(2).broadcast_to([64, 32, 64])
                self.fence(v, "v", v.tensor_tensor(out=yv, in0=yv, in1=mub, op=OP.subtract))
                self.fence(v, "v", v.tensor_tensor(out=yv, in0=yv, in1=rsb, op=OP.mult))
                lnw = d_lnw = self.lnwb[rb:rb + 64, :]
                self.fence(v, "v", v.tensor_tensor(out=ybuf[rb:rb + 64, :],
                                                   in0=ybuf[rb:rb + 64, :], in1=lnw, op=OP.mult))
                return v.tensor_tensor(out=self._ynb(rb), in0=ybuf[rb:rb + 64, :],
                                       in1=self.lnbb[rb:rb + 64, :], op=OP.add)
            self.seq(("v", v_gn5))
            yn_ev = self.cnt["sv"]

            @block.sync
            def _(sync, ct=ct, rb=rb):
                sync.wait_ge(self.sems["sv"], yn_ev)
                self.inc(sync.dma_start(out=d["ynsp"][ct], in_=self._ynb(rb)), "st")

    def _gnscrap(self, rb):
        return self.PA[64:128, 12288:14336].rearrange("p (h n) -> p h n", n=64)

    def _ynb(self, rb):
        return self.PA[rb:rb + 64, 14336:15360].bitcast(BF16)

    def _l1a_end(self):
        d, block = self.d, self.block
        PA, PB, PC, WS = self.PA, self.PB, self.PC, self.WS
        xgT = PB.bitcast(BF16)
        gsi = PC.bitcast(BF16)
        ynT = PA[:, 0:8192].bitcast(BF16)
        ytile = PA[:, 12288:13312].bitcast(BF16)
        wslot = [WS[:, 1024:1536].bitcast(BF16), WS[:, 1536:2048].bitcast(BF16)]
        banks = [self.psA[:, 0:512], self.psA[:, 512:1024],
                 self.psS[:, 0:512], self.psS[:, 512:1024],
                 self.ps[0][:, 0:512], self.ps[1][:, 0:512],
                 self.ps[2][:, 0:512], self.ps[3][:, 0:512]]

        def y_xg(sync):
            self.w(sync, "stt"); self.w(sync, "sv")
            last = None
            for fo in range(NFT):
                last = sync.dma_start(out=xgT[:, fo * 1024:(fo + 1) * 1024], in_=d["xgsp"][fo])
                if fo < NFT - 1:
                    self.inc(last, "ldh")
            return last
        self.seq(("y", y_xg))
        xg_ev = self.cnt["ldh"]

        for oh in range(2):
            for th2 in range(2):
                base_ev = self.cnt["ss"]
                for kt in range(NFT):
                    slot = kt % 2

                    @block.gpsimd
                    def _(g, oh=oh, kt=kt, slot=slot):
                        self.w(g, "stt")
                        self.inc(g.dma_start(
                            out=wslot[slot],
                            in_=d["wg2"][kt * P:(kt + 1) * P, oh * 1024:oh * 1024 + 1024]), "ldg")
                    ld_ev = self.cnt["ldg"]

                    @block.tensor
                    def _(t, oh=oh, th2=th2, kt=kt, slot=slot, base_ev=base_ev):
                        t.wait_ge(self.sems["ldg"], ld_ev)
                        t.wait_ge(self.sems["ldh"], xg_ev)
                        if kt == 0:
                            t.wait_ge(self.sems["ss"], base_ev)
                        mm = None
                        for fo in range(8):
                            mm = t.matmul(banks[fo], wslot[slot][:, fo * P:(fo + 1) * P],
                                          xgT[:, kt * 1024 + th2 * 512:kt * 1024 + th2 * 512 + 512],
                                          start=(kt == 0), stop=(kt == NFT - 1))
                        self.inc(mm, "stt")
                mm_ev = self.cnt["stt"]

                @block.scalar
                def _(s, oh=oh, th2=th2):
                    s.wait_ge(self.sems["stt"], mm_ev)
                    last = None
                    for fo in range(8):
                        fog = oh * 8 + fo
                        last = s.activation(
                            out=gsi[:, fog * 1024 + th2 * 512:fog * 1024 + th2 * 512 + 512],
                            in_=banks[fo], func=ACT.Silu)
                    self.inc(last, "ss")

        for tt in range(NTT):
            def y_yn(sync, tt=tt):
                self.w(sync, "stt")
                self.inc(sync.dma_start(out=ytile[0:64, :], in_=d["ynsp"][2 * tt]), "ldh")
                return sync.dma_start(out=ytile[64:128, :], in_=d["ynsp"][2 * tt + 1])
            self.seq(("y", y_yn))
            for ft in range(NFT):
                pst = self.ps[ft % 2].bitcast(BF16)[:, 0:128]

                def t_tr(t, ft=ft, pst=pst):
                    self.w(t, "sv")
                    self.w(t, "ldh")
                    return t.transpose(pst, ytile[:, ft * P:(ft + 1) * P], self.identb)

                def v_ev(v, ft=ft, tt=tt, pst=pst):
                    c = (ft * NTT + tt) * P
                    return v.tensor_copy(out=ynT[:, c:c + P], in_=pst)
                self.seq(("t", t_tr), ("v", v_ev))

        def v_zz(v):
            self.w(v, "ss")
            last = None
            for ft in range(NFT):
                last = v.tensor_tensor(out=ynT[:, ft * 1024:(ft + 1) * 1024],
                                       in0=ynT[:, ft * 1024:(ft + 1) * 1024],
                                       in1=gsi[:, ft * 1024:(ft + 1) * 1024], op=OP.mult)
            self.inc(last, "sv")
        self.seq(("v", v_zz))

        def y_xr(sync):
            self.w(sync, "sv"); self.w(sync, "stt")
            return sync.dma_start(
                out=self.xres[:, :].rearrange("p (tt dd) -> p tt dd", dd=D),
                in_=d["xspill"].rearrange("(tt p) dd -> p tt dd", p=P))
        self.seq(("y", y_xr))
        xr_ev = self.cnt["ldh"]

        for o2 in range(4):
            base_ev = self.cnt["sv"]
            for ft in range(NFT):
                slot = ft % 2

                @block.gpsimd
                def _(g, o2=o2, ft=ft, slot=slot):
                    self.w(g, "stt")
                    self.inc(g.dma_start(
                        out=wslot[slot][:, 0:512],
                        in_=d["wo"][ft * P:(ft + 1) * P, o2 * 512:(o2 + 1) * 512]), "ldg")
                ld_ev = self.cnt["ldg"]

                @block.tensor
                def _(t, o2=o2, ft=ft, slot=slot, base_ev=base_ev):
                    t.wait_ge(self.sems["ldg"], ld_ev)
                    t.wait_ge(self.sems["ldh"], xr_ev)
                    if ft == 0:
                        t.wait_ge(self.sems["sv"], base_ev)
                    mm = None
                    for tt in range(NTT):
                        mm = t.matmul(banks[tt],
                                      ynT[:, ft * 1024 + tt * P:ft * 1024 + tt * P + P],
                                      wslot[slot][:, 0:512],
                                      start=(ft == 0), stop=(ft == NFT - 1))
                    self.inc(mm, "stt")
            mm_ev = self.cnt["stt"]

            @block.vector
            def _(v, o2=o2):
                v.wait_ge(self.sems["stt"], mm_ev)
                last = None
                for tt in range(NTT):
                    xsl = self.xres[:, tt * D + o2 * 512:tt * D + o2 * 512 + 512]
                    last = v.tensor_tensor(out=xsl, in0=banks[tt], in1=xsl, op=OP.add)
                self.inc(last, "sv")



def _host_inputs(inputs):
    x = np.ascontiguousarray(np.asarray(inputs["L0_attn_x"], np.float32))

    def T(a):
        return np.ascontiguousarray(np.asarray(a, np.float32).T)

    def rs(v):
        return np.ascontiguousarray(np.asarray(v, np.float32).reshape(NFT, P).T)

    common = {
        "W_dkv": inputs["L0_attn_W_dkv"], "W_dq": inputs["L0_attn_W_dq"],
        "W_uk": inputs["L0_attn_W_uk"], "W_uv": inputs["L0_attn_W_uv"],
        "W_uq": inputs["L0_attn_W_uq"], "W_kr": inputs["L0_attn_W_kr"],
        "W_qr": inputs["L0_attn_W_qr"], "W_out": inputs["L0_attn_W_out"],
        "w1": inputs["L1_attn_w1"], "w2": inputs["L1_attn_w2"],
        "td_w1": inputs["L1_attn_td_w1"], "td_w2": inputs["L1_attn_td_w2"],
        "wr": inputs["L1_attn_wr"], "wk": inputs["L1_attn_wk"],
        "wv": inputs["L1_attn_wv"], "wg2": inputs["L1_attn_wg"],
        "wo": inputs["L1_attn_wo"],
        "l1f_nat": inputs["L1_ffn_weight"], "l1f_T": T(inputs["L1_ffn_weight"]),
        "l1f_bias": np.asarray(inputs["L1_ffn_bias"], np.float32).reshape(1, D),
        "tm_x": rs(np.asarray(inputs["L1_attn_time_maa_x"]).reshape(-1)),
        "tm_w": rs(np.asarray(inputs["L1_attn_time_maa_w"]).reshape(-1)),
        "tm_k": rs(np.asarray(inputs["L1_attn_time_maa_k"]).reshape(-1)),
        "tm_v": rs(np.asarray(inputs["L1_attn_time_maa_v"]).reshape(-1)),
        "tm_r": rs(np.asarray(inputs["L1_attn_time_maa_r"]).reshape(-1)),
        "tm_g": rs(np.asarray(inputs["L1_attn_time_maa_g"]).reshape(-1)),
        "tdecay": rs(np.asarray(inputs["L1_attn_time_decay"]).reshape(-1)),
        "lnw_rep": np.ascontiguousarray(np.broadcast_to(
            np.asarray(inputs["L1_attn_ln_w"], np.float32)[None, :], (P, D))),
        "lnb_rep": np.ascontiguousarray(np.broadcast_to(
            np.asarray(inputs["L1_attn_ln_b"], np.float32)[None, :], (P, D))),
    }
    fa = np.asarray(inputs["L1_attn_time_faaaa"], np.float32)
    u_bd = np.zeros((D, 32), np.float32)
    for h in range(32):
        u_bd[h * 64:(h + 1) * 64, h] = fa[h]
    common["u_bd"] = u_bd
    w2 = np.asarray(inputs["L1_attn_w2"], np.float32)
    w2pad = np.zeros((5, 160, D), np.float32)
    for f in range(5):
        w2pad[f, f * 32:(f + 1) * 32, :] = w2[f]
    common["w2pad"] = w2pad
    su = np.triu(np.ones((64, 64), np.float32), 1)
    common["masksub"] = np.ascontiguousarray(np.tile(su, (2, 16)))
    common["ident"] = np.eye(P, dtype=np.float32)
    common["mask_su"] = np.ascontiguousarray(np.triu(np.ones((64, 64), np.float32), 1))
    common["blk4"] = np.ascontiguousarray(np.repeat(np.eye(4, dtype=np.float32), 32, axis=1))
    common["ones1"] = np.ones((1, P), np.float32)

    wgn = np.ascontiguousarray(np.asarray(inputs["L0_ffn_w_gate"], np.float32))
    wun = np.ascontiguousarray(np.asarray(inputs["L0_ffn_w_up"], np.float32))
    wdn = np.ascontiguousarray(np.asarray(inputs["L0_ffn_w_down"], np.float32))
    wgT, wuT, wdT = T(wgn), T(wun), T(wdn)
    HF = I_FFN // 2
    wq_half = [{
        "wg_nat": np.ascontiguousarray(wgn[h * HF:(h + 1) * HF]),
        "wu_nat": np.ascontiguousarray(wun[h * HF:(h + 1) * HF]),
        "wd_nat": np.ascontiguousarray(wdn[:, h * HF:(h + 1) * HF]),
        "wg_T": np.ascontiguousarray(wgT[:, h * HF:(h + 1) * HF]),
        "wu_T": np.ascontiguousarray(wuT[:, h * HF:(h + 1) * HF]),
        "wd_T": np.ascontiguousarray(wdT[h * HF:(h + 1) * HF]),
    } for h in range(2)]

    inv_freq = (1.0 / 10000.0 ** (np.arange(0, 32, 2, dtype=np.float32) / np.float32(32.0))).astype(np.float32)
    in_maps = []
    for c in range(8):
        b, hf = c // 2, c % 2
        m = dict(common)
        m.update(wq_half[hf])
        m["x_in"] = np.ascontiguousarray(x[b, hf * TLOC:(hf + 1) * TLOC, :])
        t = (np.arange(TLOC, dtype=np.float32) + np.float32(hf * TLOC)) / np.float32(40.0)
        fr = np.einsum("i,j->ij", t, inv_freq).astype(np.float32)
        rot = np.concatenate([fr, fr], -1)
        cosf = np.cos(rot).astype(np.float32).reshape(NTT, P, 32).transpose(1, 0, 2)
        sinf = np.sin(rot).astype(np.float32).reshape(NTT, P, 32).transpose(1, 0, 2)
        m["rope_cos"] = np.ascontiguousarray(cosf.reshape(P, NTT * 32))
        m["rope_sin"] = np.ascontiguousarray(sinf.reshape(P, NTT * 32))
        m["mask_first"] = np.full((P, 1), 1.0 - hf, np.float32)
        m["mask_second"] = np.full((P, 1), float(hf), np.float32)
        in_maps.append(m)
    return in_maps


def _run(inputs, n_sub=N_SUB):
    in_maps = _host_inputs(inputs)
    nc = B(n_sub=n_sub).build()
    res = run_bass_kernel_spmd(nc, in_maps, core_ids=list(range(8)))
    out = np.empty((4, 2048, 2048), np.float32)
    for c in range(8):
        b, hf = c // 2, c % 2
        out[b, hf * TLOC:(hf + 1) * TLOC, :] = res.results[c]["out"]
    return out


def kernel(**inputs):
    return _run(inputs, n_sub=N_SUB)




# version 11 (final)
# speedup vs baseline: 1.0017x; 1.0017x over previous
import numpy as np
from contextlib import ExitStack

import concourse.bass as bass
import concourse.mybir as mybir
from concourse.bass_utils import run_bass_kernel_spmd

F32 = mybir.dt.float32
BF16 = mybir.dt.bfloat16
I32 = mybir.dt.int32
AX = mybir.AxisListType
OP = mybir.AluOpType
ACT = mybir.ActivationFunctionType

P = 128
TLOC = 1024
NTT = TLOC // P
D = 2048
NFT = D // P
I_FFN = 8192
LN2 = float(np.log(2.0))
INV_LN2 = float(1.0 / np.log(2.0))
ISQRT_DH = float(np.float32(1.0) / np.sqrt(np.float32(128.0)))
FLOOR_OFF = -0.49993896484375

N_SUB = 6


class B:

    def __init__(self, n_sub=N_SUB, dbg=()):
        self.n_sub = n_sub
        self.dbg = set(dbg)
        self.nc = bass.Bass("TRN2", target_bir_lowering=False, debug=False,
                            detect_race_conditions=False)
        self.cnt = {}

    def _declare_io(self):
        nc = self.nc
        d = {}

        def din(name, shape):
            d[name] = nc.dram_tensor(name, list(shape), F32, kind="ExternalInput").ap()

        din("x_in", (TLOC, D))
        for n, s in [("W_dkv", (D, 512)), ("W_dq", (D, 512)), ("W_uk", (512, 1024)),
                     ("W_uv", (512, 2048)), ("W_uq", (512, 1024)), ("W_kr", (D, 1024)),
                     ("W_qr", (512, 1024)), ("W_out", (D, D)),
                     ("wg_nat", (I_FFN // 2, D)), ("wu_nat", (I_FFN // 2, D)),
                     ("wd_nat", (D, I_FFN // 2)),
                     ("wg_T", (D, I_FFN // 2)), ("wu_T", (D, I_FFN // 2)),
                     ("wd_T", (I_FFN // 2, D)),
                     ("w1", (D, 160)), ("w2", (5, 32, D)), ("td_w1", (D, 64)),
                     ("td_w2", (64, D)), ("wr", (D, D)), ("wk", (D, D)), ("wv", (D, D)),
                     ("wg2", (D, D)), ("wo", (D, D)), ("u_bd", (D, 32)),
                     ("tm_x", (P, NFT)), ("tm_w", (P, NFT)), ("tm_k", (P, NFT)),
                     ("tm_v", (P, NFT)), ("tm_r", (P, NFT)), ("tm_g", (P, NFT)),
                     ("tdecay", (P, NFT)), ("lnw_rep", (P, D)), ("lnb_rep", (P, D)),
                     ("l1f_nat", (D, D)), ("l1f_T", (D, D)), ("l1f_bias", (1, D)),
                     ("ident", (P, P)), ("mask_su", (64, 64)), ("blk4", (4, P)),
                     ("ones1", (1, P)), ("rope_cos", (P, NTT * 32)), ("rope_sin", (P, NTT * 32)),
                     ("w2pad", (5, 160, D)), ("masksub", (P, 1024)),
                     ("mask_first", (P, 1)), ("mask_second", (P, 1))]:
            din(n, s)
        d["out"] = nc.dram_tensor("out", [TLOC, D], F32, kind="ExternalOutput").ap()
        def kind(n):
            return "ExternalOutput" if n in self.dbg else "Internal"
        d["qwg"] = nc.dram_tensor("qwg", [32, P, 4096], BF16, kind=kind("qwg")).ap()
        d["qwu"] = nc.dram_tensor("qwu", [32, P, 4096], BF16, kind=kind("qwu")).ap()
        d["qwd"] = nc.dram_tensor("qwd", [16, P, 8192], BF16, kind=kind("qwd")).ap()
        d["ql1"] = nc.dram_tensor("ql1", [NFT, P, D], BF16, kind=kind("ql1")).ap()
        d["qwgh"] = nc.dram_tensor("qwgh", [16, P, 4096], BF16, kind=kind("qwgh")).ap()
        d["qwuh"] = nc.dram_tensor("qwuh", [16, P, 4096], BF16, kind=kind("qwuh")).ap()
        d["qwdh"] = nc.dram_tensor("qwdh", [8, P, 8192], BF16, kind=kind("qwdh")).ap()
        d["thr_b"] = nc.dram_tensor("thr_b", [NFT, P], F32).ap()
        d["xspill"] = nc.dram_tensor("xspill", [TLOC, D], F32).ap()
        d["yspill"] = nc.dram_tensor("yspill", [TLOC, D], F32).ap()
        d["ccm_i"] = nc.dram_tensor("ccm_i", [P, 16 * P], F32).ap()
        d["ccm_o"] = nc.dram_tensor("ccm_o", [P, 16 * P], F32).ap()
        d["ccs_i"] = nc.dram_tensor("ccs_i", [P, 1024], F32).ap()
        d["ccs_o"] = nc.dram_tensor("ccs_o", [P, 1024], F32).ap()
        d["vsp"] = nc.dram_tensor("vsp", [NTT, P, D], BF16).ap()
        d["xgsp"] = nc.dram_tensor("xgsp", [NFT, P, TLOC], BF16).ap()
        d["ysp"] = nc.dram_tensor("ysp", [16, 64, D], BF16).ap()
        d["ynsp"] = nc.dram_tensor("ynsp", [16, 64, D], BF16).ap()
        d["cch_i"] = nc.dram_tensor("cch_i", [P, NFT], F32).ap()
        d["cch_o"] = nc.dram_tensor("cch_o", [P, NFT], F32).ap()
        self.d = d

    def _alloc(self, stack):
        nc = self.nc
        e = stack.enter_context
        self.xres = e(nc.sbuf_tensor("xres", [P, NTT * D], F32))
        self.PA = e(nc.sbuf_tensor("PA", [P, 16384], F32))
        self.PB = e(nc.sbuf_tensor("PB", [P, 8192], F32))
        self.PC = e(nc.sbuf_tensor("PC", [P, 8192], F32))
        self.WS = e(nc.sbuf_tensor("WS", [P, 2048], F32))
        self.SM = e(nc.sbuf_tensor("SM", [P, 1024], F32))
        self.CN = e(nc.sbuf_tensor("CN", [P, 512], F32))
        self.ps = [e(nc.psum_tensor("ps%d" % i, [P, 512], F32)) for i in range(4)]
        self.psA = e(nc.psum_tensor("psA", [P, 1024], F32))
        self.psS = e(nc.psum_tensor("psS", [P, 1024], F32))
        self.sems = {}
        for name in ["ldh", "ldg", "st", "sv", "ss", "stt", "sg", "cc", "ha", "sa", "ga"]:
            self.sems[name] = e(nc.semaphore(name))
            self.cnt[name] = 0
        self.block = e(nc.Block())

    def inc(self, instr, name, n=None):
        if n is None:
            n = 16 if name in ("ldh", "ldg", "st", "ha", "sa", "ga") else 1
        instr.then_inc(self.sems[name], n)
        self.cnt[name] += n
        return self.cnt[name]

    def w(self, eng, name, val=None):
        eng.wait_ge(self.sems[name], self.cnt[name] if val is None else val)

    def fence(self, eng, ek, instr):
        nm = self._SEM_OF[ek]
        ev = self.inc(instr, nm)
        eng.wait_ge(self.sems[nm], ev)
        return instr

    def barrier(self):
        block = self.block
        tgt = dict(self.cnt)
        deco = {"y": block.sync, "g": block.gpsimd, "v": block.vector,
                "s": block.scalar, "t": block.tensor}
        for ek in ["y", "g", "v", "s", "t"]:
            def body(eng, ek=ek):
                for nm, v in tgt.items():
                    if v:
                        eng.wait_ge(self.sems[nm], v)
            deco[ek](body)

    def build(self):
        self._declare_io()
        with ExitStack() as stack:
            self._alloc(stack)
            self.emit_prelude()
            if self.n_sub >= 1:
                self.emit_wquant_mx()
            if self.n_sub >= 4:
                self.emit_ternary()
            if self.n_sub >= 1:
                self.emit_l0ffn(1)
            if self.n_sub >= 2:
                self.emit_l0attn()
            if self.n_sub >= 3:
                self.emit_l0ffn(3)
            if self.n_sub >= 4:
                self.emit_l1ffn(4)
            if self.n_sub >= 5:
                self.emit_l1attn()
            if self.n_sub >= 6:
                self.emit_l1ffn(6)
            self.emit_store()
        return self.nc

    def emit_prelude(self):
        d, block = self.d, self.block
        CN = self.CN
        self.identf = CN[:, 0:128]
        self.identb = CN[:, 128:192].bitcast(BF16)
        self.masksu = CN[0:64, 192:256]
        self.blk4f = CN[0:4, 256:384]
        self.ones1t = CN[0:1, 384:448].bitcast(BF16)
        self.mfirst = CN[:, 448:449]
        self.msecond = CN[:, 449:450]
        self._ccol_next = [452]
        self._ccols = {}
        sems, cnt = self.sems, self.cnt

        @block.sync
        def _(sync):
            self.inc(sync.dma_start(
                out=self.xres[:, :].rearrange("p (tt dd) -> p tt dd", dd=D),
                in_=d["x_in"].rearrange("(tt p) dd -> p tt dd", p=P)), "ldh")
            self.inc(sync.dma_start(out=self.masksu, in_=d["mask_su"]), "ldh")
            self.inc(sync.dma_start(out=self.mfirst, in_=d["mask_first"]), "ldh")
            self.inc(sync.dma_start(out=self.msecond, in_=d["mask_second"]), "ldh")
            self.inc(sync.dma_start(out=self.identf, in_=d["ident"]), "ldh")
            self.inc(sync.dma_start(out=self.blk4f, in_=d["blk4"]), "ldh")

        @block.gpsimd
        def _(g):
            self.inc(g.dma_start(out=self.identb, in_=d["ident"]), "ldg")
            self.inc(g.dma_start(out=self.ones1t, in_=d["ones1"]), "ldg")

        consts = [1e-5, 6.4e-4, 0.5, 0.25, -1.0, 1.0,
                  LN2, -LN2, 3.0 * LN2, 5.0 * LN2, -2.0 * LN2,
                  INV_LN2, FLOOR_OFF, 1.0 / 2048.0, 0.7 / 2048.0, ISQRT_DH,
                  1.0 / 64.0, 2.0 ** -30]

        @block.vector
        def _(v):
            last = None
            for c in consts:
                col = CN[:, self._ccol_next[0]:self._ccol_next[0] + 1]
                self._ccols[float(np.float32(c))] = col
                self._ccol_next[0] += 1
                last = v.memset(col, float(np.float32(c)))
            self.inc(last, "sv")

        self.epsln = self.cc(1e-5)
        self.epsgn = self.cc(6.4e-4)
        self.base_ready = dict(self.cnt)
        self.barrier()

    def cc(self, val):
        return self._ccols[float(np.float32(val))]

    def emit_store(self):
        d, block = self.d, self.block

        @block.sync
        def _(sync):
            self.w(sync, "sv"); self.w(sync, "ss"); self.w(sync, "stt")
            self.w(sync, "ldh"); self.w(sync, "ldg"); self.w(sync, "sg")
            self.inc(sync.dma_start(
                out=d["out"].rearrange("(tt p) dd -> p tt dd", p=P),
                in_=self.xres[:, :].rearrange("p (tt dd) -> p tt dd", dd=D)), "st")
            sync.wait_ge(self.sems["st"], self.cnt["st"])

    _SEM_OF = {"v": "sv", "s": "ss", "t": "stt", "y": "ldh", "g": "ldg"}

    def seq(self, *steps):
        block = self.block
        deco = {"v": block.vector, "s": block.scalar, "t": block.tensor,
                "y": block.sync, "g": block.gpsimd}
        prev = getattr(self, "_last_seq", None)
        for ek, fn in steps:
            semname = self._SEM_OF[ek]
            pv = prev

            def body(eng, fn=fn, pv=pv, ek=ek, semname=semname):
                if pv is not None and pv[0] != ek:
                    eng.wait_ge(self.sems[self._SEM_OF[pv[0]]], pv[1])
                last = fn(eng)
                if last is not None:
                    self.inc(last, semname)
            deco[ek](body)
            prev = (ek, self.cnt[semname])
        self._last_seq = prev
        return prev

    def _scalepath_steps(self, sc, Ei, Arow, Srow, cA_ln):
        def sv(v):
            return v.tensor_single_scalar(out=sc, in_=sc, scalar=2.0 ** -30, op=OP.max)

        def s1(s):
            self.fence(s, "s", s.activation(out=sc, in_=sc, func=ACT.Ln))
            return s.activation(out=Ei, in_=sc, func=ACT.Identity,
                                bias=self.cc(FLOOR_OFF), scale=self.cc(INV_LN2))

        def v2(v):
            return v.tensor_copy(out=sc, in_=Ei)

        def s2(s):
            s.activation(out=Arow, in_=sc, func=ACT.Exp,
                         bias=self.cc(cA_ln), scale=self.cc(-LN2))
            return s.activation(out=Srow, in_=sc, func=ACT.Exp,
                                bias=self.cc(-2.0 * LN2), scale=self.cc(LN2))
        return [("v", sv), ("s", s1), ("v", v2), ("s", s2)]

    def _mxchain_steps(self, src, dst_bf, Abc, Sbc, sgb, au, r0, r1, r2, u1, u2, u3, d0):
        cap = 2.0 if d0 == 0.5 else 3.5

        def s_abs(s):
            s.activation(out=au, in_=src, func=ACT.Abs)
            return s.activation(out=sgb, in_=src, func=ACT.Sign)

        def v_zz(v):
            return v.tensor_tensor(out=au, in0=au, in1=Abc, op=OP.mult)

        def s_cast(s):
            s.copy(out=r0, in_=au)
            s.mul(out=r1, in_=au, mul=self.cc(0.5))
            return s.mul(out=r2, in_=au, mul=self.cc(0.25))

        def v_q(v):
            v.tensor_scalar(out=u1, in0=r0, scalar1=d0, scalar2=2.0, op0=OP.mult, op1=OP.min)
            v.tensor_scalar(out=u2, in0=r1, scalar1=2 * d0, scalar2=-2.0, op0=OP.mult, op1=OP.add)
            v.tensor_scalar(out=u2, in0=u2, scalar1=0.0, scalar2=2.0, op0=OP.max, op1=OP.min)
            v.tensor_scalar(out=u3, in0=r2, scalar1=4 * d0, scalar2=-4.0, op0=OP.mult, op1=OP.add)
            v.tensor_scalar(out=u3, in0=u3, scalar1=0.0, scalar2=cap, op0=OP.max, op1=OP.min)
            v.tensor_tensor(out=u1, in0=u1, in1=u2, op=OP.add)
            v.tensor_tensor(out=u1, in0=u1, in1=u3, op=OP.add)
            v.tensor_tensor(out=u1, in0=u1, in1=sgb, op=OP.mult)
            return v.tensor_tensor(out=dst_bf, in0=u1, in1=Sbc, op=OP.mult)
        return [("s", s_abs), ("v", v_zz), ("s", s_cast), ("v", v_q)]

    def emit_wquant_mx(self):
        d = self.d
        nc = self.nc
        k = "ExternalOutput" if "A_sc" in self.dbg else "Internal"
        d["A_sc"] = nc.dram_tensor("A_sc", [256, 8192], F32, kind=k).ap()
        d["S_sc"] = nc.dram_tensor("S_sc", [256, 8192], F32, kind=k).ap()
        for nat, wT, qout, gout, Om, Hm, om_loc, hm_loc in [
                ("wg_nat", "wg_T", "qwgh", "qwg", I_FFN, D, I_FFN // 2, D),
                ("wu_nat", "wu_T", "qwuh", "qwu", I_FFN, D, I_FFN // 2, D),
                ("wd_nat", "wd_T", "qwdh", "qwd", D, I_FFN, D, I_FFN // 2)]:
            self._wquant_one(d[nat], d[wT], d[qout], Om, Hm, om_loc, hm_loc)
            self.barrier()
            if getattr(self, "skip_gather", False):
                continue

            @self.block.gpsimd
            def _(g, src=qout, dst=gout):
                self.inc(g.collective_compute(
                    "AllGather", OP.bypass,
                    replica_groups=[[0, 1], [2, 3], [4, 5], [6, 7]],
                    ins=[d[src].opt()], outs=[d[dst].opt()]), "cc")
        self.barrier()

    def _wquant_one(self, nat, wT, qout, Om, Hm, om_loc, hm_loc):
        d, block = self.d, self.block
        PA, PB, PC, WS, SM = self.PA, self.PB, self.PC, self.WS, self.SM
        sems, cnt = self.sems, self.cnt
        nhb = hm_loc // 32
        assert nhb <= 128

        blk4b = WS[0:4, 512:576].bitcast(BF16)

        @block.gpsimd
        def _(g):
            self.inc(g.dma_start(out=blk4b, in_=d["blk4"]), "ldg")

        ld_ev, amax_ev, s2_ev, vev_ev, st_ev, ttr_ev = {}, {}, {}, {}, {}, {}
        for ot in range(om_loc // P):
            sl = ot % 2
            lsem = ["ldh", "ha"][sl]
            ssem = ["st", "sa"][sl]
            nt = PB[:, sl * 4096:sl * 4096 + hm_loc]
            sc = SM[:, sl * 128:sl * 128 + nhb]
            Ei = SM[:, 256 + sl * 128:256 + sl * 128 + nhb].bitcast(I32)[:, 0:nhb]
            Arow = SM[:, 512 + sl * 128:512 + sl * 128 + nhb]
            Srow = SM[:, 768 + sl * 128:768 + sl * 128 + nhb]

            @block.sync
            def _(sync, ot=ot, nt=nt):
                if ot >= 2:
                    sync.wait_ge(sems["sv"], amax_ev[ot - 2])
                self.inc(sync.dma_start(out=nt, in_=nat[ot * P:(ot + 1) * P, :]), lsem)
            ld_ev[ot] = cnt[lsem]

            self._last_seq = None

            def v_amax(v, ot=ot, nt=nt, sc=sc, lsem=lsem):
                v.wait_ge(sems[lsem], ld_ev[ot])
                if ot >= 2:
                    v.wait_ge(sems["ss"], s2_ev[ot - 2])
                    v.wait_ge(sems["stt"], ttr_ev[ot - 2])
                return v.tensor_reduce(out=sc, in_=nt.rearrange("p (b k) -> p b k", k=32),
                                       axis=AX.X, op=OP.max, apply_absolute_value=True)
            stp = [("v", v_amax)] + self._scalepath_steps(sc, Ei, Arow, Srow, 3.0 * LN2)
            self.seq(*stp)
            amax_ev[ot] = cnt["sv"]
            s2_ev[ot] = cnt["ss"]
            for qi, (row, hbm) in enumerate([(Arow, d["A_sc"]), (Srow, d["S_sc"])]):
                pst = [self.psA, self.psS][qi][:, sl * 512:sl * 512 + 128]
                tb = WS[0:nhb, (sl * 2 + qi) * 128:(sl * 2 + qi) * 128 + 128]

                def t_tr(t, row=row, pst=pst, ot=ot):
                    if ot >= 2:
                        t.wait_ge(sems["sv"], vev_ev[ot - 2])
                    return t.transpose(pst[0:nhb, 0:P], row, self.identf)

                def v_ev(v, pst=pst, tb=tb, ot=ot, ssem=ssem):
                    if ot >= 2:
                        v.wait_ge(sems[ssem], st_ev[ot - 2])
                    return v.tensor_copy(out=tb, in_=pst[0:nhb, 0:P])

                def y_st(sync, hbm=hbm, ot=ot, tb=tb, ssem=ssem):
                    self.inc(sync.dma_start(out=hbm[0:nhb, ot * P:(ot + 1) * P],
                                            in_=tb), ssem)
                    return None
                self.seq(("t", t_tr), ("v", v_ev), ("y", y_st))
            ttr_ev[ot] = cnt["stt"]
            vev_ev[ot] = cnt["sv"]
            st_ev[ot] = cnt[ssem]

        self.barrier()
        psA, psS, ps = self.psA, self.psS, self.ps
        n_osl = om_loc // 2048
        d0 = 0.5
        cap = 2.0
        wld_ev, sld_ev, bc_ev, abs_ev, zz_ev, cast_ev, vq_ev, stq_ev = {}, {}, {}, {}, {}, {}, {}, {}
        for it in range((hm_loc // P) * n_osl):
            sl = it % 2
            lsem = ["ldh", "ha"][sl]
            ssem = ["st", "sa"][sl]
            gsem = ["ldg", "ga"][sl]
            ht, osl = it // n_osl, it % n_osl
            wt = PB[:, sl * 4096:sl * 4096 + 2048]
            qb = PB[:, sl * 4096 + 2048:sl * 4096 + 3072].bitcast(BF16)
            au = PA[:, sl * 8192:sl * 8192 + 2048]
            r0 = PA[:, sl * 8192 + 2048:sl * 8192 + 4096].bitcast(I32)
            r1 = PA[:, sl * 8192 + 4096:sl * 8192 + 6144].bitcast(I32)
            r2 = PA[:, sl * 8192 + 6144:sl * 8192 + 8192].bitcast(I32)
            u2 = PA[:, sl * 8192 + 2048:sl * 8192 + 3072].bitcast(BF16)
            u3 = PA[:, sl * 8192 + 4096:sl * 8192 + 5120].bitcast(BF16)
            sgb = PC[:, sl * 2048:sl * 2048 + 1024].bitcast(BF16)
            u1 = PC[:, sl * 2048 + 1024:sl * 2048 + 2048].bitcast(BF16)
            A4 = PC[0:4, 4096 + sl * 1024:4096 + sl * 1024 + 1024].bitcast(BF16)
            S4 = PC[0:4, 6144 + sl * 1024:6144 + sl * 1024 + 1024].bitcast(BF16)

            @block.sync
            def _(sync, it=it, ht=ht, osl=osl, wt=wt, lsem=lsem):
                if it >= 2:
                    sync.wait_ge(sems["ss"], abs_ev[it - 2])
                self.inc(sync.dma_start(out=wt, in_=wT[ht * P:(ht + 1) * P,
                                                       osl * 2048:(osl + 1) * 2048]), lsem)
            wld_ev[it] = cnt[lsem]

            @block.gpsimd
            def _(g, it=it, ht=ht, osl=osl, A4=A4, S4=S4, gsem=gsem):
                if it >= 2:
                    g.wait_ge(sems["stt"], bc_ev[it - 2])
                self.inc(g.dma_start(
                    out=A4, in_=d["A_sc"][4 * ht:4 * ht + 4,
                                          osl * 2048:(osl + 1) * 2048]), gsem)
                self.inc(g.dma_start(
                    out=S4, in_=d["S_sc"][4 * ht:4 * ht + 4,
                                          osl * 2048:(osl + 1) * 2048]), gsem)
            sld_ev[it] = cnt[gsem]

            @block.tensor
            def _(t, it=it, A4=A4, S4=S4, gsem=gsem):
                t.wait_ge(sems[gsem], sld_ev[it])
                if it >= 1:
                    t.wait_ge(sems["sv"], vq_ev[it - 1])
                t.matmul(psA[:, 0:512], blk4b, A4[:, 0:512], start=True, stop=True)
                t.matmul(psA[:, 512:1024], blk4b, A4[:, 512:1024], start=True, stop=True)
                t.matmul(ps[0][:, 0:512], blk4b, A4[:, 1024:1536], start=True, stop=True)
                t.matmul(ps[1][:, 0:512], blk4b, A4[:, 1536:2048], start=True, stop=True)
                t.matmul(psS[:, 0:512], blk4b, S4[:, 0:512], start=True, stop=True)
                t.matmul(psS[:, 512:1024], blk4b, S4[:, 512:1024], start=True, stop=True)
                t.matmul(ps[2][:, 0:512], blk4b, S4[:, 1024:1536], start=True, stop=True)
                mm = t.matmul(ps[3][:, 0:512], blk4b, S4[:, 1536:2048], start=True, stop=True)
                self.inc(mm, "stt")
            bc_ev[it] = cnt["stt"]

            @block.scalar
            def _(s, it=it, wt=wt, au=au, sgb=sgb, lsem=lsem):
                s.wait_ge(sems[lsem], wld_ev[it])
                if it >= 2:
                    s.wait_ge(sems["sv"], vq_ev[it - 2])
                s.activation(out=au, in_=wt, func=ACT.Abs)
                last = s.activation(out=sgb, in_=wt, func=ACT.Sign)
                self.inc(last, "ss")
            abs_ev[it] = cnt["ss"]

            @block.vector
            def _(v, it=it, au=au):
                v.wait_ge(sems["stt"], bc_ev[it])
                v.wait_ge(sems["ss"], abs_ev[it])
                v.tensor_tensor(out=au[:, 0:1024], in0=au[:, 0:1024],
                                in1=psA[:, 0:1024], op=OP.mult)
                v.tensor_tensor(out=au[:, 1024:1536], in0=au[:, 1024:1536],
                                in1=ps[0][:, 0:512], op=OP.mult)
                last = v.tensor_tensor(out=au[:, 1536:2048], in0=au[:, 1536:2048],
                                       in1=ps[1][:, 0:512], op=OP.mult)
                self.inc(last, "sv")
            zz_ev[it] = cnt["sv"]

            @block.vector
            def _(v, it=it, qb=qb, au=au, r0=r0, r1=r1, r2=r2, u2=u2, u3=u3,
                  sgb=sgb, u1=u1, ssem=ssem):
                if it >= 2:
                    v.wait_ge(sems[ssem], stq_ev[it - 2])
                v.tensor_copy(out=r0, in_=au)
                v.tensor_scalar(out=r1, in0=au, scalar1=0.5, scalar2=None,
                                op0=OP.mult, op1=OP.bypass)
                v.tensor_scalar(out=r2, in0=au, scalar1=0.25, scalar2=None,
                                op0=OP.mult, op1=OP.bypass)
                v.tensor_scalar(out=u1, in0=r0, scalar1=d0, scalar2=2.0,
                                op0=OP.mult, op1=OP.min)
                v.tensor_scalar(out=u2, in0=r1, scalar1=2 * d0, scalar2=-2.0,
                                op0=OP.mult, op1=OP.add)
                v.tensor_scalar(out=u2, in0=u2, scalar1=0.0, scalar2=2.0,
                                op0=OP.max, op1=OP.min)
                v.tensor_scalar(out=u3, in0=r2, scalar1=4 * d0, scalar2=-4.0,
                                op0=OP.mult, op1=OP.add)
                v.tensor_scalar(out=u3, in0=u3, scalar1=0.0, scalar2=cap,
                                op0=OP.max, op1=OP.min)
                v.tensor_tensor(out=u1, in0=u1, in1=u2, op=OP.add)
                v.tensor_tensor(out=u1, in0=u1, in1=u3, op=OP.add)
                v.tensor_tensor(out=u1, in0=u1, in1=sgb, op=OP.mult)
                v.tensor_tensor(out=qb[:, 0:1024], in0=u1[:, 0:1024],
                                in1=psS[:, 0:1024], op=OP.mult)
                v.tensor_tensor(out=qb[:, 1024:1536], in0=u1[:, 1024:1536],
                                in1=ps[2][:, 0:512], op=OP.mult)
                last = v.tensor_tensor(out=qb[:, 1536:2048], in0=u1[:, 1536:2048],
                                       in1=ps[3][:, 0:512], op=OP.mult)
                self.inc(last, "sv")
            vq_ev[it] = cnt["sv"]

            @block.sync
            def _(sync, it=it, ht=ht, osl=osl, qb=qb, ssem=ssem):
                sync.wait_ge(sems["sv"], vq_ev[it])
                last = None
                if Om == I_FFN:
                    for j in range(8):
                        sid = osl * 8 + j
                        last = sync.dma_start(
                            out=qout[sid, :, ht * 256:(ht + 1) * 256],
                            in_=qb[:, j * 256:(j + 1) * 256])
                        self.inc(last, ssem)
                else:
                    for o2 in range(4):
                        sid = (ht // 16) * 4 + o2
                        last = sync.dma_start(
                            out=qout[sid, :, (ht % 16) * 512:(ht % 16) * 512 + 512],
                            in_=qb[:, o2 * 512:(o2 + 1) * 512])
                        self.inc(last, ssem)
            stq_ev[it] = cnt[ssem]
        self._last_seq = None

    def _ln_steps(self, src, dst, scrap, smb):
        s1, s2 = smb[:, 0:1], smb[:, 1:2]
        mu, ex2 = smb[:, 2:3], smb[:, 3:4]
        var, rstd, bneg = smb[:, 4:5], smb[:, 5:6], smb[:, 6:7]

        def v_sum(v):
            return v.tensor_reduce(out=s1, in_=src, axis=AX.X, op=OP.add)

        def s_sq(s):
            return s.activation(out=scrap, in_=src, func=ACT.Square, accum_out=s2)

        def v_stat(v):
            self.fence(v, "v", v.tensor_single_scalar(out=mu, in_=s1, scalar=1.0 / 2048.0, op=OP.mult))
            self.fence(v, "v", v.tensor_single_scalar(out=ex2, in_=s2, scalar=1.0 / 2048.0, op=OP.mult))
            self.fence(v, "v", v.tensor_tensor(out=var, in0=mu, in1=mu, op=OP.mult))
            return v.tensor_tensor(out=var, in0=ex2, in1=var, op=OP.subtract)

        def s_rstd(s):
            return s.activation(out=rstd, in_=var, func=ACT.Sqrt, bias=self.epsln)

        def v_bneg(v):
            self.fence(v, "v", v.reciprocal(out=rstd, in_=rstd))
            return v.scalar_tensor_tensor(out=bneg, in0=mu, scalar=-1.0, in1=rstd,
                                          op0=OP.mult, op1=OP.mult)

        def s_norm(s):
            return s.activation(out=dst, in_=src, func=ACT.Identity, bias=bneg, scale=rstd)
        return [("v", v_sum), ("s", s_sq), ("v", v_stat), ("s", s_rstd),
                ("v", v_bneg), ("s", s_norm)]

    def _actquant(self, src, dst_bf, T, smb, W=512, stt_ev=None):
        WTOT = src.shape[1]
        nb = WTOT // 32
        am = smb[:, 0:nb]
        Ei = smb[:, 64:64 + nb].bitcast(I32)[:, 0:nb]
        Arow = smb[:, 128:128 + nb]
        Srow = smb[:, 192:192 + nb]

        def v_amax(v):
            if stt_ev is None:
                self.w(v, "stt")
            else:
                v.wait_ge(self.sems["stt"], stt_ev)
            self.fence(v, "v", v.tensor_reduce(
                out=am, in_=src.rearrange("p (b k) -> p b k", k=32),
                axis=AX.X, op=OP.max, apply_absolute_value=True))
            return None
        stp = [("v", v_amax)] + self._scalepath_steps(am, Ei, Arow, Srow, 5.0 * LN2)
        self.seq(*stp)
        au = T[:, 0:512]
        r0 = T[:, 512:1024].bitcast(I32)
        r1 = T[:, 1024:1536].bitcast(I32)
        r2 = T[:, 1536:2048].bitcast(I32)
        sgb = T[:, 2048:2304].bitcast(BF16)
        u1 = T[:, 2304:2560].bitcast(BF16)
        u2 = T[:, 2560:2816].bitcast(BF16)
        u3 = T[:, 2816:3072].bitcast(BF16)
        for c0 in range(0, WTOT, W):
            blk = slice(c0 // 32, c0 // 32 + W // 32)
            Abc = Arow[:, blk].unsqueeze(2).broadcast_to([P, W // 32, 32])
            Sbc = Srow[:, blk].unsqueeze(2).broadcast_to([P, W // 32, 32])
            sh = src[:, c0:c0 + W].rearrange("p (b k) -> p b k", k=32)
            dh = dst_bf[:, c0:c0 + W].rearrange("p (b k) -> p b k", k=32)
            auv = au.rearrange("p (b k) -> p b k", k=32)
            sgv = sgb.rearrange("p (b k) -> p b k", k=32)
            u1v = u1.rearrange("p (b k) -> p b k", k=32)
            steps = self._mxchain3(sh, dh, Abc, Sbc, sgv, auv, au, r0, r1, r2,
                                   u1v, u2, u3, 0.125)
            self.seq(*steps)

    def _mxchain3(self, src, dst_bf, Abc, Sbc, sgb, au, au2, r0, r1, r2, u1, u2, u3, d0):
        cap = 2.0 if d0 == 0.5 else 3.5

        def s_abs(s):
            s.activation(out=au, in_=src, func=ACT.Abs)
            return s.activation(out=sgb, in_=src, func=ACT.Sign)

        def v_all(v):
            v.tensor_tensor(out=au, in0=au, in1=Abc, op=OP.mult)
            v.tensor_copy(out=r0, in_=au2)
            v.tensor_scalar(out=r1, in0=au2, scalar1=0.5, scalar2=None,
                            op0=OP.mult, op1=OP.bypass)
            v.tensor_scalar(out=r2, in0=au2, scalar1=0.25, scalar2=None,
                            op0=OP.mult, op1=OP.bypass)
            v.tensor_scalar(out=u1, in0=r0, scalar1=d0, scalar2=2.0, op0=OP.mult, op1=OP.min)
            v.tensor_scalar(out=u2, in0=r1, scalar1=2 * d0, scalar2=-2.0, op0=OP.mult, op1=OP.add)
            v.tensor_scalar(out=u2, in0=u2, scalar1=0.0, scalar2=2.0, op0=OP.max, op1=OP.min)
            v.tensor_scalar(out=u3, in0=r2, scalar1=4 * d0, scalar2=-4.0, op0=OP.mult, op1=OP.add)
            v.tensor_scalar(out=u3, in0=u3, scalar1=0.0, scalar2=cap, op0=OP.max, op1=OP.min)
            v.tensor_tensor(out=u1, in0=u1, in1=u2.rearrange("p (b k) -> p b k", k=32), op=OP.add)
            v.tensor_tensor(out=u1, in0=u1, in1=u3.rearrange("p (b k) -> p b k", k=32), op=OP.add)
            v.tensor_tensor(out=u1, in0=u1, in1=sgb, op=OP.mult)
            return v.tensor_tensor(out=dst_bf, in0=u1, in1=Sbc, op=OP.mult)
        return [("s", s_abs), ("v", v_all)]

    def emit_l0ffn(self, which):
        d, block = self.d, self.block
        PA, PB, PC, SM = self.PA, self.PB, self.PC, self.SM
        xres = self.xres
        qxT = PC.bitcast(BF16)
        qiT = PB.bitcast(BF16)
        xln = PA[:, 8192:10240]
        interb = PA[:, 10240:12288]
        T = PA[:, 12288:16384]
        qbuf = T[:, 3072:4096].bitcast(BF16)
        smb = SM[:, 0:16]

        for tt in range(NTT):
            xt = xres[:, tt * D:(tt + 1) * D]
            self.seq(*self._ln_steps(xt, xln, T[:, 0:2048], smb))
            self._actquant(xln, qbuf, T, SM[:, 64:320])
            for ft in range(NFT):
                pst = self.psA[:, (ft % 2) * 512:(ft % 2) * 512 + 64].bitcast(BF16)

                def t_tr(t, ft=ft, pst=pst):
                    self.w(t, "sv")
                    return t.transpose(pst, qbuf[:, ft * P:(ft + 1) * P], self.identb)

                def v_ev(v, ft=ft, tt=tt, pst=pst):
                    c = (ft * NTT + tt) * P
                    return v.tensor_copy(out=qxT[:, c:c + P], in_=pst)
                self.seq(("t", t_tr), ("v", v_ev))

        gsl = [PA[:, 0:2048].bitcast(BF16), PA[:, 2048:4096].bitcast(BF16)]
        usl = [PA[:, 4096:6144].bitcast(BF16), PA[:, 6144:8192].bitcast(BF16)]
        interbs = [PA[:, 10240:12288], PA[:, 8192:10240], self.WS[:, 0:2048]]
        sgb = SM[:, 512:768]
        slot_free = [0, 0]
        bank_free = [0, 0]
        tr_ev = {-2: 0, -1: 0}
        for Imac in range(4):
            imac_base = self.cnt["stt"]

            def emit_qt(osl2, Imac=Imac):
                ib = interbs[osl2 % 3]
                self._actquant(ib, qbuf, T, SM[:, 64:320], stt_ev=tr_ev[osl2 - 2])
                itl0 = osl2 * 2
                for j in range(2):
                    for tt in range(NTT):
                        pst = self.psS[:, ((j * NTT + tt) % 2) * 512:
                                       ((j * NTT + tt) % 2) * 512 + 64].bitcast(BF16)

                        def t_tr(t, j=j, tt=tt, pst=pst):
                            return t.transpose(pst,
                                               qbuf[:, (tt * 2 + j) * P:(tt * 2 + j) * P + P],
                                               self.identb)

                        def v_ev(v, j=j, tt=tt, pst=pst, itl0=itl0):
                            c = ((itl0 + j) * NTT + tt) * P
                            return v.tensor_copy(out=qiT[:, c:c + P], in_=pst)
                        self.seq(("t", t_tr), ("v", v_ev))
                tr_ev[osl2] = self.cnt["stt"]

            for osl in range(8):
                oc = Imac * 2048 + osl * 256
                slot = osl % 2
                ib = interbs[osl % 3]

                @block.sync
                def _(sync, Imac=Imac, osl=osl, slot=slot, sf=slot_free[slot], base=imac_base):
                    sync.wait_ge(self.sems["stt"], max(sf, base))
                    self.inc(sync.dma_start(out=gsl[slot], in_=d["qwg"][Imac * 8 + osl]),
                             "ldh")
                ld_ev = self.cnt["ldh"]

                @block.gpsimd
                def _(g, Imac=Imac, osl=osl, slot=slot, sf=slot_free[slot], base=imac_base):
                    g.wait_ge(self.sems["stt"], max(sf, base))
                    self.inc(g.dma_start(out=usl[slot], in_=d["qwu"][Imac * 8 + osl]),
                             "ldg")
                gd_ev = self.cnt["ldg"]

                for tt in range(NTT):
                    bnk = tt % 2
                    bg, bu = self.ps[bnk], self.ps[2 + bnk]

                    @block.tensor
                    def _(t, tt=tt, slot=slot, bg=bg, bu=bu, bf_ev=bank_free[bnk],
                          ld_ev=ld_ev, gd_ev=gd_ev):
                        t.wait_ge(self.sems["ldh"], ld_ev)
                        t.wait_ge(self.sems["ldg"], gd_ev)
                        if bf_ev:
                            t.wait_ge(self.sems["sv"], bf_ev)
                        mm = None
                        for ft in range(NFT):
                            lt = qxT[:, (ft * NTT + tt) * P:(ft * NTT + tt) * P + P]
                            t.matmul(bg[:, 0:256], lt, gsl[slot][:, ft * 256:(ft + 1) * 256],
                                     start=(ft == 0), stop=(ft == NFT - 1))
                            mm = t.matmul(bu[:, 0:256], lt, usl[slot][:, ft * 256:(ft + 1) * 256],
                                          start=(ft == 0), stop=(ft == NFT - 1))
                        self.inc(mm, "stt")
                    mm_ev = self.cnt["stt"]
                    if tt >= NTT - 2:
                        slot_free[slot] = mm_ev

                    @block.scalar
                    def _(s, bg=bg, mm_ev=mm_ev):
                        s.wait_ge(self.sems["stt"], mm_ev)
                        self.inc(s.activation(out=sgb, in_=bg[:, 0:256], func=ACT.Silu), "ss")
                    si_ev = self.cnt["ss"]

                    @block.vector
                    def _(v, tt=tt, bu=bu, si_ev=si_ev, ib=ib):
                        v.wait_ge(self.sems["ss"], si_ev)
                        self.inc(v.tensor_tensor(out=ib[:, tt * 256:(tt + 1) * 256],
                                                 in0=sgb, in1=bu[:, 0:256], op=OP.mult), "sv")
                    bank_free[bnk] = self.cnt["sv"]

                if osl >= 2:
                    emit_qt(osl - 2)
            emit_qt(6)
            emit_qt(7)

            dsl = [PA[:, 0:4096].bitcast(BF16), PA[:, 4096:8192].bitcast(BF16)]
            dfree = [0, 0]
            down_base = self.cnt["stt"]
            for o2 in range(4):
                slot = o2 % 2

                def y_ld(sync, Imac=Imac, o2=o2, slot=slot, sf=dfree[slot], base=down_base):
                    sync.wait_ge(self.sems["stt"], max(sf, base))
                    return sync.dma_start(out=dsl[slot], in_=d["qwd"][Imac * 4 + o2])

                @block.sync
                def _(sync, f=y_ld):
                    self.inc(f(sync), "ldh")
                ld_ev = self.cnt["ldh"]

                for tt in range(NTT):
                    pd = self.psA[:, (tt % 2) * 512:(tt % 2) * 512 + 512]


                    @block.tensor
                    def _(t, tt=tt, slot=slot, pd=pd):
                        t.wait_ge(self.sems["ldh"], ld_ev)
                        self.w(t, "sv")
                        mm = None
                        for it in range(16):
                            lt = qiT[:, (it * NTT + tt) * P:(it * NTT + tt) * P + P]
                            mm = t.matmul(pd, lt, dsl[slot][:, it * 512:(it + 1) * 512],
                                          start=(it == 0), stop=(it == 15))
                        self.inc(mm, "stt")
                    mm_ev = self.cnt["stt"]
                    if tt >= NTT - 2:
                        dfree[slot] = mm_ev

                    @block.vector
                    def _(v, tt=tt, o2=o2, pd=pd):
                        v.wait_ge(self.sems["stt"], mm_ev)
                        xsl = xres[:, tt * D + o2 * 512:tt * D + o2 * 512 + 512]
                        self.inc(v.scalar_tensor_tensor(
                            out=xsl, in0=pd, scalar=0.5, in1=xsl,
                            op0=OP.mult, op1=OP.add), "sv")
        self.barrier()

    def emit_l0attn(self):
        d, block = self.d, self.block
        PA, PB, PC, WS, SM = self.PA, self.PB, self.PC, self.WS, self.SM
        xres = self.xres
        xlnT = PC.bitcast(BF16)
        kT = PC.bitcast(BF16)
        xln = PA[:, 8192:10240]
        T = PA[:, 12288:16384]
        xlnb = T[:, 3072:4096].bitcast(BF16)
        asm = T[:, 0:1024].bitcast(BF16)
        rtmp = T[:, 1024:1536]
        smb = SM[:, 0:16]
        cosb = SM[:, 320:576]
        sinb = SM[:, 576:832]
        ckvT = PA[:, 0:2048].bitcast(BF16)
        cqT = PA[:, 2048:4096].bitcast(BF16)
        WREG = PA[:, 4096:8192]
        krnat = PA[:, 8192:12288].bitcast(BF16)
        vnat = PA[:, 8192:16384].bitcast(BF16)
        qnat = PB.bitcast(BF16)
        aoT = PB.bitcast(BF16)

        def y_rope(sync):
            self.inc(sync.dma_start(out=cosb, in_=d["rope_cos"]), "ldh")
            return sync.dma_start(out=sinb, in_=d["rope_sin"])
        self.seq(("y", y_rope))

        for tt in range(NTT):
            xt = xres[:, tt * D:(tt + 1) * D]
            steps = self._ln_steps(xt, xln, T[:, 0:2048], smb)

            def s_cast(s):
                return s.copy(out=xlnb, in_=xln)
            steps.append(("s", s_cast))
            self.seq(*steps)
            for ft in range(NFT):
                pst = self.psA[:, (ft % 2) * 512:(ft % 2) * 512 + 64].bitcast(BF16)

                def t_tr(t, ft=ft, pst=pst):
                    self.w(t, "sv")
                    return t.transpose(pst, xlnb[:, ft * P:(ft + 1) * P], self.identb)

                def v_ev(v, ft=ft, tt=tt, pst=pst):
                    c = (ft * NTT + tt) * P
                    return v.tensor_copy(out=xlnT[:, c:c + P], in_=pst)
                self.seq(("t", t_tr), ("v", v_ev))

        def load_w(name, K, N, wreg, cols=None):
            @block.gpsimd
            def _(g):
                self.w(g, "stt")
                for kt in range(K // P):
                    src = d[name][kt * P:(kt + 1) * P, :] if cols is None else                         d[name][kt * P:(kt + 1) * P, cols[0]:cols[1]]
                    self.inc(g.dma_start(out=wreg[:, kt * N:(kt + 1) * N], in_=src), "ldg")
            return self.cnt["ldg"]

        def proj_mm(t, pdst, W, nkt, lhs_fn, rhs_fn):
            mm = None
            for kt in range(nkt):
                for o2 in range(W // 512):
                    mm = t.matmul(pdst[:, o2 * 512:(o2 + 1) * 512], lhs_fn(kt),
                                  rhs_fn(kt, o2), start=(kt == 0), stop=(kt == nkt - 1))
            return mm

        for wname, dst in [("W_dkv", ckvT), ("W_dq", cqT)]:
            wreg = WREG.bitcast(BF16)
            ldg_ev = load_w(wname, D, 512, wreg)
            for mt in range(4):
                for t2 in range(2):
                    pm = self.ps[(mt * 2 + t2) % 2]

                    @block.tensor
                    def _(t, mt=mt, t2=t2, pm=pm, wreg=wreg, ldg_ev=ldg_ev):
                        t.wait_ge(self.sems["ldg"], ldg_ev)
                        self.w(t, "sv")
                        mm = None
                        for ft in range(NFT):
                            mm = t.matmul(pm[:, 0:512],
                                          wreg[:, ft * 512 + mt * P:ft * 512 + mt * P + P],
                                          xlnT[:, ft * 1024 + t2 * 512:ft * 1024 + t2 * 512 + 512],
                                          start=(ft == 0), stop=(ft == NFT - 1))
                        self.inc(mm, "stt")
                    mm_ev = self.cnt["stt"]

                    @block.vector
                    def _(v, dst=dst, mt=mt, t2=t2, pm=pm):
                        v.wait_ge(self.sems["stt"], mm_ev)
                        self.inc(v.tensor_copy(
                            out=dst[:, mt * 1024 + t2 * 512:mt * 1024 + t2 * 512 + 512],
                            in_=pm[:, 0:512]), "sv")

        def emit_rope(v, psrc, dstbf, tt):
            hv = psrc.rearrange("p (h x) -> p h x", x=64)
            c1 = cosb[:, tt * 32:tt * 32 + 16].unsqueeze(1).broadcast_to([P, 16, 16])
            s1 = sinb[:, tt * 32:tt * 32 + 16].unsqueeze(1).broadcast_to([P, 16, 16])
            t1 = rtmp[:, 0:256].rearrange("p (h x) -> p h x", x=16)
            t2 = rtmp[:, 256:512].rearrange("p (h x) -> p h x", x=16)
            dv = dstbf.rearrange("p (h x) -> p h x", x=128)
            x1, x2 = hv[:, :, 0:16], hv[:, :, 16:32]
            v.tensor_tensor(out=t1, in0=x1, in1=c1, op=OP.mult)
            v.tensor_tensor(out=t2, in0=x2, in1=s1, op=OP.mult)
            self.fence(v, "v", v.tensor_tensor(out=dv[:, :, 64:80], in0=t1, in1=t2, op=OP.subtract))
            v.tensor_tensor(out=t1, in0=x2, in1=c1, op=OP.mult)
            v.tensor_tensor(out=t2, in0=x1, in1=s1, op=OP.mult)
            self.fence(v, "v", v.tensor_tensor(out=dv[:, :, 80:96], in0=t1, in1=t2, op=OP.add))
            return v.tensor_copy(out=dv[:, :, 96:128], in_=hv[:, :, 32:64])

        wkr = WREG.bitcast(BF16)
        for hh in range(2):
            ldg_ev = load_w("W_kr", D, 512, wkr, cols=(hh * 512, hh * 512 + 512))
            for tt in range(NTT):
                pk = self.ps[tt % 2]

                @block.tensor
                def _(t, tt=tt, pk=pk, ldg_ev=ldg_ev):
                    t.wait_ge(self.sems["ldg"], ldg_ev)
                    self.w(t, "sv")
                    mm = None
                    for kt in range(NFT):
                        mm = t.matmul(pk[:, 0:512],
                                      xlnT[:, kt * 1024 + tt * P:kt * 1024 + tt * P + P],
                                      wkr[:, kt * 512:(kt + 1) * 512],
                                      start=(kt == 0), stop=(kt == NFT - 1))
                    self.inc(mm, "stt")
                mm_ev = self.cnt["stt"]

                @block.vector
                def _(v, tt=tt, pk=pk, hh=hh):
                    v.wait_ge(self.sems["stt"], mm_ev)
                    kr = krnat[:, tt * 1024 + hh * 512:tt * 1024 + hh * 512 + 512]
                    hv = pk[:, 0:512].rearrange("p (h x) -> p h x", x=64)
                    c1 = cosb[:, tt * 32:tt * 32 + 16].unsqueeze(1).broadcast_to([P, 8, 16])
                    s1 = sinb[:, tt * 32:tt * 32 + 16].unsqueeze(1).broadcast_to([P, 8, 16])
                    t1 = rtmp[:, 0:128].rearrange("p (h x) -> p h x", x=16)
                    t2 = rtmp[:, 128:256].rearrange("p (h x) -> p h x", x=16)
                    dv = kr.rearrange("p (h x) -> p h x", x=64)
                    x1, x2 = hv[:, :, 0:16], hv[:, :, 16:32]
                    v.tensor_tensor(out=t1, in0=x1, in1=c1, op=OP.mult)
                    self.fence(v, "v", v.tensor_tensor(out=t2, in0=x2, in1=s1, op=OP.mult))
                    self.fence(v, "v", v.tensor_tensor(out=dv[:, :, 0:16], in0=t1, in1=t2, op=OP.subtract))
                    v.tensor_tensor(out=t1, in0=x2, in1=c1, op=OP.mult)
                    self.fence(v, "v", v.tensor_tensor(out=t2, in0=x1, in1=s1, op=OP.mult))
                    v.tensor_tensor(out=dv[:, :, 16:32], in0=t1, in1=t2, op=OP.add)
                    self.inc(v.tensor_copy(out=dv[:, :, 32:64], in_=hv[:, :, 32:64]), "sv")

        wuk = WREG.bitcast(BF16)
        ldg_ev = load_w("W_uk", 512, 1024, wuk)
        for tt in range(NTT):
            pk = [self.psA, self.psS][tt % 2]

            @block.tensor
            def _(t, tt=tt, pk=pk, ldg_ev=ldg_ev):
                t.wait_ge(self.sems["ldg"], ldg_ev)
                self.w(t, "sv")
                self.inc(proj_mm(t, pk, 1024, 4,
                                 lambda kt: ckvT[:, kt * 1024 + tt * P:kt * 1024 + tt * P + P],
                                 lambda kt, o2: wuk[:, kt * 1024 + o2 * 512:kt * 1024 + o2 * 512 + 512]), "stt")
            mm_ev = self.cnt["stt"]

            @block.vector
            def _(v, tt=tt, pk=pk):
                v.wait_ge(self.sems["stt"], mm_ev)
                self.w(v, "stt")
                av = asm.rearrange("p (h x) -> p h x", x=128)
                v.tensor_copy(out=av[:, :, 0:64],
                              in_=pk[:, 0:1024].rearrange("p (h x) -> p h x", x=64))
                self.inc(v.tensor_copy(
                    out=av[:, :, 64:128],
                    in_=krnat[:, tt * 1024:(tt + 1) * 1024].rearrange("p (h x) -> p h x", x=64)), "sv")
            for ft in range(NFT):
                pst = self.ps[2 + ft % 2][:, 0:64].bitcast(BF16)

                def t_tr(t, ft=ft, pst=pst):
                    self.w(t, "sv")
                    return t.transpose(pst, asm[:, ft * P:(ft + 1) * P], self.identb)

                def v_ev(v, ft=ft, tt=tt, pst=pst):
                    c = (ft * NTT + tt) * P
                    return v.tensor_copy(out=kT[:, c:c + P], in_=pst)
                self.seq(("t", t_tr), ("v", v_ev))

        wuq = WREG.bitcast(BF16)
        ldg_ev = load_w("W_uq", 512, 1024, wuq)
        wqr = PA[:, 6144:8192].bitcast(BF16)
        ldg_ev2 = load_w("W_qr", 512, 1024, wqr)
        for tt in range(NTT):
            pq = [self.psA, self.psS][tt % 2]
            pr = self.ps[tt % 2]

            @block.tensor
            def _(t, tt=tt, pq=pq, ldg_ev=ldg_ev2):
                t.wait_ge(self.sems["ldg"], ldg_ev)
                self.w(t, "sv")
                proj_mm(t, pq, 1024, 4,
                        lambda kt: cqT[:, kt * 1024 + tt * P:kt * 1024 + tt * P + P],
                        lambda kt, o2: wuq[:, kt * 1024 + o2 * 512:kt * 1024 + o2 * 512 + 512])
                mm = None
                for kt in range(4):
                    for o2 in range(2):
                        mm = t.matmul(self.ps[o2][:, 0:512],
                                      cqT[:, kt * 1024 + tt * P:kt * 1024 + tt * P + P],
                                      wqr[:, kt * 1024 + o2 * 512:kt * 1024 + o2 * 512 + 512],
                                      start=(kt == 0), stop=(kt == 3))
                self.inc(mm, "stt")
            mm_ev = self.cnt["stt"]

            @block.vector
            def _(v, tt=tt, pq=pq):
                v.wait_ge(self.sems["stt"], mm_ev)
                qn = qnat[:, tt * 2048:(tt + 1) * 2048]
                qv = qn.rearrange("p (h x) -> p h x", x=128)
                v.tensor_copy(out=qv[:, :, 0:64],
                              in_=pq[:, 0:1024].rearrange("p (h x) -> p h x", x=64))
                for half in range(2):
                    hv = self.ps[half][:, 0:512].rearrange("p (h x) -> p h x", x=64)
                    c1 = cosb[:, tt * 32:tt * 32 + 16].unsqueeze(1).broadcast_to([P, 8, 16])
                    s1 = sinb[:, tt * 32:tt * 32 + 16].unsqueeze(1).broadcast_to([P, 8, 16])
                    t1 = rtmp[:, 0:128].rearrange("p (h x) -> p h x", x=16)
                    t2 = rtmp[:, 128:256].rearrange("p (h x) -> p h x", x=16)
                    dv = qv[:, half * 8:half * 8 + 8, :]
                    x1, x2 = hv[:, :, 0:16], hv[:, :, 16:32]
                    v.tensor_tensor(out=t1, in0=x1, in1=c1, op=OP.mult)
                    self.fence(v, "v", v.tensor_tensor(out=t2, in0=x2, in1=s1, op=OP.mult))
                    self.fence(v, "v", v.tensor_tensor(out=dv[:, :, 64:80], in0=t1, in1=t2, op=OP.subtract))
                    v.tensor_tensor(out=t1, in0=x2, in1=c1, op=OP.mult)
                    self.fence(v, "v", v.tensor_tensor(out=t2, in0=x1, in1=s1, op=OP.mult))
                    v.tensor_tensor(out=dv[:, :, 80:96], in0=t1, in1=t2, op=OP.add)
                    v.tensor_copy(out=dv[:, :, 96:128], in_=hv[:, :, 32:64])
                self.inc(v.engine_nop() if False else v.tensor_copy(
                    out=rtmp[:, 0:1], in_=rtmp[:, 0:1]), "sv")

        wuv = WREG.bitcast(BF16)
        ldg_ev = load_w("W_uv", 512, 2048, wuv)
        for tt in range(NTT):
            @block.tensor
            def _(t, tt=tt, ldg_ev=ldg_ev):
                t.wait_ge(self.sems["ldg"], ldg_ev)
                self.w(t, "sv")
                for half, pv in [(0, self.psA), (1, self.psS)]:
                    self.inc(proj_mm(t, pv, 1024, 4,
                             lambda kt: ckvT[:, kt * 1024 + tt * P:kt * 1024 + tt * P + P],
                             lambda kt, o2, half=half: wuv[:, kt * 2048 + half * 1024 + o2 * 512:
                                                           kt * 2048 + half * 1024 + o2 * 512 + 512]), "stt")
            mm_ev = self.cnt["stt"]

            @block.vector
            def _(v, tt=tt):
                v.wait_ge(self.sems["stt"], mm_ev)
                v.tensor_copy(out=vnat[:, tt * 2048:tt * 2048 + 1024], in_=self.psA[:, 0:1024])
                self.inc(v.tensor_copy(out=vnat[:, tt * 2048 + 1024:tt * 2048 + 2048],
                                       in_=self.psS[:, 0:1024]), "sv")

        Msb = WS[:, 0:2048]
        @block.tensor
        def _(t):
            self.w(t, "sv")
            for h in range(16):
                pm = [self.psA, self.psS][h // 8][:, (h % 8) * P:(h % 8) * P + P]
                mm = None
                for tt in range(NTT):
                    mm = t.matmul(pm, qnat[:, tt * 2048 + h * P:tt * 2048 + h * P + P],
                                  vnat[:, tt * 2048 + h * P:tt * 2048 + h * P + P],
                                  start=(tt == 0), stop=(tt == NTT - 1))
                self.inc(mm, "stt")
        mm_ev = self.cnt["stt"]

        @block.vector
        def _(v):
            v.wait_ge(self.sems["stt"], mm_ev)
            v.tensor_scalar(out=Msb[:, 0:1024], in0=self.psA[:, 0:1024],
                            scalar1=ISQRT_DH, scalar2=None, op0=OP.mult)
            self.inc(v.tensor_scalar(out=Msb[:, 1024:2048], in0=self.psS[:, 0:1024],
                                     scalar1=ISQRT_DH, scalar2=None, op0=OP.mult), "sv")
        sv_ev = self.cnt["sv"]

        Mb = WS[:, 0:1024].bitcast(BF16)
        @block.gpsimd
        def _(g):
            g.wait_ge(self.sems["sv"], sv_ev)
            self.inc(g.dma_start(out=d["ccm_i"], in_=Msb), "ldg")
            self.w(g, "ldg")
            self.inc(g.collective_compute(
                "AllReduce", OP.add,
                replica_groups=[[0, 1], [2, 3], [4, 5], [6, 7]],
                ins=[d["ccm_i"].opt()], outs=[d["ccm_o"].opt()]), "cc")
            self.w(g, "cc")
            self.inc(g.dma_start(out=Mb, in_=d["ccm_o"]), "ldg")
        cc_ev = self.cnt["ldg"]

        for h in range(16):
            for t2 in range(2):
                po = self.ps[(h * 2 + t2) % 2]

                @block.tensor
                def _(t, h=h, t2=t2, po=po):
                    t.wait_ge(self.sems["ldg"], cc_ev)
                    self.w(t, "sv")
                    self.inc(t.matmul(po[:, 0:512], Mb[:, h * P:(h + 1) * P],
                                      kT[:, h * 1024 + t2 * 512:h * 1024 + t2 * 512 + 512],
                                      start=True, stop=True), "stt")
                mm_ev = self.cnt["stt"]

                @block.vector
                def _(v, h=h, t2=t2, po=po):
                    v.wait_ge(self.sems["stt"], mm_ev)
                    self.inc(v.tensor_copy(
                        out=aoT[:, h * 1024 + t2 * 512:h * 1024 + t2 * 512 + 512],
                        in_=po[:, 0:512]), "sv")

        wo_slot = [WS[:, 0:256].bitcast(BF16), WS[:, 256:512].bitcast(BF16)]
        banks = [self.psA[:, 0:512], self.psA[:, 512:1024],
                 self.psS[:, 0:512], self.psS[:, 512:1024],
                 self.ps[0][:, 0:512], self.ps[1][:, 0:512],
                 self.ps[2][:, 0:512], self.ps[3][:, 0:512]]
        for o2 in range(4):
            base_ev = self.cnt["sv"]
            for ft in range(NFT):
                slot = ft % 2

                @block.gpsimd
                def _(g, o2=o2, ft=ft, slot=slot):
                    self.w(g, "stt")
                    self.inc(g.dma_start(
                        out=wo_slot[slot],
                        in_=d["W_out"][ft * P:(ft + 1) * P, o2 * 512:(o2 + 1) * 512]), "ldg")
                ld_ev = self.cnt["ldg"]

                @block.tensor
                def _(t, o2=o2, ft=ft, slot=slot, base_ev=base_ev):
                    t.wait_ge(self.sems["ldg"], ld_ev)
                    if ft == 0:
                        t.wait_ge(self.sems["sv"], base_ev)
                    mm = None
                    for tt in range(NTT):
                        mm = t.matmul(banks[tt], aoT[:, ft * 1024 + tt * P:ft * 1024 + tt * P + P],
                                      wo_slot[slot], start=(ft == 0), stop=(ft == NFT - 1))
                    self.inc(mm, "stt")
            mm_ev = self.cnt["stt"]

            @block.vector
            def _(v, o2=o2):
                v.wait_ge(self.sems["stt"], mm_ev)
                last = None
                for tt in range(NTT):
                    xsl = xres[:, tt * D + o2 * 512:tt * D + o2 * 512 + 512]
                    last = v.tensor_tensor(out=xsl, in0=banks[tt], in1=xsl, op=OP.add)
                self.inc(last, "sv")
        self.barrier()

    def emit_ternary(self):
        d, block = self.d, self.block
        PA, PB, SM = self.PA, self.PB, self.SM
        thrb = SM[:, 832:848]
        negb = SM[:, 848:864]
        qfull = PA.bitcast(BF16)

        ld_ev, thr_ev, tr_ev = {}, {}, {-2: 0, -1: 0}
        PC = self.PC
        for ot in range(NFT):
            pool = [PB, PC][ot % 2]
            lsem = ["ldh", "ha"][ot % 2]
            nt = pool[:, 0:2048]
            ab = pool[:, 2048:4096]
            qn = pool[:, 2048:3072].bitcast(BF16)
            c1 = pool[:, 4096:6144]
            c2 = pool[:, 6144:8192]

            @block.sync
            def _(sync, ot=ot, nt=nt, lsem=lsem):
                if ot >= 2:
                    sync.wait_ge(self.sems["sv"], thr_ev[ot - 2])
                self.inc(sync.dma_start(out=nt, in_=d["l1f_nat"][ot * P:(ot + 1) * P, :]),
                         lsem)
            ld_ev[ot] = self.cnt[lsem]

            self._last_seq = None

            def s_abs(s, ot=ot, nt=nt, ab=ab, lsem=lsem):
                s.wait_ge(self.sems[lsem], ld_ev[ot])
                if ot >= 2:
                    s.wait_ge(self.sems["stt"], tr_ev[ot - 2])
                    s.wait_ge(self.sems["sv"], thr_ev[ot - 2])
                return s.activation(out=ab, in_=nt, func=ACT.Abs)

            def v_thr(v, ot=ot, ab=ab, nt=nt, c1=c1, c2=c2, qn=qn):
                self.fence(v, "v", v.tensor_reduce(out=thrb[:, ot:ot + 1], in_=ab,
                                                   axis=AX.X, op=OP.add))
                self.fence(v, "v", v.tensor_single_scalar(
                    out=thrb[:, ot:ot + 1], in_=thrb[:, ot:ot + 1],
                    scalar=0.7 / 2048.0, op=OP.mult))
                self.fence(v, "v", v.tensor_single_scalar(
                    out=negb[:, ot:ot + 1], in_=thrb[:, ot:ot + 1],
                    scalar=-1.0, op=OP.mult))
                v.tensor_single_scalar(out=c1, in_=nt, scalar=thrb[:, ot:ot + 1], op=OP.is_gt)
                self.fence(v, "v", v.tensor_single_scalar(
                    out=c2, in_=nt, scalar=negb[:, ot:ot + 1], op=OP.is_lt))
                return v.tensor_tensor(out=qn, in0=c1, in1=c2, op=OP.subtract)
            self.seq(("s", s_abs), ("v", v_thr))
            thr_ev[ot] = self.cnt["sv"]
            for it in range(NFT):
                pst = self.psA[:, (it % 2) * 512:(it % 2) * 512 + 64].bitcast(BF16)

                def t_tr(t, it=it, pst=pst, qn=qn):
                    self.w(t, "sv")
                    return t.transpose(pst, qn[:, it * P:(it + 1) * P], self.identb)

                def v_ev(v, it=it, ot=ot, pst=pst):
                    return v.tensor_copy(out=qfull[:, it * 2048 + ot * P:it * 2048 + ot * P + P],
                                         in_=pst)
                self.seq(("t", t_tr), ("v", v_ev))
            tr_ev[ot] = self.cnt["stt"]

        def y_store(sync):
            self.w(sync, "sv")
            last = None
            for it in range(NFT):
                last = sync.dma_start(out=d["ql1"][it], in_=qfull[:, it * 2048:(it + 1) * 2048])
                if it < NFT - 1:
                    self.inc(last, "st")
            return last
        self.seq(("y", y_store))
        self.barrier()

    def emit_l1ffn(self, which):
        d, block = self.d, self.block
        PA, PB, PC, SM = self.PA, self.PB, self.PC, self.SM
        xres = self.xres
        xlnT = PC.bitcast(BF16)
        xln = PA[:, 8192:10240]
        T = PA[:, 12288:16384]
        xlnb = T[:, 3072:4096].bitcast(BF16)
        smb = SM[:, 0:16]
        biasrow = PA[:, 10240:11264].bitcast(BF16)

        def g_bias(g):
            return g.dma_start(out=biasrow[0:1, :], in_=d["l1f_bias"])
        self.seq(("g", g_bias))

        for tt in range(NTT):
            xt = xres[:, tt * D:(tt + 1) * D]
            steps = self._ln_steps(xt, xln, T[:, 0:2048], smb)

            def s_cast(s):
                return s.copy(out=xlnb, in_=xln)
            steps.append(("s", s_cast))
            self.seq(*steps)
            for ft in range(NFT):
                pst = self.psA[:, (ft % 2) * 512:(ft % 2) * 512 + 64].bitcast(BF16)

                def t_tr(t, ft=ft, pst=pst):
                    self.w(t, "sv")
                    return t.transpose(pst, xlnb[:, ft * P:(ft + 1) * P], self.identb)

                def v_ev(v, ft=ft, tt=tt, pst=pst):
                    c = (ft * NTT + tt) * P
                    return v.tensor_copy(out=xlnT[:, c:c + P], in_=pst)
                self.seq(("t", t_tr), ("v", v_ev))

        qsl = [PB.bitcast(BF16), PA[:, 0:8192].bitcast(BF16)]

        def y_lq(sync):
            last = None
            for it in range(NFT):
                dstb = qsl[it // 8]
                last = sync.dma_start(out=dstb[:, (it % 8) * 2048:(it % 8) * 2048 + 2048],
                                      in_=d["ql1"][it])
                if it < NFT - 1:
                    self.inc(last, "ldh")
            return last
        self.seq(("y", y_lq))
        lq_ev = self.cnt["ldh"]

        for tt in range(NTT):
            for o2 in range(4):
                po = self.ps[(tt * 4 + o2) % 2]

                @block.tensor
                def _(t, tt=tt, o2=o2, po=po):
                    t.wait_ge(self.sems["ldh"], lq_ev)
                    self.w(t, "ldg")
                    self.w(t, "sv")
                    for it in range(NFT):
                        dstb = qsl[it // 8]
                        t.matmul(po[:, 0:512],
                                 xlnT[:, it * 1024 + tt * P:it * 1024 + tt * P + P],
                                 dstb[:, (it % 8) * 2048 + o2 * 512:(it % 8) * 2048 + o2 * 512 + 512],
                                 start=(it == 0), stop=False)
                    self.inc(t.matmul(po[:, 0:512], self.ones1t,
                                      biasrow[0:1, o2 * 512:o2 * 512 + 512],
                                      start=False, stop=True), "stt")
                mm_ev = self.cnt["stt"]

                @block.vector
                def _(v, tt=tt, o2=o2, po=po):
                    v.wait_ge(self.sems["stt"], mm_ev)
                    xsl = xres[:, tt * D + o2 * 512:tt * D + o2 * 512 + 512]
                    self.inc(v.scalar_tensor_tensor(out=xsl, in0=po[:, 0:512], scalar=0.5,
                                                    in1=xsl, op0=OP.mult, op1=OP.add), "sv")
        self.barrier()

    def emit_l1attn(self, stop=99):
        d, block = self.d, self.block
        PA, PB, PC, WS, SM = self.PA, self.PB, self.PC, self.WS, self.SM
        xres = self.xres
        self.m0f = PB.bitcast(BF16)
        self.m1f = PA[:, 0:8192].bitcast(BF16)
        self.kTf = PC.bitcast(BF16)
        self.rTf = xres[:, 8192:16384].bitcast(BF16)
        self.vnf = xres[:, 0:8192].bitcast(BF16)
        self.mixedin = PA[:, 12288:16384].bitcast(BF16)
        self.smLN = SM[:, 0:16]
        self.smGN = SM[:, 16:32]
        self.tmv = SM[:, 32:144]
        self.hcol = SM[:, 144:160]
        self.hprev = SM[:, 160:176]
        self.cLb = SM[:, 176:432]
        self.ALb = SM[:, 432:688]
        self.Lam = SM[:, 688:944]
        self.zbuf = SM[:, 960:1024]

        def v_z(v):
            return v.memset(self.zbuf, 0.0)
        self.seq(("v", v_z))

        def y_sp(sync):
            self.inc(sync.dma_start(
                out=d["xspill"].rearrange("(tt p) dd -> p tt dd", p=P),
                in_=xres[:, :].rearrange("p (tt dd) -> p tt dd", dd=D)), "st")
            last = None
            for i, nm in enumerate(["tm_x", "tm_w", "tm_k", "tm_v", "tm_r", "tm_g", "tdecay"]):
                last = sync.dma_start(out=self.tmv[:, i * 16:(i + 1) * 16], in_=d[nm])
                if i < 6:
                    self.inc(last, "ldh")
            return last
        self.seq(("y", y_sp))

        self.ubdb = WS[:, 0:256].bitcast(BF16)

        if stop >= 2:
            self._l1a_halo()
        for th in range(2):
            if stop < 3 + th * 3:
                break
            self._l1a_lnhalf(th)
            if stop < 4 + th * 3:
                break
            self._l1a_mpath(th)
            if stop < 5 + th * 3:
                break
            for pname in ["w", "g", "k", "v", "r"]:
                self._l1a_mix(pname, th)
                self._l1a_sweep(pname, th)
        if stop >= 9:
            self._l1a_pass1()
        if stop >= 10:
            self._l1a_pass2()
        if stop >= 11:
            self._l1a_end()
        self.barrier()

    def _l1a_halo(self):
        d, block = self.d, self.block
        PA, SM = self.PA, self.SM
        hrow = PA[0:1, 11264:13312]
        smb = self.smGN
        xrow = PA[0:1, 8192:10240]

        def y_row(sync):
            return sync.dma_start(out=xrow, in_=self.xres[127:128, 7 * D:8 * D])
        self.seq(("y", y_row))

        def v_s(v):
            self.fence(v, "v", v.tensor_reduce(out=smb[0:1, 0:1], in_=xrow, axis=AX.X, op=OP.add))
            return None

        def s_sq(s):
            return s.activation(out=hrow, in_=xrow, func=ACT.Square, accum_out=smb[0:1, 1:2])

        def v_st(v):
            self.fence(v, "v", v.tensor_single_scalar(out=smb[0:1, 2:3], in_=smb[0:1, 0:1],
                                                      scalar=1.0 / 2048.0, op=OP.mult))
            self.fence(v, "v", v.tensor_single_scalar(out=smb[0:1, 3:4], in_=smb[0:1, 1:2],
                                                      scalar=1.0 / 2048.0, op=OP.mult))
            self.fence(v, "v", v.tensor_tensor(out=smb[0:1, 4:5], in0=smb[0:1, 2:3],
                                               in1=smb[0:1, 2:3], op=OP.mult))
            return v.tensor_tensor(out=smb[0:1, 4:5], in0=smb[0:1, 3:4],
                                   in1=smb[0:1, 4:5], op=OP.subtract)

        def s_sd(s):
            return s.activation(out=smb[0:1, 5:6], in_=smb[0:1, 4:5], func=ACT.Sqrt,
                                bias=self.epsln[0:1, :])

        def v_r(v):
            self.fence(v, "v", v.reciprocal(out=smb[0:1, 5:6], in_=smb[0:1, 5:6]))
            return v.scalar_tensor_tensor(out=smb[0:1, 6:7], in0=smb[0:1, 2:3],
                                          scalar=-1.0, in1=smb[0:1, 5:6],
                                          op0=OP.mult, op1=OP.mult)

        def s_n(s):
            return s.activation(out=hrow, in_=xrow, func=ACT.Identity,
                                bias=smb[0:1, 6:7], scale=smb[0:1, 5:6])
        self.seq(("v", v_s), ("s", s_sq), ("v", v_st), ("s", s_sd), ("v", v_r), ("s", s_n))
        for ft in range(NFT):
            pst = self.ps[ft % 2]

            def t_tr(t, ft=ft, pst=pst):
                self.w(t, "sv")
                return t.transpose(pst[:, 0:1], hrow[0:1, ft * P:(ft + 1) * P],
                                   self.identf[0:1, 0:1])

            def v_ev(v, ft=ft, pst=pst):
                return v.tensor_copy(out=self.hcol[:, ft:ft + 1], in_=pst[:, 0:1])
            self.seq(("t", t_tr), ("v", v_ev))

        def v_m(v):
            return v.tensor_single_scalar(out=self.hcol, in_=self.hcol,
                                          scalar=self.mfirst, op=OP.mult)
        self.seq(("v", v_m))

        @block.gpsimd
        def _(g):
            self.w(g, "sv")
            self.inc(g.dma_start(out=d["cch_i"], in_=self.hcol), "ldg")
            self.w(g, "ldg")
            self.inc(g.collective_compute(
                "AllReduce", OP.add, replica_groups=[[0, 1], [2, 3], [4, 5], [6, 7]],
                ins=[d["cch_i"].opt()], outs=[d["cch_o"].opt()]), "cc")
            self.w(g, "cc")
            self.inc(g.dma_start(out=self.hcol, in_=d["cch_o"]), "ldg")
        hl_ev = self.cnt["ldg"]

        def v_m2(v):
            v.wait_ge(self.sems["ldg"], hl_ev)
            return v.tensor_single_scalar(out=self.hcol, in_=self.hcol,
                                          scalar=self.msecond, op=OP.mult)
        self.seq(("v", v_m2))

    def _l1a_lnhalf(self, th):
        d = self.d
        PA = self.PA
        xbuf = PA[:, 13312:15360]
        xln = PA[:, 11264:13312]
        smb = self.smLN
        if th == 1:
            def v_hp(v):
                self.w(v, "stt")
                return v.tensor_copy(
                    out=self.hprev,
                    in_=self.xres[:, 0:8192].rearrange("p (ft s) -> p ft s", s=512)[:, :, 511])
            self.seq(("v", v_hp))
        for ttl in range(4):
            tt = th * 4 + ttl

            def y_ld(sync, tt=tt):
                self.w(sync, "ss")
                return sync.dma_start(
                    out=xbuf, in_=d["xspill"][tt * P:(tt + 1) * P, :])
            self.seq(("y", y_ld))
            self.seq(*self._ln_steps(xbuf, xln, xln, smb))
            for ft in range(NFT):
                pst = self.psA[:, (ft % 2) * 512:(ft % 2) * 512 + 128]

                def t_tr(t, ft=ft, pst=pst, xln=xln):
                    self.w(t, "sv")
                    return t.transpose(pst, xln[:, ft * P:(ft + 1) * P], self.identf)

                def v_ev(v, ft=ft, ttl=ttl, pst=pst):
                    c = (ft * 4 + ttl) * P
                    return v.tensor_copy(out=self.xres[:, c:c + P], in_=pst)
                self.seq(("t", t_tr), ("v", v_ev))

    def _l1a_mpath(self, th):
        d, block = self.d, self.block
        PA, WS = self.PA, self.WS
        mixedin = self.mixedin
        mTa = PA[:, 10240:10496].bitcast(BF16)
        mTb = PA[0:32, 10752:11008].bitcast(BF16)
        prev = self.hcol if th == 0 else self.hprev

        def v_xxx(v, th=th, prev=prev):
            self.w(v, "stt")
            xxp = PA[:, 8704:9216]
            for ct in range(NFT):
                xp = self.xres[:, ct * 512:(ct + 1) * 512]
                v.tensor_tensor(out=xxp[:, 1:512], in0=xp[:, 0:511], in1=xp[:, 1:512],
                                op=OP.subtract)
                self.fence(v, "v", v.tensor_tensor(out=xxp[:, 0:1], in0=prev[:, ct:ct + 1],
                                                   in1=xp[:, 0:1], op=OP.subtract))
                tmcol = self.tmv[:, 0 * 16 + ct:0 * 16 + ct + 1]
                self.fence(v, "v", v.tensor_single_scalar(out=xxp, in_=xxp, scalar=tmcol,
                                                          op=OP.mult))
                self.fence(v, "v", v.tensor_tensor(
                    out=mixedin[:, ct * 512:(ct + 1) * 512], in0=xxp, in1=xp, op=OP.add))
            return None
        self.seq(("v", v_xxx))

        w1b = WS[:, 0:1280].bitcast(BF16)

        def g_w1(g):
            self.w(g, "stt")
            last = None
            for kt in range(NFT):
                last = g.dma_start(out=w1b[:, kt * 160:(kt + 1) * 160],
                                   in_=d["w1"][kt * P:(kt + 1) * P, :])
                if kt < NFT - 1:
                    self.inc(last, "ldg")
            return last
        self.seq(("g", g_w1))
        ld_ev = self.cnt["ldg"]

        @block.tensor
        def _(t):
            t.wait_ge(self.sems["ldg"], ld_ev)
            self.w(t, "sv")
            self.w(t, "ss")
            mm = None
            for kt in range(NFT):
                t.matmul(self.ps[0][:, 0:512], w1b[:, kt * 160:kt * 160 + 128],
                         mixedin[:, kt * 512:(kt + 1) * 512],
                         start=(kt == 0), stop=(kt == NFT - 1))
                mm = t.matmul(self.ps[1][0:32, 0:512], w1b[:, kt * 160 + 128:kt * 160 + 160],
                              mixedin[:, kt * 512:(kt + 1) * 512],
                              start=(kt == 0), stop=(kt == NFT - 1))
            self.inc(mm, "stt")
        mm_ev = self.cnt["stt"]

        @block.scalar
        def _(s):
            s.wait_ge(self.sems["stt"], mm_ev)
            s.activation(out=mTa, in_=self.ps[0][:, 0:512], func=ACT.Tanh)
            self.inc(s.activation(out=mTb, in_=self.ps[1][0:32, 0:512], func=ACT.Tanh), "ss")

    def _l1a_mix(self, pname, th):
        d, block = self.d, self.block
        PA = self.PA
        pidx = {"w": 1, "k": 2, "v": 3, "r": 4, "g": 5}[pname]
        widx = {"w": 0, "k": 1, "v": 2, "r": 3, "g": 4}[pname]
        mTa = PA[:, 10240:10496].bitcast(BF16)
        mTb = PA[0:32, 10752:11008].bitcast(BF16)
        mixedin = self.mixedin
        wpA = PA[:, 11264:12288].bitcast(BF16)
        wpB = self.WS[0:32, 0:1024].bitcast(BF16)

        def g_w2(g, widx=widx):
            self.w(g, "stt")
            self.w(g, "sv")
            self.w(g, "ss")
            self.inc(g.dma_start(out=wpA, in_=d["w2pad"][widx, 0:128, :]), "ldg")
            return g.dma_start(out=wpB, in_=d["w2pad"][widx, 128:160, :])
        self.seq(("g", g_w2))
        ld_ev = self.cnt["ldg"]

        for ct in range(NFT):
            pm = self.ps[2 + ct % 2]

            @block.tensor
            def _(t, ct=ct, pm=pm, ld_ev=ld_ev):
                t.wait_ge(self.sems["ldg"], ld_ev)
                self.w(t, "sv")
                t.matmul(pm[:, 0:512], wpA[:, ct * P:(ct + 1) * P], mTa,
                         start=True, stop=False)
                self.inc(t.matmul(pm[:, 0:512], wpB[:, ct * P:(ct + 1) * P], mTb,
                                  start=False, stop=True), "stt")
            mm_ev = self.cnt["stt"]

            @block.vector
            def _(v, ct=ct, pm=pm, pidx=pidx, th=th):
                v.wait_ge(self.sems["stt"], mm_ev)
                e1 = PA[:, 8192:8704]
                xxp = PA[:, 8704:9216]
                xp = self.xres[:, ct * 512:(ct + 1) * 512]
                prev = self.hcol if th == 0 else self.hprev
                v.tensor_tensor(out=xxp[:, 1:512], in0=xp[:, 0:511], in1=xp[:, 1:512],
                                op=OP.subtract)
                self.fence(v, "v", v.tensor_tensor(out=xxp[:, 0:1], in0=prev[:, ct:ct + 1],
                                                   in1=xp[:, 0:1], op=OP.subtract))
                tmcol = self.tmv[:, pidx * 16 + ct:pidx * 16 + ct + 1]
                self.fence(v, "v", v.tensor_single_scalar(
                    out=e1, in_=pm[:, 0:512], scalar=tmcol, op=OP.add))
                self.fence(v, "v", v.tensor_tensor(out=e1, in0=e1, in1=xxp, op=OP.mult))
                self.inc(v.tensor_tensor(
                    out=mixedin[:, ct * 512:(ct + 1) * 512],
                    in0=e1, in1=xp, op=OP.add), "sv")

    def _l1a_sweep(self, pname, th):
        if pname == "w":
            self._l1a_sweep_w(th)
        elif pname == "g":
            d = self.d

            def y_xgspill(sync, th=th):
                self.w(sync, "sv")
                last = None
                for ct in range(NFT):
                    last = sync.dma_start(
                        out=d["xgsp"][ct, :, th * 512:(th + 1) * 512],
                        in_=self.mixedin[:, ct * 512:(ct + 1) * 512])
                    if ct < NFT - 1:
                        self.inc(last, "st")
                return last
            self.seq(("y", y_xgspill))
        elif pname == "k":
            self._l1a_sweep_big("wk", th, out="kT")
        elif pname == "v":
            self._l1a_sweep_big("wv", th, out="vsp")
        elif pname == "r":
            self._l1a_sweep_big("wr", th, out="rT")

    def _l1a_sweep_big(self, wname, th, out, spdst=None):
        d, block = self.d, self.block
        WS = self.WS
        mixedin = self.mixedin
        wslot = [WS[:, 1024:1536].bitcast(BF16), WS[:, 1536:2048].bitcast(BF16)]
        banks = [self.psA[:, 0:512], self.psA[:, 512:1024],
                 self.psS[:, 0:512], self.psS[:, 512:1024],
                 self.ps[0][:, 0:512], self.ps[1][:, 0:512],
                 self.ps[2][:, 0:512], self.ps[3][:, 0:512]]
        natural = (out == "vsp")
        for oh in range(2):
            base_ev = self.cnt["sv"]
            for kt in range(NFT):
                slot = kt % 2

                @block.gpsimd
                def _(g, oh=oh, kt=kt, slot=slot):
                    self.w(g, "stt")
                    self.inc(g.dma_start(
                        out=wslot[slot],
                        in_=d[wname][kt * P:(kt + 1) * P, oh * 1024:oh * 1024 + 1024]), "ldg")
                ld_ev = self.cnt["ldg"]

                @block.tensor
                def _(t, oh=oh, kt=kt, slot=slot, base_ev=base_ev):
                    t.wait_ge(self.sems["ldg"], ld_ev)
                    if kt == 0:
                        t.wait_ge(self.sems["sv"], base_ev)
                    mm = None
                    if natural:
                        for ttl in range(4):
                            for o2 in range(2):
                                mm = t.matmul(banks[ttl * 2 + o2],
                                              mixedin[:, (kt * 4 + ttl) * P:(kt * 4 + ttl) * P + P],
                                              wslot[slot][:, o2 * 512:o2 * 512 + 512],
                                              start=(kt == 0), stop=(kt == NFT - 1))
                    else:
                        for fo in range(8):
                            mm = t.matmul(banks[fo],
                                          wslot[slot][:, fo * P:(fo + 1) * P],
                                          mixedin[:, kt * 512:(kt + 1) * 512],
                                          start=(kt == 0), stop=(kt == NFT - 1))
                    self.inc(mm, "stt")
            mm_ev = self.cnt["stt"]

            if natural or out == "spill":
                stage = self.PA[:, 11264:12288].bitcast(BF16)
                for grp in range(2):
                    @block.vector
                    def _(v, oh=oh, grp=grp):
                        v.wait_ge(self.sems["stt"], mm_ev)
                        self.w(v, "ldg")
                        last = None
                        for j in range(4):
                            last = v.tensor_copy(out=stage[:, j * 512:(j + 1) * 512],
                                                 in_=banks[grp * 4 + j])
                        self.inc(last, "sv")
                    sv_ev = self.cnt["sv"]

                    @block.gpsimd
                    def _(g, oh=oh, grp=grp, sv_ev=sv_ev):
                        g.wait_ge(self.sems["sv"], sv_ev)
                        if natural:
                            self.inc(g.dma_start(
                                out=d["vsp"][th * 4 + grp * 2:th * 4 + grp * 2 + 2, :,
                                             oh * 1024:oh * 1024 + 1024]
                                    .rearrange("ttl p s -> p ttl s"),
                                in_=stage.rearrange("p (ttl s) -> p ttl s", s=1024)), "ldg")
                        else:
                            fo0 = oh * 8 + grp * 4
                            self.inc(g.dma_start(
                                out=d["xgsp"][fo0:fo0 + 4, :, th * 512:(th + 1) * 512]
                                    .rearrange("fo p s -> p fo s"),
                                in_=stage.rearrange("p (fo s) -> p fo s", s=512)), "ldg")
            else:
                @block.vector
                def _(v, oh=oh):
                    v.wait_ge(self.sems["stt"], mm_ev)
                    last = None
                    for fo in range(8):
                        fog = oh * 8 + fo
                        dst = self.kTf if out == "kT" else self.rTf
                        dv = dst.rearrange("p (ft tt s) -> p ft tt s", tt=NTT, s=P)
                        last = v.tensor_copy(
                            out=dv[:, fog, th * 4:(th + 1) * 4, :],
                            in_=banks[fo].rearrange("p (ttl s) -> p ttl s", s=P))
                    self.inc(last, "sv")

    def _l1a_sweep_w(self, th):
        d, block = self.d, self.block
        PA, WS = self.PA, self.WS
        mixedin = self.mixedin
        tw1b = WS[:, 1280:1792].bitcast(BF16)
        twb = PA[64:128, 10752:11008].bitcast(BF16)
        tw2b = WS[64:128, 0:1024].bitcast(BF16)

        def g_td(g):
            self.w(g, "stt")
            for kt in range(NFT):
                self.inc(g.dma_start(out=tw1b[:, kt * 64:(kt + 1) * 64],
                                     in_=d["td_w1"][kt * P:(kt + 1) * P, :]), "ldg")
            return g.dma_start(out=tw2b, in_=d["td_w2"][:, :])
        self.seq(("g", g_td))
        ld_ev = self.cnt["ldg"]

        @block.tensor
        def _(t):
            t.wait_ge(self.sems["ldg"], ld_ev)
            self.w(t, "sv")
            mm = None
            for kt in range(NFT):
                mm = t.matmul(self.ps[0][64:128, 0:512], tw1b[:, kt * 64:(kt + 1) * 64],
                              mixedin[:, kt * 512:(kt + 1) * 512],
                              start=(kt == 0), stop=(kt == NFT - 1))
            self.inc(mm, "stt")
        mm_ev = self.cnt["stt"]

        @block.scalar
        def _(s):
            s.wait_ge(self.sems["stt"], mm_ev)
            self.inc(s.activation(out=twb, in_=self.ps[0][64:128, 0:512], func=ACT.Tanh), "ss")
        tw_ev = self.cnt["ss"]

        dbuf = PA[:, 11264:11776]
        cbuf = PA[:, 11776:12288]
        for fo in range(NFT):
            pw = self.ps[2 + fo % 2]

            @block.tensor
            def _(t, fo=fo, pw=pw):
                self.w(t, "ss")
                self.w(t, "sv")
                self.inc(t.matmul(pw[64:128, 0:512] if False else pw[:, 0:512],
                                  tw2b[:, fo * P:(fo + 1) * P], twb,
                                  start=True, stop=True), "stt")
            mm_ev2 = self.cnt["stt"]

            @block.scalar
            def _(s, fo=fo, pw=pw):
                s.wait_ge(self.sems["stt"], mm_ev2)
                self.w(s, "sv")
                tdcol = self.tmv[:, 6 * 16 + fo:6 * 16 + fo + 1]
                self.inc(s.activation(out=dbuf, in_=pw[:, 0:512], func=ACT.Exp,
                                      bias=tdcol), "ss")
            d_ev = self.cnt["ss"]

            @block.vector
            def _(v, fo=fo, th=th):
                v.wait_ge(self.sems["ss"], d_ev)
                last = None
                for ch in range(8):
                    last = v.tensor_tensor_scan(
                        out=cbuf[:, ch * 64:(ch + 1) * 64],
                        data0=dbuf[:, ch * 64:(ch + 1) * 64],
                        data1=self.zbuf,
                        initial=0.0, op0=OP.add, op1=OP.add)
                self.fence(v, "v", last)
                gch0 = th * 8
                self.fence(v, "v", v.tensor_copy(
                    out=self.cLb[:, fo * 16 + gch0:fo * 16 + gch0 + 8],
                    in_=cbuf.rearrange("p (ch s) -> p ch s", s=64)[:, :, 63]))
                return v.tensor_tensor(out=dbuf, in0=dbuf, in1=cbuf, op=OP.subtract)
            cd_ev = self.cnt["sv"]

            @block.scalar
            def _(s, fo=fo, th=th):
                s.wait_ge(self.sems["sv"], cd_ev)
                dst0 = self.m0f.rearrange("p (ft tt s) -> p ft tt s", tt=NTT, s=P)
                dst1 = self.m1f.rearrange("p (ft tt s) -> p ft tt s", tt=NTT, s=P)
                s.activation(out=dst0[:, fo, th * 4:(th + 1) * 4, :],
                             in_=dbuf.rearrange("p (a b) -> p a b", b=P), func=ACT.Exp)
                self.inc(s.activation(out=dst1[:, fo, th * 4:(th + 1) * 4, :],
                                      in_=cbuf.rearrange("p (a b) -> p a b", b=P),
                                      func=ACT.Exp), "ss")
        def s_al(s, th=th):
            gch0 = th * 8
            alv = self.ALb.rearrange("p (f c) -> p f c", c=16)[:, :, gch0:gch0 + 8]
            clv = self.cLb.rearrange("p (f c) -> p f c", c=16)[:, :, gch0:gch0 + 8]
            return s.activation(out=alv, in_=clv, func=ACT.Exp, scale=self.cc(-1.0))
        self.seq(("s", s_al))



    def _l1a_pass1(self, nch=16, exch=True, csteps=10):
        d, block = self.d, self.block
        PA = self.PA
        Sb = PA[:, 8192:9216]

        msb = PA[:, 9216:10240]
        Pmb = [PA[:, 10240:10752].bitcast(BF16), PA[:, 10752:11264].bitcast(BF16)]
        khn = PA[:, 10240:11264].bitcast(BF16)
        tb2 = PA[:, 14336:14848].bitcast(BF16)
        kpb = PA[:, 14848:15360].bitcast(BF16)
        khb = PA[:, 15360:15872].bitcast(BF16)
        dsb = PA[0:32, 15872:15936]
        dTs = PA[:, 15936:15968]
        Sbc = PA[:, 11264:11776].bitcast(BF16)
        ybuf = PA[:, 12288:14336]
        self.Sb, self.ybuf = Sb, ybuf

        def y_pre(sync):
            self.w(sync, "stt"); self.w(sync, "sv"); self.w(sync, "ss")
            for tt in range(NTT):
                self.inc(sync.dma_start(out=self.vnf[:, tt * 2048:(tt + 1) * 2048],
                                        in_=d["vsp"][tt]), "ldh")
            return sync.dma_start(out=msb, in_=d["masksub"])
        self.seq(("y", y_pre))

        def g_ubd(g):
            self.w(g, "stt"); self.w(g, "sv")
            last = None
            for ft in range(NFT):
                last = g.dma_start(out=self.ubdb[:, ft * 32:(ft + 1) * 32],
                                   in_=d["u_bd"][ft * P:(ft + 1) * P, :])
                if ft < NFT - 1:
                    self.inc(last, "ldg")
            return last
        self.seq(("g", g_ubd))

        def v_init(v):
            self.fence(v, "v", v.memset(Sb, 0.0))
            self.fence(v, "v", v.memset(Sbc, 0.0))
            lam = self.Lam.rearrange("p (f c) -> p f c", c=16)
            return v.memset(lam[:, :, 0], 1.0)
        self.seq(("v", v_init))

        vstage2 = self.WS[0:64, 256:1280].bitcast(BF16)
        vhi = self.WS[64:128, 256:768].bitcast(BF16)
        for ct in range(nch):
            tt, rb = ct // 2, 0

            def cview(buf):
                return buf.rearrange("p (ft tt u s) -> p ft tt u s",
                                     tt=NTT, u=2, s=64)[:, :, tt, ct % 2, :]
            kv, rv = cview(self.kTf), cview(self.rTf)
            m0v, m1v = cview(self.m0f), cview(self.m1f)
            alv = self.ALb.rearrange("p (f c) -> p f c", c=16)[:, :, ct:ct + 1].broadcast_to([P, 16, 64])
            tb2v = tb2.rearrange("p (f s) -> p f s", s=64)
            kpv = kpb.rearrange("p (f s) -> p f s", s=64)
            khv = khb.rearrange("p (f s) -> p f s", s=64)

            def v_prep(v, kv=kv, rv=rv, m0v=m0v, m1v=m1v, alv=alv):
                self.w(v, "stt")
                v.tensor_tensor(out=tb2v, in0=rv, in1=kv, op=OP.mult)
                v.tensor_tensor(out=kpv, in0=kv, in1=m1v, op=OP.mult)
                self.fence(v, "v", v.tensor_tensor(out=khv, in0=kpv, in1=alv, op=OP.mult))
                return v.tensor_tensor(out=rv, in0=rv, in1=m0v, op=OP.mult)
            self.seq(("v", v_prep))
            prep_ev = self.cnt["sv"]
            if csteps < 2:
                continue

            if ct % 2 == 1:
                def y_vst(sync, tt=tt):
                    self.w(sync, "stt")
                    return sync.dma_start(out=vstage2,
                                          in_=self.vnf[64:128, tt * 2048:(tt + 1) * 2048])
                self.seq(("y", y_vst))
            vst_ev = self.cnt["ldh"]

            def v_vhi(v, ct=ct, tt=tt, vst_ev=vst_ev):
                v.wait_ge(self.sems["ldh"], vst_ev)
                vsrc = (self.vnf[0:64, tt * 2048:(tt + 1) * 2048] if ct % 2 == 0
                        else vstage2)
                vv = vsrc.rearrange("p (f u s) -> p f u s", u=2, s=64)
                return v.tensor_copy(out=vhi.rearrange("p (f s) -> p f s", s=64),
                                     in_=vv[:, :, 1, :])
            self.seq(("v", v_vhi))

            def vsl(ft, sub, tt=tt, ct=ct):
                if ct % 2 == 0:
                    return self.vnf[0:64, (tt * 16 + ft) * P + sub * 64:
                                    (tt * 16 + ft) * P + sub * 64 + 64]
                return vstage2[0:64, ft * P + sub * 64:ft * P + sub * 64 + 64]

            @block.tensor
            def _(t, ct=ct, tt=tt, rb=rb):
                t.wait_ge(self.sems["sv"], prep_ev)
                mm = None
                for ft in range(NFT):
                    mm = t.matmul(self.ps[2][0:32, 0:64], self.ubdb[:, ft * 32:(ft + 1) * 32],
                                  tb2[:, ft * 64:(ft + 1) * 64],
                                  start=(ft == 0), stop=(ft == NFT - 1))
                self.inc(mm, "stt")
                for ft in range(NFT):
                    for sub in range(2):
                        pb = [self.psA, self.psS][sub]
                        qslice = self.rTf[sub * 64:sub * 64 + 64,
                                          ((ft * NTT + tt) * P + (ct % 2) * 64) * 1:
                                          (ft * NTT + tt) * P + (ct % 2) * 64 + 64]
                        mm = t.matmul(pb[sub * 64:sub * 64 + 64, ft * 64:(ft + 1) * 64],
                                      kpb[sub * 64:sub * 64 + 64, ft * 64:(ft + 1) * 64],
                                      qslice, start=True, stop=True)
                self.inc(mm, "stt")
            pmm_ev = self.cnt["stt"]
            if csteps < 3:
                continue

            def v_de(v, rb=rb):
                v.wait_ge(self.sems["stt"], pmm_ev)
                v.tensor_copy(out=dsb, in_=self.ps[2][0:32, 0:64])
                v.tensor_tensor(out=Pmb[0][0:64, :], in0=self.psA[0:64, :],
                                in1=msb[0:64, :], op=OP.mult)
                return v.tensor_tensor(out=Pmb[1][64:128, :], in0=self.psS[64:128, :],
                                       in1=msb[64:128, :], op=OP.mult)
            self.seq(("v", v_de))
            if csteps < 4:
                continue

            def t_dtr(t, rb=rb):
                self.w(t, "sv")
                return t.transpose(self.ps[3][rb:rb + 64, 0:32], dsb, self.identf[0:32, 0:32])

            def v_dev(v, rb=rb):
                return v.tensor_copy(out=dTs[rb:rb + 64, :], in_=self.ps[3][rb:rb + 64, 0:32])
            self.seq(("t", t_dtr), ("v", v_dev))
            pm_ev = self.cnt["sv"]
            if csteps < 5:
                continue

            @block.tensor
            def _(t, ct=ct, tt=tt, rb=rb, vsl=vsl, vst_ev=vst_ev):
                t.wait_ge(self.sems["sv"], pm_ev)
                t.wait_ge(self.sems["ldh"], vst_ev)
                mm = None
                for ft in range(NFT):
                    for sub in range(2):
                        pb = [self.psA, self.psS][sub]
                        ysl = pb[rb:rb + 64, ft * 64:(ft + 1) * 64]
                        sl = slice(sub * 64, sub * 64 + 64)
                        t.matmul(ysl, Pmb[sub][sl, ft * 64:(ft + 1) * 64],
                                 vsl(ft, 0) if sub == 0 else vhi[:, ft * 64:(ft + 1) * 64],
                                 start=True, stop=False)
                        qslice = self.rTf[sl,
                                          (ft * NTT + tt) * P + (ct % 2) * 64:
                                          (ft * NTT + tt) * P + (ct % 2) * 64 + 64]
                        mm = t.matmul(ysl, qslice,
                                      Sbc[sl, ft * 64:(ft + 1) * 64],
                                      start=False, stop=True)
                self.inc(mm, "stt")
            ymm_ev = self.cnt["stt"]
            if csteps < 6:
                continue

            def v_ye(v, ct=ct, tt=tt, rb=rb):
                v.wait_ge(self.sems["stt"], ymm_ev)
                yscr = PA[64:128, 12288:14336]
                vsrc = (self.vnf[0:64, tt * 2048:(tt + 1) * 2048] if ct % 2 == 0
                        else vstage2)
                vvw = vsrc.rearrange("p (f u s) -> p f u s", u=2, s=64)
                ybw = ybuf[rb:rb + 64, :].rearrange("p (f u s) -> p f u s", u=2, s=64)
                ysw = yscr.rearrange("p (f u s) -> p f u s", u=2, s=64)
                dtw = dTs[rb:rb + 64, :].rearrange("p (f u) -> p f u", u=2)
                last = None
                for sub in range(2):
                    dbc = dtw[:, :, sub:sub + 1].broadcast_to([64, 16, 64])
                    self.fence(v, "v", v.tensor_tensor(out=ysw[:, :, sub, :],
                                                       in0=vvw[:, :, sub, :], in1=dbc, op=OP.mult))
                    pb = [self.psA, self.psS][sub]
                    last = v.tensor_tensor(
                        out=ybw[:, :, sub, :], in0=ysw[:, :, sub, :],
                        in1=pb[rb:rb + 64, :].rearrange("p (f s) -> p f s", s=64), op=OP.add)
                self.inc(last, "sv")
            self.seq(("v", v_ye))
            ye_ev = self.cnt["sv"]
            if csteps < 7:
                continue

            @block.gpsimd
            def _(g, ct=ct, rb=rb):
                g.wait_ge(self.sems["sv"], ye_ev)
                self.inc(g.dma_start(out=d["ysp"][ct], in_=ybuf[rb:rb + 64, :]), "ldg")
            if csteps < 8:
                continue

            for ft in range(NFT):
                pst = self.ps[ft % 2][:, 0:64].bitcast(BF16)

                def t_ktr(t, ft=ft, pst=pst, rb=rb):
                    self.w(t, "sv")
                    return t.transpose(pst[rb:rb + 64, 0:128] if False else
                                       self.ps[ft % 2].bitcast(BF16)[rb:rb + 64, 0:128],
                                       khb[:, ft * 64:(ft + 1) * 64], self.identb)

                def v_kev(v, ft=ft, rb=rb):
                    return v.tensor_copy(out=khn[rb:rb + 64, ft * P:(ft + 1) * P],
                                         in_=self.ps[ft % 2].bitcast(BF16)[rb:rb + 64, 0:128])
                self.seq(("t", t_ktr), ("v", v_kev))
            ktr_ev = self.cnt["sv"]
            if csteps < 9:
                continue

            @block.tensor
            def _(t, ct=ct, tt=tt, rb=rb, vsl=vsl):
                t.wait_ge(self.sems["sv"], ktr_ev)
                mm = None
                for ft in range(NFT):
                    for sub in range(2):
                        pb = [self.psA, self.psS][sub]
                        mm = t.matmul(pb[sub * 64:sub * 64 + 64, ft * 64:(ft + 1) * 64],
                                      khn[rb:rb + 64, ft * P + sub * 64:ft * P + sub * 64 + 64],
                                      vsl(ft, sub),
                                      start=True, stop=True)
                self.inc(mm, "stt")
            kv_ev = self.cnt["stt"]
            if csteps < 10:
                continue

            def v_su(v, ct=ct):
                v.wait_ge(self.sems["stt"], kv_ev)
                albc = self.ALb.rearrange("p (f c) -> p f c", c=16)[:, :, ct:ct + 1]
                for sub in range(2):
                    pb = [self.psA, self.psS][sub]
                    sl = slice(sub * 64, sub * 64 + 64)
                    sv3 = Sb[sl, :].rearrange("p (f s) -> p f s", s=64)
                    self.fence(v, "v", v.tensor_tensor(
                        out=sv3, in0=sv3, in1=albc[sl].broadcast_to([64, 16, 64]), op=OP.mult))
                    self.fence(v, "v", v.tensor_tensor(
                        out=Sb[sl, :], in0=Sb[sl, :], in1=pb[sl, :], op=OP.add))
                self.fence(v, "v", v.tensor_copy(out=Sbc, in_=Sb))
                if ct < 15:
                    lam = self.Lam.rearrange("p (f c) -> p f c", c=16)
                    alv2 = self.ALb.rearrange("p (f c) -> p f c", c=16)
                    return v.tensor_tensor(out=lam[:, :, ct + 1], in0=lam[:, :, ct],
                                           in1=alv2[:, :, ct], op=OP.mult)
                return None
            self.seq(("v", v_su))

        if not exch:
            return
        sbb = PA[:, 14336:15360]

        def v_msk(v):
            return v.tensor_single_scalar(out=sbb, in_=Sb, scalar=self.mfirst, op=OP.mult)
        self.seq(("v", v_msk))
        msk_ev = self.cnt["sv"]

        @block.gpsimd
        def _(g):
            g.wait_ge(self.sems["sv"], msk_ev)
            self.inc(g.dma_start(out=d["ccs_i"], in_=sbb), "ldg")
            self.w(g, "ldg")
            self.inc(g.collective_compute(
                "AllReduce", OP.add, replica_groups=[[0, 1], [2, 3], [4, 5], [6, 7]],
                ins=[d["ccs_i"].opt()], outs=[d["ccs_o"].opt()]), "cc")
            self.w(g, "cc")
            self.inc(g.dma_start(out=Sb, in_=d["ccs_o"]), "ldg")
        sl_ev = self.cnt["ldg"]

        def v_msk2(v):
            v.wait_ge(self.sems["ldg"], sl_ev)
            return v.tensor_single_scalar(out=Sb, in_=Sb, scalar=self.msecond, op=OP.mult)
        self.seq(("v", v_msk2))

    def Sbb(self, sub, ft):
        return self.Sb[sub * 64:sub * 64 + 64, ft * 64:(ft + 1) * 64]

    def _alcol(self, ct, sl):
        return self.ALb.rearrange("p (f c) -> p f c", c=16)[:, :, ct][sl, 0:0 + 16][:, 0:1] if False \
            else self.ALb[sl, ct:ct + 1]

    def _l1a_pass2(self):
        d, block = self.d, self.block
        PA = self.PA
        Sb, ybuf = self.Sb, self.ybuf
        stmp = PA[:, 14336:14848].bitcast(BF16)
        gnm = PA[:, 15360:15968]
        ynb = PA[:, 15968:16384].bitcast(BF16)

        self.lnwb = PA[:, 9216:11264]
        self.lnbb = PA[:, 11264:13312] if False else None

        def y_ln(sync):
            self.w(sync, "sv"); self.w(sync, "stt")
            return sync.dma_start(out=self.lnwb, in_=d["lnw_rep"])
        self.seq(("y", y_ln))
        self.lnbb = PB_lnb = self.PB[:, 0:2048]

        def y_ln2(sync):
            self.w(sync, "sv"); self.w(sync, "stt")
            return sync.dma_start(out=self.lnbb, in_=d["lnb_rep"])
        self.seq(("y", y_ln2))

        for ct in range(16):
            tt, rb = ct // 2, 0

            @block.gpsimd
            def _(g, ct=ct, rb=rb):
                self.w(g, "sv")
                self.inc(g.dma_start(out=ybuf[rb:rb + 64, :], in_=d["ysp"][ct]), "ldg")
            yld_ev = self.cnt["ldg"]

            def v_st(v, ct=ct):
                self.w(v, "stt")
                self.w(v, "st")
                lamv = self.Lam.rearrange("p (f c) -> p f c", c=16)[:, :, ct:ct + 1]
                return v.tensor_tensor(out=stmp.rearrange("p (f s) -> p f s", s=64),
                                       in0=Sb.rearrange("p (f s) -> p f s", s=64),
                                       in1=lamv.broadcast_to([P, 16, 64]), op=OP.mult)
            self.seq(("v", v_st))
            st_ev = self.cnt["sv"]

            @block.tensor
            def _(t, ct=ct, tt=tt, rb=rb):
                t.wait_ge(self.sems["sv"], st_ev)
                mm = None
                for ft in range(NFT):
                    for sub in range(2):
                        pb = [self.psA, self.psS][sub]
                        qslice = self.rTf[sub * 64:sub * 64 + 64,
                                          (ft * NTT + tt) * P + (ct % 2) * 64:
                                          (ft * NTT + tt) * P + (ct % 2) * 64 + 64]
                        mm = t.matmul(pb[rb:rb + 64, ft * 64:(ft + 1) * 64], qslice,
                                      stmp[sub * 64:sub * 64 + 64, ft * 64:(ft + 1) * 64],
                                      start=True, stop=True)
                self.inc(mm, "stt")
            t3_ev = self.cnt["stt"]

            def v_add(v, ct=ct, rb=rb):
                v.wait_ge(self.sems["stt"], t3_ev)
                v.wait_ge(self.sems["ldg"], yld_ev)
                ybw = ybuf[rb:rb + 64, :].rearrange("p (f u s) -> p f u s", u=2, s=64)
                last = None
                for sub in range(2):
                    pb = [self.psA, self.psS][sub]
                    last = v.tensor_tensor(
                        out=ybw[:, :, sub, :], in0=ybw[:, :, sub, :],
                        in1=pb[rb:rb + 64, :].rearrange("p (f s) -> p f s", s=64), op=OP.add)
                self.inc(last, "sv")
            self.seq(("v", v_add))

            mu = gnm[rb:rb + 64, 0:32]
            s2c = gnm[rb:rb + 64, 32:64]
            rs = gnm[rb:rb + 64, 64:96]
            yv = ybuf[rb:rb + 64, :].rearrange("p (h n) -> p h n", n=64)

            def v_gn1(v):
                self.fence(v, "v", v.tensor_reduce(out=mu, in_=yv, axis=AX.X, op=OP.add))
                return v.tensor_single_scalar(out=mu, in_=mu, scalar=1.0 / 64.0, op=OP.mult)
            self.seq(("v", v_gn1))

            def s_gn2(s, rb=rb):
                return s.activation(out=self._gnscrap(rb), in_=yv, func=ACT.Square)
            self.seq(("s", s_gn2))

            def v_gn2b(v, rb=rb):
                return v.tensor_reduce(out=s2c, in_=self._gnscrap(rb), axis=AX.X, op=OP.add)
            self.seq(("v", v_gn2b))

            def v_gn3(v):
                self.fence(v, "v", v.tensor_single_scalar(out=s2c, in_=s2c,
                                                          scalar=1.0 / 64.0, op=OP.mult))
                self.fence(v, "v", v.scalar_tensor_tensor(out=rs, in0=mu, scalar=-1.0,
                                                          in1=mu, op0=OP.mult, op1=OP.mult))
                return v.tensor_tensor(out=rs, in0=s2c, in1=rs, op=OP.add)
            self.seq(("v", v_gn3))

            def s_gn4(s, rb=rb):
                return s.activation(out=rs, in_=rs, func=ACT.Sqrt,
                                    bias=self.epsgn[rb:rb + 64, :])
            self.seq(("s", s_gn4))

            def v_gn5(v, ct=ct, rb=rb):
                self.fence(v, "v", v.reciprocal(out=rs, in_=rs))
                mub = mu.unsqueeze(2).broadcast_to([64, 32, 64])
                rsb = rs.unsqueeze(2).broadcast_to([64, 32, 64])
                self.fence(v, "v", v.tensor_tensor(out=yv, in0=yv, in1=mub, op=OP.subtract))
                self.fence(v, "v", v.tensor_tensor(out=yv, in0=yv, in1=rsb, op=OP.mult))
                lnw = d_lnw = self.lnwb[rb:rb + 64, :]
                self.fence(v, "v", v.tensor_tensor(out=ybuf[rb:rb + 64, :],
                                                   in0=ybuf[rb:rb + 64, :], in1=lnw, op=OP.mult))
                return v.tensor_tensor(out=self._ynb(rb), in0=ybuf[rb:rb + 64, :],
                                       in1=self.lnbb[rb:rb + 64, :], op=OP.add)
            self.seq(("v", v_gn5))
            yn_ev = self.cnt["sv"]

            @block.sync
            def _(sync, ct=ct, rb=rb):
                sync.wait_ge(self.sems["sv"], yn_ev)
                self.inc(sync.dma_start(out=d["ynsp"][ct], in_=self._ynb(rb)), "st")

    def _gnscrap(self, rb):
        return self.PA[64:128, 12288:14336].rearrange("p (h n) -> p h n", n=64)

    def _ynb(self, rb):
        return self.PA[rb:rb + 64, 14336:15360].bitcast(BF16)

    def _l1a_end(self):
        d, block = self.d, self.block
        PA, PB, PC, WS = self.PA, self.PB, self.PC, self.WS
        xgT = PB.bitcast(BF16)
        gsi = PC.bitcast(BF16)
        ynT = PA[:, 0:8192].bitcast(BF16)
        ytile = PA[:, 12288:13312].bitcast(BF16)
        wslot = [WS[:, 1024:1536].bitcast(BF16), WS[:, 1536:2048].bitcast(BF16)]
        banks = [self.psA[:, 0:512], self.psA[:, 512:1024],
                 self.psS[:, 0:512], self.psS[:, 512:1024],
                 self.ps[0][:, 0:512], self.ps[1][:, 0:512],
                 self.ps[2][:, 0:512], self.ps[3][:, 0:512]]

        def y_xg(sync):
            self.w(sync, "stt"); self.w(sync, "sv")
            last = None
            for fo in range(NFT):
                last = sync.dma_start(out=xgT[:, fo * 1024:(fo + 1) * 1024], in_=d["xgsp"][fo])
                if fo < NFT - 1:
                    self.inc(last, "ldh")
            return last
        self.seq(("y", y_xg))
        xg_ev = self.cnt["ldh"]

        for oh in range(2):
            for th2 in range(2):
                base_ev = self.cnt["ss"]
                for kt in range(NFT):
                    slot = kt % 2

                    @block.gpsimd
                    def _(g, oh=oh, kt=kt, slot=slot):
                        self.w(g, "stt")
                        self.inc(g.dma_start(
                            out=wslot[slot],
                            in_=d["wg2"][kt * P:(kt + 1) * P, oh * 1024:oh * 1024 + 1024]), "ldg")
                    ld_ev = self.cnt["ldg"]

                    @block.tensor
                    def _(t, oh=oh, th2=th2, kt=kt, slot=slot, base_ev=base_ev):
                        t.wait_ge(self.sems["ldg"], ld_ev)
                        t.wait_ge(self.sems["ldh"], xg_ev)
                        if kt == 0:
                            t.wait_ge(self.sems["ss"], base_ev)
                        mm = None
                        for fo in range(8):
                            mm = t.matmul(banks[fo], wslot[slot][:, fo * P:(fo + 1) * P],
                                          xgT[:, kt * 1024 + th2 * 512:kt * 1024 + th2 * 512 + 512],
                                          start=(kt == 0), stop=(kt == NFT - 1))
                        self.inc(mm, "stt")
                mm_ev = self.cnt["stt"]

                @block.scalar
                def _(s, oh=oh, th2=th2):
                    s.wait_ge(self.sems["stt"], mm_ev)
                    last = None
                    for fo in range(8):
                        fog = oh * 8 + fo
                        last = s.activation(
                            out=gsi[:, fog * 1024 + th2 * 512:fog * 1024 + th2 * 512 + 512],
                            in_=banks[fo], func=ACT.Silu)
                    self.inc(last, "ss")

        for tt in range(NTT):
            def y_yn(sync, tt=tt):
                self.w(sync, "stt")
                self.inc(sync.dma_start(out=ytile[0:64, :], in_=d["ynsp"][2 * tt]), "ldh")
                return sync.dma_start(out=ytile[64:128, :], in_=d["ynsp"][2 * tt + 1])
            self.seq(("y", y_yn))
            for ft in range(NFT):
                pst = self.ps[ft % 2].bitcast(BF16)[:, 0:128]

                def t_tr(t, ft=ft, pst=pst):
                    self.w(t, "sv")
                    self.w(t, "ldh")
                    return t.transpose(pst, ytile[:, ft * P:(ft + 1) * P], self.identb)

                def v_ev(v, ft=ft, tt=tt, pst=pst):
                    c = (ft * NTT + tt) * P
                    return v.tensor_copy(out=ynT[:, c:c + P], in_=pst)
                self.seq(("t", t_tr), ("v", v_ev))

        def v_zz(v):
            self.w(v, "ss")
            last = None
            for ft in range(NFT):
                last = v.tensor_tensor(out=ynT[:, ft * 1024:(ft + 1) * 1024],
                                       in0=ynT[:, ft * 1024:(ft + 1) * 1024],
                                       in1=gsi[:, ft * 1024:(ft + 1) * 1024], op=OP.mult)
            self.inc(last, "sv")
        self.seq(("v", v_zz))

        def y_xr(sync):
            self.w(sync, "sv"); self.w(sync, "stt")
            return sync.dma_start(
                out=self.xres[:, :].rearrange("p (tt dd) -> p tt dd", dd=D),
                in_=d["xspill"].rearrange("(tt p) dd -> p tt dd", p=P))
        self.seq(("y", y_xr))
        xr_ev = self.cnt["ldh"]

        for o2 in range(4):
            base_ev = self.cnt["sv"]
            for ft in range(NFT):
                slot = ft % 2

                @block.gpsimd
                def _(g, o2=o2, ft=ft, slot=slot):
                    self.w(g, "stt")
                    self.inc(g.dma_start(
                        out=wslot[slot][:, 0:512],
                        in_=d["wo"][ft * P:(ft + 1) * P, o2 * 512:(o2 + 1) * 512]), "ldg")
                ld_ev = self.cnt["ldg"]

                @block.tensor
                def _(t, o2=o2, ft=ft, slot=slot, base_ev=base_ev):
                    t.wait_ge(self.sems["ldg"], ld_ev)
                    t.wait_ge(self.sems["ldh"], xr_ev)
                    if ft == 0:
                        t.wait_ge(self.sems["sv"], base_ev)
                    mm = None
                    for tt in range(NTT):
                        mm = t.matmul(banks[tt],
                                      ynT[:, ft * 1024 + tt * P:ft * 1024 + tt * P + P],
                                      wslot[slot][:, 0:512],
                                      start=(ft == 0), stop=(ft == NFT - 1))
                    self.inc(mm, "stt")
            mm_ev = self.cnt["stt"]

            @block.vector
            def _(v, o2=o2):
                v.wait_ge(self.sems["stt"], mm_ev)
                last = None
                for tt in range(NTT):
                    xsl = self.xres[:, tt * D + o2 * 512:tt * D + o2 * 512 + 512]
                    last = v.tensor_tensor(out=xsl, in0=banks[tt], in1=xsl, op=OP.add)
                self.inc(last, "sv")



def _host_inputs(inputs):
    x = np.ascontiguousarray(np.asarray(inputs["L0_attn_x"], np.float32))

    def T(a):
        return np.ascontiguousarray(np.asarray(a, np.float32).T)

    def rs(v):
        return np.ascontiguousarray(np.asarray(v, np.float32).reshape(NFT, P).T)

    common = {
        "W_dkv": inputs["L0_attn_W_dkv"], "W_dq": inputs["L0_attn_W_dq"],
        "W_uk": inputs["L0_attn_W_uk"], "W_uv": inputs["L0_attn_W_uv"],
        "W_uq": inputs["L0_attn_W_uq"], "W_kr": inputs["L0_attn_W_kr"],
        "W_qr": inputs["L0_attn_W_qr"], "W_out": inputs["L0_attn_W_out"],
        "w1": inputs["L1_attn_w1"], "w2": inputs["L1_attn_w2"],
        "td_w1": inputs["L1_attn_td_w1"], "td_w2": inputs["L1_attn_td_w2"],
        "wr": inputs["L1_attn_wr"], "wk": inputs["L1_attn_wk"],
        "wv": inputs["L1_attn_wv"], "wg2": inputs["L1_attn_wg"],
        "wo": inputs["L1_attn_wo"],
        "l1f_nat": inputs["L1_ffn_weight"], "l1f_T": T(inputs["L1_ffn_weight"]),
        "l1f_bias": np.asarray(inputs["L1_ffn_bias"], np.float32).reshape(1, D),
        "tm_x": rs(np.asarray(inputs["L1_attn_time_maa_x"]).reshape(-1)),
        "tm_w": rs(np.asarray(inputs["L1_attn_time_maa_w"]).reshape(-1)),
        "tm_k": rs(np.asarray(inputs["L1_attn_time_maa_k"]).reshape(-1)),
        "tm_v": rs(np.asarray(inputs["L1_attn_time_maa_v"]).reshape(-1)),
        "tm_r": rs(np.asarray(inputs["L1_attn_time_maa_r"]).reshape(-1)),
        "tm_g": rs(np.asarray(inputs["L1_attn_time_maa_g"]).reshape(-1)),
        "tdecay": rs(np.asarray(inputs["L1_attn_time_decay"]).reshape(-1)),
        "lnw_rep": np.ascontiguousarray(np.broadcast_to(
            np.asarray(inputs["L1_attn_ln_w"], np.float32)[None, :], (P, D))),
        "lnb_rep": np.ascontiguousarray(np.broadcast_to(
            np.asarray(inputs["L1_attn_ln_b"], np.float32)[None, :], (P, D))),
    }
    fa = np.asarray(inputs["L1_attn_time_faaaa"], np.float32)
    u_bd = np.zeros((D, 32), np.float32)
    for h in range(32):
        u_bd[h * 64:(h + 1) * 64, h] = fa[h]
    common["u_bd"] = u_bd
    w2 = np.asarray(inputs["L1_attn_w2"], np.float32)
    w2pad = np.zeros((5, 160, D), np.float32)
    for f in range(5):
        w2pad[f, f * 32:(f + 1) * 32, :] = w2[f]
    common["w2pad"] = w2pad
    su = np.triu(np.ones((64, 64), np.float32), 1)
    common["masksub"] = np.ascontiguousarray(np.tile(su, (2, 16)))
    common["ident"] = np.eye(P, dtype=np.float32)
    common["mask_su"] = np.ascontiguousarray(np.triu(np.ones((64, 64), np.float32), 1))
    common["blk4"] = np.ascontiguousarray(np.repeat(np.eye(4, dtype=np.float32), 32, axis=1))
    common["ones1"] = np.ones((1, P), np.float32)

    wgn = np.ascontiguousarray(np.asarray(inputs["L0_ffn_w_gate"], np.float32))
    wun = np.ascontiguousarray(np.asarray(inputs["L0_ffn_w_up"], np.float32))
    wdn = np.ascontiguousarray(np.asarray(inputs["L0_ffn_w_down"], np.float32))
    wgT, wuT, wdT = T(wgn), T(wun), T(wdn)
    HF = I_FFN // 2
    wq_half = [{
        "wg_nat": np.ascontiguousarray(wgn[h * HF:(h + 1) * HF]),
        "wu_nat": np.ascontiguousarray(wun[h * HF:(h + 1) * HF]),
        "wd_nat": np.ascontiguousarray(wdn[:, h * HF:(h + 1) * HF]),
        "wg_T": np.ascontiguousarray(wgT[:, h * HF:(h + 1) * HF]),
        "wu_T": np.ascontiguousarray(wuT[:, h * HF:(h + 1) * HF]),
        "wd_T": np.ascontiguousarray(wdT[h * HF:(h + 1) * HF]),
    } for h in range(2)]

    inv_freq = (1.0 / 10000.0 ** (np.arange(0, 32, 2, dtype=np.float32) / np.float32(32.0))).astype(np.float32)
    in_maps = []
    for c in range(8):
        b, hf = c // 2, c % 2
        m = dict(common)
        m.update(wq_half[hf])
        m["x_in"] = np.ascontiguousarray(x[b, hf * TLOC:(hf + 1) * TLOC, :])
        t = (np.arange(TLOC, dtype=np.float32) + np.float32(hf * TLOC)) / np.float32(40.0)
        fr = np.einsum("i,j->ij", t, inv_freq).astype(np.float32)
        rot = np.concatenate([fr, fr], -1)
        cosf = np.cos(rot).astype(np.float32).reshape(NTT, P, 32).transpose(1, 0, 2)
        sinf = np.sin(rot).astype(np.float32).reshape(NTT, P, 32).transpose(1, 0, 2)
        m["rope_cos"] = np.ascontiguousarray(cosf.reshape(P, NTT * 32))
        m["rope_sin"] = np.ascontiguousarray(sinf.reshape(P, NTT * 32))
        m["mask_first"] = np.full((P, 1), 1.0 - hf, np.float32)
        m["mask_second"] = np.full((P, 1), float(hf), np.float32)
        in_maps.append(m)
    return in_maps


def _run(inputs, n_sub=N_SUB):
    in_maps = _host_inputs(inputs)
    nc = B(n_sub=n_sub).build()
    res = run_bass_kernel_spmd(nc, in_maps, core_ids=list(range(8)))
    out = np.empty((4, 2048, 2048), np.float32)
    for c in range(8):
        b, hf = c // 2, c % 2
        out[b, hf * TLOC:(hf + 1) * TLOC, :] = res.results[c]["out"]
    return out


def kernel(**inputs):
    return _run(inputs, n_sub=N_SUB)


